# version 01
# baseline (speedup 1.0000x reference)
import numpy as np
from contextlib import ExitStack

import concourse.bass as bass
import concourse.mybir as mybir
from concourse.bass_utils import run_bass_kernel_spmd
import ml_dtypes

BFH = ml_dtypes.bfloat16

F32 = mybir.dt.float32
BF16 = mybir.dt.bfloat16
AF = mybir.ActivationFunctionType
ALU = mybir.AluOpType
AX = mybir.AxisListType

C_EPS, C_NEGHALF, C_NEG20, C_EIGHTH, C_ONE, C_NEG1E30, C_ZERO = range(7)
NCST = 7


class Cfg:
    def __init__(self, T=2048, MF=32, n_sink=20):
        self.NCORE = 8
        self.B = 2
        self.D = 1024
        self.DC = 8
        self.T = T
        self.NT = T // 128
        self.NTK = self.NT + 2
        self.S = (self.NCORE // self.B) * T
        self.HA = 16
        self.HB = 16
        self.E = 8
        self.MF = MF
        self.F = MF * 128
        if MF >= 6:
            a = (MF + 2) // 3
            self.GROUPS = [list(range(0, a)), list(range(a, 2 * a)),
                           list(range(2 * a, MF))]
        else:
            a = max(1, MF // 2)
            self.GROUPS = [list(range(0, a)), list(range(a, MF))]
        self.GROUPS = [g for g in self.GROUPS if g]
        self.GMAX = max(len(g) for g in self.GROUPS)
        self.n_sink = n_sink
        self.TA = self.NCORE * T
        self.JW = self.TA // 16
        self.TT = self._tiles(T, 512)
        self.TTK = self._tiles(T + 256, 512)

    @staticmethod
    def _tiles(total, sz):
        out, off = [], 0
        while off < total:
            n = min(sz, total - off)
            out.append((off, n))
            off += n
        return out


class MK:
    def __init__(self, cfg):
        self.c = cfg
        self.nc = bass.Bass("TRN2", target_bir_lowering=False)
        self.cnt = {}
        self.sem = {}
        self.uid = 0

    def mksem(self, name):
        self.sem[name] = self.ctx.enter_context(self.nc.semaphore(name))
        self.cnt[name] = 0

    def inc(self, instr, name, n=1):
        instr.then_inc(self.sem[name], n)
        self.cnt[name] += n
        return self.cnt[name]

    def w(self, eng, name, val):
        if val > 0:
            eng.wait_ge(self.sem[name], val)

    def fence(self, eng, instr):
        n = self.inc(instr, "F")
        eng.wait_ge(self.sem["F"], n)

    def barrier(self):
        snap = dict(self.cnt)

        def body(e):
            for nm in ("T", "V", "A", "G", "Y", "C", "F",
                       "Wa", "Wd", "Wg0", "Wg1", "Wc"):
                self.w(e, nm, snap[nm])
        for en in ("tensor", "vector", "scalar", "gpsimd", "sync"):
            getattr(self.block, en)(body)

    def sb(self, ctx, name, shape, dt):
        self.uid += 1
        return ctx.enter_context(
            self.nc.sbuf_tensor(f"{name}_{self.uid}", shape, dt))

    def ps(self, ctx, name, shape, dt):
        self.uid += 1
        return ctx.enter_context(
            self.nc.psum_tensor(f"{name}_{self.uid}", shape, dt))

    def build(self):
        c = self.c
        nc = self.nc
        D, DC, T, NT, NTK = c.D, c.DC, c.T, c.NT, c.NTK

        dtens = lambda n, s, d, k: nc.dram_tensor(n, s, d, kind=k)
        x_in = dtens("x_in", [T + 256, D], F32, "ExternalInput")
        wq = dtens("wq", [D, D], F32, "ExternalInput")
        wk_dup = dtens("wk_dup", [D, 256], F32, "ExternalInput")
        wv = dtens("wv", [D, 128], F32, "ExternalInput")
        wo = dtens("wo", [D, D], F32, "ExternalInput")
        wqkv = dtens("wqkv", [D, 3 * D], F32, "ExternalInput")
        gate_wt = dtens("gate_wt", [D, 8], F32, "ExternalInput")
        wg_t = dtens("wg_t", [c.E, D, c.F], F32, "ExternalInput")
        wu_t = dtens("wu_t", [c.E, D, c.F], F32, "ExternalInput")
        wd_t = dtens("wd_t", [c.E, c.F, D], F32, "ExternalInput")
        bq_t = dtens("bq_t", [128, DC], F32, "ExternalInput")
        bk_dupi = dtens("bk_dupi", [128, 2], F32, "ExternalInput")
        bo_row = dtens("bo_row", [1, D], F32, "ExternalInput")
        bv_row = dtens("bv_row", [1, 128], F32, "ExternalInput")
        tri = dtens("tri", [128, 256], BF16, "ExternalInput")
        ident_b = dtens("ident_b", [128, 128], BF16, "ExternalInput")
        ident_f = dtens("ident_f", [128, 128], F32, "ExternalInput")
        onesf = dtens("onesf", [1, 128], F32, "ExternalInput")
        cst = dtens("cst", [128, NCST], F32, "ExternalInput")
        pcst = dtens("pcst", [128, 6], F32, "ExternalInput")
        sel_halo = dtens("sel_halo", [128, 32], F32, "ExternalInput")
        esel = dtens("esel", [128, 8], F32, "ExternalInput")
        eselT = dtens("eselT", [8, 128], F32, "ExternalInput")
        tsel = dtens("tsel", [128, 16], F32, "ExternalInput")
        tselT = dtens("tselT", [16, 128], F32, "ExternalInput")
        out_t = dtens("out", [T, D], F32, "ExternalOutput")
        cc_la_in = nc.dram_tensor("cc_la_in", [8, T], F32)
        cc_la_out = nc.dram_tensor("cc_la_out", [8 * 8, T], F32)
        cc_h_in = nc.dram_tensor("cc_h_in", [256, D], F32)
        cc_h_out = nc.dram_tensor("cc_h_out", [8 * 256, D], F32)

        with ExitStack() as ctx:
            self.ctx = ctx
            for s in ("T", "V", "A", "G", "Y", "C", "F",
                      "Wa", "Wd", "Wg0", "Wg1", "Wc"):
                self.mksem(s)
            sb, ps = self.sb, self.ps
            x = sb(ctx, "x", [128, NTK, D], F32)
            cst_s = sb(ctx, "cst_s", [128, NCST], F32)
            pcst_s = sb(ctx, "pcst_s", [128, 6], F32)
            selh_s = sb(ctx, "selh_s", [128, 32], F32)
            tri_s = sb(ctx, "tri_s", [128, 256], BF16)
            idb_s = sb(ctx, "idb_s", [128, 128], BF16)
            idf_s = sb(ctx, "idf_s", [128, 128], F32)
            onesf_s = sb(ctx, "onesf_s", [1, 128], F32)
            bq_s = sb(ctx, "bq_s", [128, DC], F32)
            bkd_s = sb(ctx, "bkd_s", [128, 2], F32)
            esel_s = sb(ctx, "esel_s", [128, 8], F32)
            eselT_s = sb(ctx, "eselT_s", [8, 128], F32)
            tsel_s = sb(ctx, "tsel_s", [128, 16], F32)
            tselT_s = sb(ctx, "tselT_s", [16, 128], F32)
            gw_s = sb(ctx, "gw_s", [128, DC, 8], BF16)
            W_sel = sb(ctx, "W_sel", [128, NT, 8], F32)
            la1own = sb(ctx, "la1own", [128, NT, 8], F32)
            colb = sb(ctx, "colb", [128, c.n_sink, 8], F32)
            sk_t = sb(ctx, "sk_t", [128, NT, 8], F32)
            sk_m = sb(ctx, "sk_m", [128, NT, 2], F32)
            pi_s = sb(ctx, "pi_s", [128, NT, 8], F32)
            m1_s = sb(ctx, "m1_s", [128, NT, 8], F32)
            m2_s = sb(ctx, "m2_s", [128, NT, 8], F32)
            ln_st = sb(ctx, "ln_st", [128, 2, 12], F32)
            ln_mv = sb(ctx, "ln_mv", [128, 2, 2], F32)
            ln_r = sb(ctx, "ln_r", [128, 2, 2], F32)
            ln_tm = sb(ctx, "ln_tm", [128, 2, D], BF16)
            xn = sb(ctx, "xn", [128, DC, T + 256], BF16)

            self.t = dict(x=x, cst=cst_s, pcst=pcst_s, selh=selh_s, tri=tri_s,
                          idb=idb_s, idf=idf_s, onesf=onesf_s, bq=bq_s,
                          bkd=bkd_s, esel=esel_s,
                          eselT=eselT_s, tsel=tsel_s, tselT=tselT_s, gw=gw_s,
                          bo_dram=bo_row, bv_dram=bv_row,
                          W_sel=W_sel, la1own=la1own, colb=colb, sk_t=sk_t,
                          sk_m=sk_m, pi=pi_s, m1=m1_s, m2=m2_s,
                          ln_st=ln_st, ln_mv=ln_mv,
                          ln_r=ln_r, ln_tm=ln_tm, xn=xn)
            self.dram = dict(wq=wq, wk_dup=wk_dup, wv=wv, wo=wo, wqkv=wqkv,
                             gate_wt=gate_wt, wg_t=wg_t, wu_t=wu_t, wd_t=wd_t,
                             cc_la_in=cc_la_in, cc_la_out=cc_la_out,
                             cc_h_in=cc_h_in, cc_h_out=cc_h_out)

            self.block = ctx.enter_context(nc.Block())
            blk = self.block

            def p0_sync(e):
                for dst, src in ((cst_s, cst), (pcst_s, pcst), (selh_s, sel_halo),
                                 (tri_s, tri), (idb_s, ident_b), (idf_s, ident_f),
                                 (onesf_s, onesf), (bq_s, bq_t), (bkd_s, bk_dupi),
                                 (esel_s, esel), (eselT_s, eselT),
                                 (tsel_s, tsel), (tselT_s, tselT)):
                    self.inc(e.dma_start(dst[:], src[:]), "Y", 16)
                self.inc(e.dma_start(
                    x[:], x_in.ap().rearrange("(t p) d -> p t d", p=128)), "Y", 16)
            blk.sync(p0_sync)

            def p0_gps(e):
                self.inc(e.dma_start(gw_s[:], gate_wt.ap().rearrange(
                    "(kc p) e -> p kc e", p=128)), "Wa", 16)
            blk.gpsimd(p0_gps)
            self.barrier()

            self._ln(range(NTK), toff=0)
            self._attn_a()
            self.barrier()

            self._ln(range(1, NT + 1), toff=128)
            self._sinkhorn()
            self._ffn()
            self.barrier()

            self._halo()
            self.barrier()

            self._ln(range(NTK), toff=0)
            self._attn_b()
            self.barrier()

            self._ln(range(1, NT + 1), toff=128)
            self._sinkhorn()
            self._ffn()
            self.barrier()

            def pout(e):
                self.inc(e.dma_start(
                    out_t.ap().rearrange("(t p) d -> p t d", p=128),
                    x[:, 1:NT + 1, :]), "Y", 16)
                self.w(e, "Y", self.cnt["Y"])
            blk.sync(pout)

        return nc

    def _ln(self, tiles, toff):
        c = self.c
        t = self.t
        tiles = list(tiles)
        with ExitStack() as pctx:
            pt = [self.ps(pctx, "ln_pt0", [128, 1024], BF16),
                  self.ps(pctx, "ln_pt1", [128, 1024], BF16)]
            x, xn = t["x"], t["xn"]
            ln_st, ln_mv, ln_r, ln_tm = t["ln_st"], t["ln_mv"], t["ln_r"], t["ln_tm"]
            cst_s, idb_s = t["cst"], t["idb"]

            v0, a0, t0 = self.cnt["V"], self.cnt["A"], self.cnt["T"]
            V_STATS = lambda i: v0 + 4 * i + 1
            V_AGGR = lambda i: v0 + 4 * i + 2
            V_NORM = lambda i: v0 + 4 * i + 3
            V_COPY = lambda i: v0 + 4 * i + 4
            A_LN = lambda i: a0 + 2 * i + 1
            A_EXP = lambda i: a0 + 2 * i + 2
            T_TR = lambda i: t0 + i + 1

            def vbody(e):
                for i, ti in enumerate(tiles):
                    r = i % 2
                    if i >= 2:
                        self.w(e, "A", A_EXP(i - 2))
                    e.bn_stats(ln_st[:, r, 0:6], x[:, ti, 0:512])
                    self.inc(e.bn_stats(ln_st[:, r, 6:12], x[:, ti, 512:1024]), "V")
                    self.w(e, "V", V_STATS(i))
                    self.inc(e.bn_aggr(ln_mv[:, r, :], ln_st[:, r, :]), "V")
                    self.w(e, "A", A_EXP(i))
                    if i >= 2:
                        self.w(e, "T", T_TR(i - 2))
                    self.inc(e.tensor_scalar(
                        ln_tm[:, r, :], x[:, ti, :], ln_mv[:, r, 0:1],
                        ln_r[:, r, 1:2], op0=ALU.subtract, op1=ALU.mult), "V")
                    self.w(e, "T", T_TR(i))
                    for dc in range(c.DC):
                        ins = e.tensor_copy(
                            xn[:, dc, toff + i * 128: toff + (i + 1) * 128],
                            pt[r][:, dc * 128:(dc + 1) * 128])
                    self.inc(ins, "V")

            def abody(e):
                for i, ti in enumerate(tiles):
                    r = i % 2
                    self.w(e, "V", V_AGGR(i))
                    ins = e.activation(ln_r[:, r, 0:1], ln_mv[:, r, 1:2],
                                       AF.Ln, bias=cst_s[:, C_EPS:C_EPS + 1])
                    self.inc(ins, "A")
                    self.w(e, "A", A_LN(i))
                    self.inc(e.activation(
                        ln_r[:, r, 1:2], ln_r[:, r, 0:1], AF.Exp,
                        scale=cst_s[:, C_NEGHALF:C_NEGHALF + 1]), "A")

            def tbody(e):
                for i, ti in enumerate(tiles):
                    r = i % 2
                    self.w(e, "V", V_NORM(i))
                    if i >= 2:
                        self.w(e, "V", V_COPY(i - 2))
                    for dc in range(c.DC):
                        ins = e.transpose(pt[r][:, dc * 128:(dc + 1) * 128],
                                          ln_tm[:, r, dc * 128:(dc + 1) * 128],
                                          idb_s[:])
                    self.inc(ins, "T")

            self.block.vector(vbody)
            self.block.scalar(abody)
            self.block.tensor(tbody)

    def _attn_a(self):
        c = self.c
        t = self.t
        NT, DC, T, NTK = c.NT, c.DC, c.T, c.NTK
        blk = self.block
        x, xn = t["x"], t["xn"]
        with ExitStack() as pctx:
            sb, ps = self.sb, self.ps
            q_sb = sb(pctx, "q_sb", [128, 8, T], BF16)
            k_sb = sb(pctx, "k_sb", [128, 2, T + 256], BF16)
            v_tok = sb(pctx, "v_tok", [128, NTK, 2, 66], BF16)
            ao_tm = sb(pctx, "ao_tm", [128, c.D], BF16)
            ao_fm = xn[:, :, 0:T]
            er = sb(pctx, "er", [128, 2, 2, 128], BF16)
            ef = sb(pctx, "ef", [128, 2, 128], BF16)
            exp_sb = sb(pctx, "exp_sb", [128, 2, 384], BF16)
            rc_sb = sb(pctx, "rc_sb", [128, 2, 1], F32)
            bo_s = sb(pctx, "bo_s", [1, c.D], F32)
            bv_s = sb(pctx, "bv_s", [1, 128], F32)
            ictx = ExitStack()
            wq_b = sb(ictx, "wq_b", [128, DC, c.D], BF16)
            wkd_b = sb(ictx, "wkd_b", [128, DC, 256], BF16)
            wv_b = sb(ictx, "wv_b", [128, DC, 128], BF16)
            PQ = [ps(pctx, "a_pq0", [128, 512], F32), ps(pctx, "a_pq1", [128, 512], F32)]
            PS_ = [ps(pctx, "a_ps0", [128, 512], F32), ps(pctx, "a_ps1", [128, 512], F32)]
            PY = [ps(pctx, "a_py0", [128, 512], F32), ps(pctx, "a_py1", [128, 512], F32)]
            PT = [ps(pctx, "a_pt0", [128, 1024], BF16), ps(pctx, "a_pt1", [128, 1024], BF16)]

            cst_s, pcst_s, tri_s = t["cst"], t["pcst"], t["tri"]
            idb_s, onesf_s = t["idb"], t["onesf"]
            bq_s, bkd_s = t["bq"], t["bkd"]
            dr = self.dram

            w0 = self.cnt["Wa"]

            def gbody(e):
                for dst, src in ((wq_b, dr["wq"]), (wkd_b, dr["wk_dup"]),
                                 (wv_b, dr["wv"])):
                    self.inc(e.dma_start(dst[:], src.ap().rearrange(
                        "(kc p) f -> p kc f", p=128)), "Wa", 16)
                self.inc(e.dma_start(bo_s[:], t["bo_dram"][:, :]), "Wa", 16)
                self.inc(e.dma_start(bv_s[:], t["bv_dram"][:, :]), "Wa", 16)
                self.inc(e.memset(v_tok[:, :, :, 64:66], 1.0), "G")
            blk.gpsimd(gbody)
            W_QKV = w0 + 80
            G_ONES = self.cnt["G"]
            v_ln_done = self.cnt["V"]

            t0, v0 = self.cnt["T"], self.cnt["V"]
            T_PROJ = lambda j: t0 + j + 1
            V_PROJ = lambda j: v0 + j + 1
            nQ, nK, nV = DC * len(c.TT), 2 * len(c.TTK), NTK

            def tproj(e):
                self.w(e, "Wa", W_QKV)
                self.w(e, "V", v_ln_done)
                j = 0
                for oc in range(DC):
                    for (off, sz) in c.TT:
                        if j >= 2:
                            self.w(e, "V", V_PROJ(j - 2))
                        p = PQ[j % 2]
                        for k in range(DC):
                            ins = e.matmul(p[:, 0:sz],
                                           wq_b[:, k, oc * 128:(oc + 1) * 128],
                                           xn[:, k, 128 + off:128 + off + sz],
                                           start=(k == 0), stop=(k == DC - 1))
                        self.inc(ins, "T")
                        j += 1
                for oc2 in range(2):
                    for (off, sz) in c.TTK:
                        if j >= 2:
                            self.w(e, "V", V_PROJ(j - 2))
                        p = PQ[j % 2]
                        for k in range(DC):
                            ins = e.matmul(p[:, 0:sz],
                                           wkd_b[:, k, oc2 * 128:(oc2 + 1) * 128],
                                           xn[:, k, off:off + sz],
                                           start=(k == 0), stop=(k == DC - 1))
                        self.inc(ins, "T")
                        j += 1
                self.w(e, "G", G_ONES)
                for ti in range(NTK):
                    if j >= 2:
                        self.w(e, "V", V_PROJ(j - 2))
                    p = PQ[j % 2]
                    for k in range(DC):
                        e.matmul(p[:, 0:128], xn[:, k, ti * 128:(ti + 1) * 128],
                                 wv_b[:, k, 0:128], start=(k == 0), stop=False)
                    ins = e.matmul(p[:, 0:128], onesf_s[:], bv_s[:],
                                   start=False, stop=True)
                    self.inc(ins, "T")
                    j += 1
            blk.tensor(tproj)

            def vproj(e):
                j = 0
                for oc in range(DC):
                    for (off, sz) in c.TT:
                        self.w(e, "T", T_PROJ(j))
                        p = PQ[j % 2]
                        for z in range(2):
                            ins = e.tensor_scalar(
                                q_sb[z * 64:z * 64 + 64, oc, off:off + sz],
                                p[z * 64:z * 64 + 64, 0:sz],
                                bq_s[z * 64:z * 64 + 64, oc:oc + 1], None,
                                op0=ALU.add)
                        self.inc(ins, "V")
                        j += 1
                for oc2 in range(2):
                    for (off, sz) in c.TTK:
                        self.w(e, "T", T_PROJ(j))
                        p = PQ[j % 2]
                        self.inc(e.tensor_scalar(
                            k_sb[:, oc2, off:off + sz], p[:, 0:sz],
                            bkd_s[:, oc2:oc2 + 1], None, op0=ALU.add), "V")
                        j += 1
                for ti in range(NTK):
                    self.w(e, "T", T_PROJ(j))
                    p = PQ[j % 2]
                    for kv in range(2):
                        dst = v_tok[:, ti, kv, 0:64]
                        src = p[:, kv * 64:kv * 64 + 64]
                        if ti == 0:
                            ins = e.tensor_scalar(dst, src, pcst_s[:, 0:1],
                                                  None, op0=ALU.mult)
                        elif ti == NTK - 1:
                            ins = e.tensor_scalar(dst, src, pcst_s[:, 2:3],
                                                  None, op0=ALU.mult)
                        else:
                            ins = e.tensor_copy(dst, src)
                    self.inc(ins, "V")
                    j += 1
            blk.vector(vproj)
            t_proj_end = self.cnt["T"]
            ictx.close()
            wo_b = sb(pctx, "wo_b", [128, DC, c.D], BF16)
            wo_w0 = self.cnt["Wa"]

            def gbody2(e):
                self.w(e, "T", t_proj_end)
                self.inc(e.dma_start(wo_b[:], dr["wo"].ap().rearrange(
                    "(kc p) f -> p kc f", p=128)), "Wa", 16)
            blk.gpsimd(gbody2)
            W_O = wo_w0 + 16

            ta0, aa0, va0 = self.cnt["T"], self.cnt["A"], self.cnt["V"]
            HA = c.HA
            T_SC = lambda qt, h: ta0 + 33 * qt + 2 * h + 1
            T_AV = lambda qt, h: ta0 + 33 * qt + 2 * h + 2
            T_TRQ = lambda qt: ta0 + 33 * (qt + 1)
            A_EXP = lambda qt, h: aa0 + 16 * qt + h + 1
            V_MSK = lambda qt, h: va0 + 49 * qt + 3 * h + 1
            V_RC = lambda qt, h: va0 + 49 * qt + 3 * h + 2
            V_EV = lambda qt, h: va0 + 49 * qt + 3 * h + 3
            V_CPQ = lambda qt: va0 + 49 * (qt + 1)

            def tatt(e):
                for qt in range(NT):
                    for h in range(HA):
                        z = h % 2
                        slot = (h // 8) ^ z
                        kv = h // 8
                        idx = qt * HA + h
                        i = idx % 2
                        if idx >= 2:
                            pq, ph = divmod(idx - 2, HA)
                            self.w(e, "A", A_EXP(pq, ph))
                        for j, kc in enumerate((qt, qt + 1, qt + 2)):
                            ins = e.matmul(
                                PS_[i][:, j * 128:(j + 1) * 128],
                                k_sb[z * 64:z * 64 + 64, slot,
                                     kc * 128:(kc + 1) * 128],
                                q_sb[z * 64:z * 64 + 64, h // 2,
                                     qt * 128:(qt + 1) * 128],
                                start=True, stop=True)
                        self.inc(ins, "T")
                        self.w(e, "A", A_EXP(qt, h))
                        self.w(e, "V", V_MSK(qt, h))
                        if idx >= 2:
                            pq, ph = divmod(idx - 2, HA)
                            self.w(e, "V", V_EV(pq, ph))
                        for j, kc in enumerate((qt, qt + 1, qt + 2)):
                            ins = e.matmul(
                                PY[i][:, 0:66],
                                exp_sb[:, i, j * 128:(j + 1) * 128],
                                v_tok[:, kc, kv, 0:66],
                                start=(j == 0), stop=(j == 2))
                        self.inc(ins, "T")
                    self.w(e, "V", V_EV(qt, HA - 1))
                    if qt >= 2:
                        self.w(e, "V", V_CPQ(qt - 2))
                    for dc in range(DC):
                        ins = e.transpose(PT[qt % 2][:, dc * 128:(dc + 1) * 128],
                                          ao_tm[:, dc * 128:(dc + 1) * 128],
                                          idb_s[:])
                    self.inc(ins, "T")
            blk.tensor(tatt)

            def aatt(e):
                for qt in range(NT):
                    for h in range(HA):
                        idx = qt * HA + h
                        i = idx % 2
                        self.w(e, "T", T_SC(qt, h))
                        if idx >= 2:
                            pq, ph = divmod(idx - 2, HA)
                            self.w(e, "V", V_MSK(pq, ph))
                            self.w(e, "T", T_AV(pq, ph))
                        e.activation(er[:, i, 0, :], PS_[i][:, 0:128], AF.Exp)
                        e.activation(exp_sb[:, i, 128:256], PS_[i][:, 128:256],
                                     AF.Exp)
                        self.inc(e.activation(er[:, i, 1, :],
                                              PS_[i][:, 256:384], AF.Exp), "A")
            blk.scalar(aatt)

            def vatt(e):
                for qt in range(NT):
                    if qt >= 1:
                        self.w(e, "T", T_TRQ(qt - 1))
                    for h in range(HA):
                        idx = qt * HA + h
                        i = idx % 2
                        self.w(e, "A", A_EXP(qt, h))
                        if idx >= 2:
                            pq, ph = divmod(idx - 2, HA)
                            self.w(e, "T", T_AV(pq, ph))
                        src0 = er[:, i, 0, :]
                        if qt == 0:
                            ins = e.tensor_scalar(ef[:, i, :], src0,
                                                  pcst_s[:, 0:1], pcst_s[:, 1:2],
                                                  op0=ALU.mult, op1=ALU.add)
                            self.fence(e, ins)
                            src0 = ef[:, i, :]
                        e.tensor_tensor(exp_sb[:, i, 0:128], src0,
                                        tri_s[:, 0:128], op=ALU.mult)
                        src2 = er[:, i, 1, :]
                        if qt == NT - 1:
                            ins = e.tensor_scalar(ef[:, i, :], src2,
                                                  pcst_s[:, 2:3], pcst_s[:, 3:4],
                                                  op0=ALU.mult, op1=ALU.add)
                            self.fence(e, ins)
                            src2 = ef[:, i, :]
                        self.inc(e.tensor_tensor(exp_sb[:, i, 256:384], src2,
                                                 tri_s[:, 128:256],
                                                 op=ALU.mult), "V")
                        self.w(e, "T", T_AV(qt, h))
                        self.inc(e.reciprocal(rc_sb[:, i, :], PY[i][:, 64:65]),
                                 "V")
                        self.w(e, "V", V_RC(qt, h))
                        self.inc(e.tensor_scalar(ao_tm[:, h * 64:(h + 1) * 64],
                                                 PY[i][:, 0:64], rc_sb[:, i, :],
                                                 None, op0=ALU.mult), "V")
                    self.w(e, "T", T_TRQ(qt))
                    for dc in range(DC):
                        ins = e.tensor_copy(ao_fm[:, dc, qt * 128:(qt + 1) * 128],
                                            PT[qt % 2][:, dc * 128:(dc + 1) * 128])
                    self.inc(ins, "V")
            blk.vector(vatt)
            v_att_end = self.cnt["V"]

            to0, vo0 = self.cnt["T"], self.cnt["V"]
            T_OP = lambda j: to0 + j + 1
            V_OP = lambda j: vo0 + j + 1

            def topr(e):
                self.w(e, "Wa", W_O)
                self.w(e, "V", v_att_end)
                j = 0
                for qt in range(NT):
                    for dh in range(2):
                        if j >= 2:
                            self.w(e, "V", V_OP(j - 2))
                        p = PQ[j % 2]
                        for k in range(DC):
                            e.matmul(p[:, 0:512],
                                     ao_fm[:, k, qt * 128:(qt + 1) * 128],
                                     wo_b[:, k, dh * 512:(dh + 1) * 512],
                                     start=(k == 0), stop=False)
                        ins = e.matmul(p[:, 0:512], onesf_s[:],
                                       bo_s[0:1, dh * 512:(dh + 1) * 512],
                                       start=False, stop=True)
                        self.inc(ins, "T")
                        j += 1
            blk.tensor(topr)

            def vopr(e):
                j = 0
                for qt in range(NT):
                    for dh in range(2):
                        self.w(e, "T", T_OP(j))
                        p = PQ[j % 2]
                        self.inc(e.scalar_tensor_tensor(
                            x[:, qt + 1, dh * 512:(dh + 1) * 512], p[:, 0:512],
                            cst_s[:, C_ONE:C_ONE + 1],
                            x[:, qt + 1, dh * 512:(dh + 1) * 512],
                            op0=ALU.mult, op1=ALU.add), "V")
                        j += 1
            blk.vector(vopr)

    def _sinkhorn(self):
        c = self.c
        t = self.t
        NT, NS, DC = c.NT, c.n_sink, c.DC
        blk = self.block
        xn, gw_s = t["xn"], t["gw"]
        la1own, colb, W_sel = t["la1own"], t["colb"], t["W_sel"]
        sk_t, sk_m, pi_s, m1_s, m2_s = t["sk_t"], t["sk_m"], t["pi"], t["m1"], t["m2"]
        cst_s, idf_s, onesf_s = t["cst"], t["idf"], t["onesf"]
        esel_s, eselT_s, tsel_s, tselT_s = t["esel"], t["eselT"], t["tsel"], t["tselT"]
        dr = self.dram
        jw = c.JW
        njt = (jw + 511) // 512

        with ExitStack() as pctx:
            la_em = self.sb(pctx, "la_em", [8, c.T], F32)
            la_full = self.sb(pctx, "la_full", [128, c.JW], F32)
            exp_f = self.sb(pctx, "exp_f", [128, c.JW], F32)
            psums = self.sb(pctx, "psums", [128, 1], F32)
            rls = self.sb(pctx, "rls", [16, c.JW], F32)
            clsb = self.sb(pctx, "clsb", [8, 1], F32)
            clrow = self.sb(pctx, "clrow", [1, c.n_sink * 8], F32)
            P1 = self.ps(pctx, "sk_p1", [128, 512], F32)
            P2 = self.ps(pctx, "sk_p2", [128, 512], F32)
            jw_pad = max(512, c.JW)
            P3 = self.ps(pctx, "sk_p3", [16, jw_pad], F32)
            P4 = self.ps(pctx, "sk_p4", [128, jw_pad], F32)

            t0, v0, a0 = self.cnt["T"], self.cnt["V"], self.cnt["A"]
            T_LG = lambda i: t0 + 2 * i + 1
            T_TR = lambda i: t0 + 2 * i + 2
            V_RAW = lambda i: v0 + 6 * i + 1
            V_RMX = lambda i: v0 + 6 * i + 2
            V_TT1 = lambda i: v0 + 6 * i + 3
            V_RSM = lambda i: v0 + 6 * i + 4
            V_LA1 = lambda i: v0 + 6 * i + 5
            V_CEM = lambda i: v0 + 6 * i + 6
            A_EXP = lambda i: a0 + 2 * i + 1
            A_LSE = lambda i: a0 + 2 * i + 2

            def tlg(e):
                self.w(e, "V", v0)
                for i in range(NT):
                    if i >= 2:
                        self.w(e, "V", V_RAW(i - 2))
                    p = P1 if i % 2 == 0 else P2
                    for k in range(DC):
                        ins = e.matmul(p[:, 0:8],
                                       xn[:, k, 128 + i * 128:128 + (i + 1) * 128],
                                       gw_s[:, k, :], start=(k == 0),
                                       stop=(k == DC - 1))
                    self.inc(ins, "T")
                    self.w(e, "V", V_LA1(i))
                    if i >= 2:
                        self.w(e, "V", V_CEM(i - 2))
                    ins = e.transpose(P4[0:8, (i % 2) * 128:(i % 2) * 128 + 128],
                                      la1own[:, i, :], idf_s[:])
                    self.inc(ins, "T")
            blk.tensor(tlg)

            def alg(e):
                for i in range(NT):
                    self.w(e, "V", V_TT1(i))
                    self.inc(e.activation(pi_s[:, i, :], la1own[:, i, :],
                                          AF.Exp), "A")
                    self.w(e, "V", V_RSM(i))
                    self.inc(e.activation(sk_m[:, i, 0:1], sk_m[:, i, 1:2],
                                          AF.Ln), "A")
            blk.scalar(alg)

            def vlg(e):
                for i in range(NT):
                    self.w(e, "T", T_LG(i))
                    p = P1 if i % 2 == 0 else P2
                    self.inc(e.tensor_scalar(sk_t[:, i, :], p[:, 0:8],
                                             cst_s[:, C_NEG20:C_NEG20 + 1],
                                             None, op0=ALU.mult), "V")
                    self.w(e, "V", V_RAW(i))
                    self.inc(e.tensor_reduce(sk_m[:, i, 0:1], sk_t[:, i, :],
                                             axis=AX.X, op=ALU.max), "V")
                    self.w(e, "V", V_RMX(i))
                    self.inc(e.tensor_tensor(
                        la1own[:, i, :], sk_t[:, i, :],
                        sk_m[:, i, 0:1].broadcast_to([128, 8]),
                        op=ALU.subtract), "V")
                    self.w(e, "A", A_EXP(i))
                    self.inc(e.tensor_reduce(sk_m[:, i, 1:2], pi_s[:, i, :],
                                             axis=AX.X, op=ALU.add), "V")
                    self.w(e, "A", A_LSE(i))
                    self.inc(e.tensor_tensor(
                        la1own[:, i, :], la1own[:, i, :],
                        sk_m[:, i, 0:1].broadcast_to([128, 8]),
                        op=ALU.subtract), "V")
                    self.w(e, "T", T_TR(i))
                    self.inc(e.tensor_copy(la_em[:, i * 128:(i + 1) * 128],
                                           P4[0:8, (i % 2) * 128:(i % 2) * 128 + 128]),
                             "V")
            blk.vector(vlg)
            v_em_done = self.cnt["V"]

            def gcc(e):
                self.w(e, "V", v_em_done)
                self.inc(e.dma_start(dr["cc_la_in"][:, :], la_em[:]), "Wc", 16)
                self.w(e, "Wc", self.cnt["Wc"])
                self.inc(e.collective_compute(
                    "AllGather", ALU.bypass, replica_groups=[list(range(8))],
                    ins=[dr["cc_la_in"].ap().opt()],
                    outs=[dr["cc_la_out"].ap().opt()]), "C")
                self.w(e, "C", self.cnt["C"])
                self.inc(e.dma_start(
                    la_full[:],
                    dr["cc_la_out"].ap().rearrange(
                        "(r e) (h j) -> (r e h) j", e=8, h=2)), "Wc", 16)
            blk.gpsimd(gcc)
            w_lafull = self.cnt["Wc"]

            seq = [("col", 0)]
            for it in range(1, NS):
                seq += [("row", it), ("col", it)]
            cA, cV, cT = self.cnt["A"], self.cnt["V"], self.cnt["T"]
            marks = []
            for kind, it in seq:
                m = {"kind": kind, "it": it}
                if kind == "col":
                    cA += 1; m["exp"] = cA
                    cV += 1; m["red"] = cV
                    cT += 1; m["mm1"] = cT
                    cA += 1; m["ln"] = cA
                    cT += 1; m["mm2"] = cT
                    cV += 1; m["sub"] = cV
                    cV += 1; m["crow"] = cV
                else:
                    cA += 1; m["exp"] = cA
                    cT += 1; m["mm1"] = cT
                    cA += 1; m["ln"] = cA
                    cT += 1; m["mm2"] = cT
                    cV += 1; m["sub"] = cV
                marks.append(m)

            def tb(e):
                prev_crow = 0
                self.w(e, "Wc", w_lafull)
                for idx, m in enumerate(marks):
                    if m["kind"] == "col":
                        self.w(e, "V", m["red"])
                        if prev_crow:
                            self.w(e, "V", prev_crow)
                        ins = e.matmul(P1[0:8, 0:1], esel_s[:], psums[:],
                                       start=True, stop=True)
                        ins.then_inc(self.sem["T"], 1)
                        self.w(e, "A", m["ln"])
                        e.matmul(P2[0:128, 0:1], eselT_s[:], clsb[:],
                                 start=True, stop=True)
                        ins = e.transpose(P1[0:1, 8:16], clsb[:],
                                          idf_s[0:8, 0:8])
                        ins.then_inc(self.sem["T"], 1)
                        prev_crow = m["crow"]
                    else:
                        self.w(e, "A", m["exp"])
                        for jt in range(njt):
                            o = jt * 512
                            sz = min(512, jw - o)
                            ins = e.matmul(P3[:, o:o + sz], tsel_s[:],
                                           exp_f[:, o:o + sz],
                                           start=True, stop=True)
                        ins.then_inc(self.sem["T"], 1)
                        self.w(e, "A", m["ln"])
                        for jt in range(njt):
                            o = jt * 512
                            sz = min(512, jw - o)
                            ins = e.matmul(P4[:, o:o + sz], tselT_s[:],
                                           rls[:, o:o + sz],
                                           start=True, stop=True)
                        ins.then_inc(self.sem["T"], 1)

            def ab(e):
                for idx, m in enumerate(marks):
                    if idx == 0:
                        self.w(e, "Wc", w_lafull)
                    else:
                        self.w(e, "V", marks[idx - 1]["sub"])
                    ins = e.activation(exp_f[:], la_full[:], AF.Exp)
                    ins.then_inc(self.sem["A"], 1)
                    self.w(e, "T", m["mm1"])
                    if m["kind"] == "col":
                        ins = e.activation(clsb[:], P1[0:8, 0:1], AF.Ln)
                    else:
                        ins = e.activation(rls[:, 0:jw], P3[:, 0:jw], AF.Ln)
                    ins.then_inc(self.sem["A"], 1)

            def vb(e):
                for idx, m in enumerate(marks):
                    if m["kind"] == "col":
                        self.w(e, "A", m["exp"])
                        ins = e.tensor_reduce(psums[:], exp_f[:], axis=AX.X,
                                              op=ALU.add)
                        ins.then_inc(self.sem["V"], 1)
                        self.w(e, "T", m["mm2"])
                        ins = e.tensor_scalar(la_full[:], la_full[:],
                                              P2[0:128, 0:1], None,
                                              op0=ALU.subtract)
                        ins.then_inc(self.sem["V"], 1)
                        ins = e.tensor_copy(
                            clrow[0:1, m["it"] * 8:(m["it"] + 1) * 8],
                            P1[0:1, 8:16])
                        ins.then_inc(self.sem["V"], 1)
                    else:
                        self.w(e, "T", m["mm2"])
                        ins = e.tensor_tensor(la_full[:], la_full[:],
                                              P4[:, 0:jw], op=ALU.subtract)
                        ins.then_inc(self.sem["V"], 1)

            blk.tensor(tb)
            blk.scalar(ab)
            blk.vector(vb)
            self.cnt["T"], self.cnt["V"], self.cnt["A"] = cT, cV, cA
            v_phA = self.cnt["V"]

            def tcolb(e):
                self.w(e, "V", v_phA)
                for it in range(NS):
                    ins = e.matmul(P4[:, it * 8:(it + 1) * 8], onesf_s[:],
                                   clrow[0:1, it * 8:(it + 1) * 8],
                                   start=True, stop=True)
                self.inc(ins, "T")
            blk.tensor(tcolb)
            t_colb = self.cnt["T"]

            def vcolb(e):
                self.w(e, "T", t_colb)
                self.inc(e.tensor_copy(
                    colb[:].rearrange("p a b -> p (a b)"),
                    P4[:, 0:NS * 8]), "V")
            blk.vector(vcolb)
            self.sink_psum_free = (self.cnt["V"], self.cnt["A"], self.cnt["T"])

            a0b = self.cnt["A"]
            A_BEXP = lambda it: a0b + 2 * it + 1
            A_BLN = lambda it: a0b + 2 * it + 2
            A_PI = a0b + 2 * (NS - 1) + 1
            f_tt1 = {}
            f_rsum = {}
            f_colsub_last = [0]
            v_colb_done = self.cnt["V"]

            def vphB(e):
                self.w(e, "V", v_colb_done)
                for it in range(NS):
                    cb = colb[:, it:it + 1, :].broadcast_to([128, NT, 8])
                    ins = e.tensor_tensor(la1own[:], la1own[:], cb,
                                          op=ALU.subtract)
                    self.fence(e, ins)
                    if it == NS - 1:
                        f_colsub_last[0] = self.cnt["F"]
                    if it < NS - 1:
                        self.fence(e, e.tensor_reduce(
                            sk_m[:, :, 0:1], la1own[:], axis=AX.X, op=ALU.max))
                        ins = e.tensor_tensor(
                            sk_t[:], la1own[:],
                            sk_m[:, :, 0:1].broadcast_to([128, NT, 8]),
                            op=ALU.subtract)
                        self.fence(e, ins)
                        f_tt1[it] = self.cnt["F"]
                        self.w(e, "A", A_BEXP(it))
                        self.fence(e, e.tensor_reduce(
                            sk_m[:, :, 1:2], pi_s[:], axis=AX.X, op=ALU.add))
                        f_rsum[it] = self.cnt["F"]
                        self.w(e, "A", A_BLN(it))
                        self.fence(e, e.tensor_tensor(
                            la1own[:], sk_t[:],
                            sk_m[:, :, 0:1].broadcast_to([128, NT, 8]),
                            op=ALU.subtract))
                self.w(e, "A", A_PI)
                self.fence(e, e.tensor_reduce(sk_m[:, :, 0:1], pi_s[:],
                                              axis=AX.X, op=ALU.max))
                self.fence(e, e.tensor_tensor(
                    m1_s[:], pi_s[:],
                    sk_m[:, :, 0:1].broadcast_to([128, NT, 8]),
                    op=ALU.is_equal))
                self.fence(e, e.scalar_tensor_tensor(
                    sk_t[:], m1_s[:], cst_s[:, C_NEG1E30:C_NEG1E30 + 1],
                    pi_s[:], op0=ALU.mult, op1=ALU.add))
                self.fence(e, e.tensor_reduce(sk_m[:, :, 1:2], sk_t[:],
                                              axis=AX.X, op=ALU.max))
                self.fence(e, e.tensor_tensor(
                    m2_s[:], sk_t[:],
                    sk_m[:, :, 1:2].broadcast_to([128, NT, 8]),
                    op=ALU.is_equal))
                self.fence(e, e.tensor_tensor(
                    sk_t[:], m1_s[:],
                    pi_s[:, :, 0:1].broadcast_to([128, NT, 8]), op=ALU.mult))
                self.fence(e, e.tensor_tensor(
                    m1_s[:], m2_s[:],
                    pi_s[:, :, 1:2].broadcast_to([128, NT, 8]), op=ALU.mult))
                self.inc(e.tensor_tensor(W_sel[:], sk_t[:], m1_s[:],
                                         op=ALU.add), "V")
            blk.vector(vphB)
            self.v_wsel = self.cnt["V"]

            la_fl = la1own[:].rearrange("p a b -> p (a b)")
            skt_fl = sk_t[:].rearrange("p a b -> p (a b)")
            pi_fl = pi_s[:].rearrange("p a b -> p (a b)")

            def aphB(e):
                for it in range(NS - 1):
                    self.w(e, "F", f_tt1[it])
                    self.inc(e.activation(pi_fl, skt_fl, AF.Exp), "A")
                    self.w(e, "F", f_rsum[it])
                    self.inc(e.activation(sk_m[:, :, 0:1], sk_m[:, :, 1:2],
                                          AF.Ln), "A")
                self.w(e, "F", f_colsub_last[0])
                self.inc(e.activation(pi_fl, la_fl, AF.Exp), "A")
            blk.scalar(aphB)

    def _ffn(self):
        c = self.c
        t = self.t
        NT, DC, T = c.NT, c.DC, c.T
        blk = self.block
        x, xn, W_sel = t["x"], t["xn"], t["W_sel"]
        cst_s = t["cst"]
        dr = self.dram
        ntt = len(c.TT)

        with ExitStack() as pctx:
            wgu_b = self.sb(pctx, "wgu_b", [128, 2, 2, DC, 256], BF16)
            wd_b = self.sb(pctx, "wd_b", [128, c.GMAX, c.D], BF16)
            h_b = self.sb(pctx, "h_b", [128, c.GMAX, T], BF16)
            sg_b = self.sb(pctx, "sg_b", [128, 2, 512], BF16)
            PG = [self.ps(pctx, "f_pg0", [128, 512], F32),
                  self.ps(pctx, "f_pg1", [128, 512], F32)]
            PU = [self.ps(pctx, "f_pu0", [128, 512], F32),
                  self.ps(pctx, "f_pu1", [128, 512], F32)]
            PYF = [self.ps(pctx, "f_py0", [128, 512], F32),
                   self.ps(pctx, "f_py1", [128, 512], F32)]

            t0, a0, v0 = self.cnt["T"], self.cnt["A"], self.cnt["V"]
            units = []
            for e_i in range(c.E):
                for gi, G in enumerate(c.GROUPS):
                    for mi in range(len(G)):
                        for tti in range(ntt):
                            units.append(("p1", e_i, gi, mi, tti))
                    for tt2 in range(NT):
                        for dh in range(2):
                            units.append(("p2", e_i, gi, tt2, dh))
            T_U = {}
            A_U = {}
            V_U = {}
            cT, cA, cV = t0, a0, v0
            p1list, p2list = [], []
            for u in units:
                cT += 1
                T_U[u] = cT
                if u[0] == "p1":
                    cA += 1
                    A_U[u] = cA
                    p1list.append(u)
                cV += 1
                V_U[u] = cV
                if u[0] == "p2":
                    p2list.append(u)
            pair_of = {}
            PAIR_B = {}
            PAIR_SEQ = []
            gp = 0
            for e_i in range(c.E):
                for gi, G in enumerate(c.GROUPS):
                    nm = len(G)
                    for pi2 in range((nm + 1) // 2):
                        pair_of[(e_i, gi, pi2)] = list(range(2 * pi2,
                                                             min(2 * pi2 + 2, nm)))
                        PAIR_B[(e_i, gi, pi2)] = gp % 2
                        PAIR_SEQ.append((e_i, gi, pi2))
                        gp += 1
            T_PAIR_DONE = {}
            for (e_i, gi, pi2), mis in pair_of.items():
                T_PAIR_DONE[(e_i, gi, pi2)] = T_U[("p1", e_i, gi, mis[-1],
                                                   ntt - 1)]
            T_P2_DONE = {}
            V_P2_DONE = {}
            for e_i in range(c.E):
                for gi in range(len(c.GROUPS)):
                    T_P2_DONE[(e_i, gi)] = T_U[("p2", e_i, gi, NT - 1, 1)]
                    V_P2_DONE[(e_i, gi)] = V_U[("p2", e_i, gi, NT - 1, 1)]
            V_P1_DONE = {}
            for e_i in range(c.E):
                for gi, G in enumerate(c.GROUPS):
                    V_P1_DONE[(e_i, gi)] = V_U[("p1", e_i, gi, len(G) - 1,
                                                ntt - 1)]

            W_WD = {}
            W_PAIR = {}
            cWd = self.cnt["Wd"]
            cWg = {0: self.cnt["Wg0"], 1: self.cnt["Wg1"]}
            for e_i in range(c.E):
                for gi, G in enumerate(c.GROUPS):
                    cWd += 16
                    W_WD[(e_i, gi)] = cWd
                    for pi2 in range((len(G) + 1) // 2):
                        b = PAIR_B[(e_i, gi, pi2)]
                        cWg[b] += 32
                        W_PAIR[(e_i, gi, pi2)] = (b, cWg[b])

            def prev_eg(e_i, gi):
                k = e_i * len(c.GROUPS) + gi - 1
                if k < 0:
                    return None
                return divmod(k, len(c.GROUPS))

            v_psum_free, a_psum_free, t_psum_free = self.sink_psum_free

            def gstream(e):
                self.w(e, "V", v_psum_free)
                self.w(e, "A", a_psum_free)
                self.w(e, "T", t_psum_free)
                for e_i in range(c.E):
                    wg_e = dr["wg_t"].ap()[e_i].rearrange("(kc p) f -> p kc f", p=128)
                    wu_e = dr["wu_t"].ap()[e_i].rearrange("(kc p) f -> p kc f", p=128)
                    wd_e = dr["wd_t"].ap()[e_i].rearrange("(mc p) d -> p mc d", p=128)
                    for gi, G in enumerate(c.GROUPS):
                        pv = prev_eg(e_i, gi)
                        if pv is not None:
                            self.w(e, "T", T_P2_DONE[pv])
                        self.inc(e.dma_start(
                            wd_b[:, 0:len(G), :],
                            wd_e[:, G[0]:G[0] + len(G), :],
                            max_dma_last_dim=8192), "Wd", 16)
                        npair = (len(G) + 1) // 2
                        for pi2 in range(npair):
                            mis = pair_of[(e_i, gi, pi2)]
                            mw = len(mis) * 128
                            f0 = (G[0] + mis[0]) * 128
                            gidx = PAIR_SEQ.index((e_i, gi, pi2))
                            if gidx >= 2:
                                self.w(e, "T", T_PAIR_DONE[PAIR_SEQ[gidx - 2]])
                            b = PAIR_B[(e_i, gi, pi2)]
                            self.inc(e.dma_start(
                                wgu_b[:, b, 0, :, 0:mw],
                                wg_e[:, :, f0:f0 + mw],
                                max_dma_last_dim=8192), f"Wg{b}", 16)
                            self.inc(e.dma_start(
                                wgu_b[:, b, 1, :, 0:mw],
                                wu_e[:, :, f0:f0 + mw],
                                max_dma_last_dim=8192), f"Wg{b}", 16)
            blk.gpsimd(gstream)


            def tffn(e):
                self.w(e, "V", v_psum_free)
                self.w(e, "A", a_psum_free)
                p1i = 0
                p2i = 0
                for e_i in range(c.E):
                    for gi, G in enumerate(c.GROUPS):
                        for mi in range(len(G)):
                            pi2, z = divmod(mi, 2)
                            b = PAIR_B[(e_i, gi, pi2)]
                            for tti, (off, sz) in enumerate(c.TT):
                                u = ("p1", e_i, gi, mi, tti)
                                if tti == 0 and z == 0:
                                    wb_, wv_ = W_PAIR[(e_i, gi, pi2)]
                                    self.w(e, f"Wg{wb_}", wv_)
                                ii = p1i % 2
                                if p1i >= 2:
                                    pu = p1list[p1i - 2]
                                    self.w(e, "A", A_U[pu])
                                    self.w(e, "V", V_U[pu])
                                for k in range(DC):
                                    e.matmul(PG[ii][:, 0:sz],
                                             wgu_b[:, b, 0, k,
                                                   z * 128:(z + 1) * 128],
                                             xn[:, k, 128 + off:128 + off + sz],
                                             start=(k == 0), stop=(k == DC - 1))
                                for k in range(DC):
                                    ins = e.matmul(
                                        PU[ii][:, 0:sz],
                                        wgu_b[:, b, 1, k, z * 128:(z + 1) * 128],
                                        xn[:, k, 128 + off:128 + off + sz],
                                        start=(k == 0), stop=(k == DC - 1))
                                self.inc(ins, "T")
                                assert self.cnt["T"] == T_U[u]
                                p1i += 1
                        self.w(e, "Wd", W_WD[(e_i, gi)])
                        for tt2 in range(NT):
                            for dh in range(2):
                                u = ("p2", e_i, gi, tt2, dh)
                                jj = p2i % 2
                                if p2i >= 2:
                                    self.w(e, "V", V_U[p2list[p2i - 2]])
                                if tt2 == 0 and dh == 0:
                                    self.w(e, "V", V_P1_DONE[(e_i, gi)])
                                for mi in range(len(G)):
                                    ins = e.matmul(
                                        PYF[jj][:, 0:512],
                                        h_b[:, mi, tt2 * 128:(tt2 + 1) * 128],
                                        wd_b[:, mi, dh * 512:(dh + 1) * 512],
                                        start=(mi == 0),
                                        stop=(mi == len(G) - 1))
                                self.inc(ins, "T")
                                assert self.cnt["T"] == T_U[u]
                                p2i += 1
            blk.tensor(tffn)

            def affn(e):
                p1i = 0
                for e_i in range(c.E):
                    for gi, G in enumerate(c.GROUPS):
                        for mi in range(len(G)):
                            for tti, (off, sz) in enumerate(c.TT):
                                u = ("p1", e_i, gi, mi, tti)
                                ii = p1i % 2
                                self.w(e, "T", T_U[u])
                                if p1i >= 2:
                                    self.w(e, "V", V_U[p1list[p1i - 2]])
                                self.inc(e.activation(sg_b[:, ii, 0:sz],
                                                      PG[ii][:, 0:sz],
                                                      AF.Silu), "A")
                                assert self.cnt["A"] == A_U[u]
                                p1i += 1
            blk.scalar(affn)

            wsel_fence = self.v_wsel
            prev_evac = {}

            def vffn(e):
                self.w(e, "V", wsel_fence)
                p1i = 0
                p2i = 0
                for e_i in range(c.E):
                    for gi, G in enumerate(c.GROUPS):
                        for mi in range(len(G)):
                            for tti, (off, sz) in enumerate(c.TT):
                                u = ("p1", e_i, gi, mi, tti)
                                ii = p1i % 2
                                self.w(e, "A", A_U[u])
                                if mi == 0 and tti == 0:
                                    pv = prev_eg(e_i, gi)
                                    if pv is not None:
                                        self.w(e, "T", T_P2_DONE[pv])
                                self.inc(e.tensor_tensor(
                                    h_b[:, mi, off:off + sz],
                                    sg_b[:, ii, 0:sz], PU[ii][:, 0:sz],
                                    op=ALU.mult), "V")
                                assert self.cnt["V"] == V_U[u]
                                p1i += 1
                        for tt2 in range(NT):
                            for dh in range(2):
                                u = ("p2", e_i, gi, tt2, dh)
                                jj = p2i % 2
                                self.w(e, "T", T_U[u])
                                key = (tt2, dh)
                                if key in prev_evac:
                                    self.w(e, "V", prev_evac[key])
                                self.inc(e.scalar_tensor_tensor(
                                    x[:, tt2 + 1, dh * 512:(dh + 1) * 512],
                                    PYF[jj][:, 0:512],
                                    W_sel[:, tt2, e_i:e_i + 1],
                                    x[:, tt2 + 1, dh * 512:(dh + 1) * 512],
                                    op0=ALU.mult, op1=ALU.add), "V")
                                prev_evac[key] = self.cnt["V"]
                                assert self.cnt["V"] == V_U[u]
                                p2i += 1
            blk.vector(vffn)

    def _halo(self):
        c = self.c
        t = self.t
        blk = self.block
        x, selh_s = t["x"], t["selh"]
        dr = self.dram
        NT = c.NT

        with ExitStack() as pctx:
            G_sb = self.sb(pctx, "G_sb", [128, 16, c.D], F32)

            y0 = self.cnt["Y"]

            def ysync(e):
                self.inc(e.dma_start(dr["cc_h_in"][0:128, :], x[:, 1, :]),
                         "Y", 16)
                self.inc(e.dma_start(dr["cc_h_in"][128:256, :], x[:, NT, :]),
                         "Y", 16)
            blk.sync(ysync)

            def gcc(e):
                self.w(e, "Y", y0 + 32)
                self.inc(e.collective_compute(
                    "AllGather", ALU.bypass, replica_groups=[list(range(8))],
                    ins=[dr["cc_h_in"].ap().opt()],
                    outs=[dr["cc_h_out"].ap().opt()]), "C")
                self.w(e, "C", self.cnt["C"])
                self.inc(e.dma_start(G_sb[:], dr["cc_h_out"].ap().rearrange(
                    "(t p) d -> p t d", p=128)), "Wc", 16)
            blk.gpsimd(gcc)
            w_g = self.cnt["Wc"]

            def vsel(e):
                self.w(e, "Wc", w_g)
                for side, dst_ti, col0 in ((0, 0, 0), (1, c.NTK - 1, 16)):
                    ins = e.tensor_scalar(x[:, dst_ti, :], G_sb[:, 0, :],
                                          selh_s[:, col0:col0 + 1], None,
                                          op0=ALU.mult)
                    self.fence(e, ins)
                    for ti in range(1, 16):
                        ins = e.scalar_tensor_tensor(
                            x[:, dst_ti, :], G_sb[:, ti, :],
                            selh_s[:, col0 + ti:col0 + ti + 1],
                            x[:, dst_ti, :], op0=ALU.mult, op1=ALU.add)
                        self.fence(e, ins)
            blk.vector(vsel)

    def _attn_b(self):
        c = self.c
        t = self.t
        NT, DC, T, NTK, HB = c.NT, c.DC, c.T, c.NTK, c.HB
        blk = self.block
        x, xn = t["x"], t["xn"]
        cst_s, pcst_s, idb_s = t["cst"], t["pcst"], t["idb"]
        dr = self.dram

        with ExitStack() as pctx:
            sb, ps = self.sb, self.ps
            wb_b = sb(pctx, "wb_b", [128, DC, 1536], BF16)
            q_sb = sb(pctx, "q_sb_b", [128, 4, T], BF16)
            k_sb = sb(pctx, "k_sb_b", [128, 4, T + 256], BF16)
            v_tok = sb(pctx, "v_tok_b", [128, NTK, 8, 66], BF16)
            er = sb(pctx, "er_b", [128, 2, 2, 128], BF16)
            exp_sb = sb(pctx, "exp_sb_b", [128, 2, 384], BF16)
            rc_sb = sb(pctx, "rc_sb_b", [128, 2, 1], F32)
            PQ = [ps(pctx, "b_pq0", [128, 512], F32),
                  ps(pctx, "b_pq1", [128, 512], F32)]
            PS_ = [ps(pctx, "b_ps0", [128, 512], F32),
                   ps(pctx, "b_ps1", [128, 512], F32)]
            PY = [ps(pctx, "b_py0", [128, 512], F32),
                  ps(pctx, "b_py1", [128, 512], F32)]

            v_ln_done = self.cnt["V"]
            for grp in range(2):
                w0 = self.cnt["Wa"]

                def gw(e, grp=grp):
                    if grp > 0:
                        self.w(e, "T", self._b_t_end)
                        self.w(e, "V", self._b_v_end)
                    for part in range(3):
                        src = dr["wqkv"].ap().rearrange(
                            "(kc p) f -> p kc f", p=128)[
                            :, :, part * c.D + grp * 512:
                            part * c.D + grp * 512 + 512]
                        self.inc(e.dma_start(wb_b[:, :, part * 512:
                                                   part * 512 + 512], src),
                                 "Wa", 16)
                    self.inc(e.memset(v_tok[:, :, :, 64:66], 1.0), "G")
                blk.gpsimd(gw)
                W_G = w0 + 48
                G_ONES = self.cnt["G"]

                t0, v0 = self.cnt["T"], self.cnt["V"]
                T_PROJ = lambda j: t0 + j + 1
                V_PROJ = lambda j: v0 + j + 1

                def tproj(e, grp=grp):
                    self.w(e, "Wa", W_G)
                    self.w(e, "V", v_ln_done)
                    self.w(e, "G", G_ONES)
                    j = 0
                    for oc in range(4):
                        for (off, sz) in c.TT:
                            if j >= 2:
                                self.w(e, "V", V_PROJ(j - 2))
                            p = PQ[j % 2]
                            for k in range(DC):
                                ins = e.matmul(
                                    p[:, 0:sz],
                                    wb_b[:, k, oc * 128:(oc + 1) * 128],
                                    xn[:, k, 128 + off:128 + off + sz],
                                    start=(k == 0), stop=(k == DC - 1))
                            self.inc(ins, "T")
                            j += 1
                    for oc in range(4):
                        for (off, sz) in c.TTK:
                            if j >= 2:
                                self.w(e, "V", V_PROJ(j - 2))
                            p = PQ[j % 2]
                            for k in range(DC):
                                ins = e.matmul(
                                    p[:, 0:sz],
                                    wb_b[:, k, 512 + oc * 128:512 + (oc + 1) * 128],
                                    xn[:, k, off:off + sz],
                                    start=(k == 0), stop=(k == DC - 1))
                            self.inc(ins, "T")
                            j += 1
                    for ti in range(NTK):
                        if j >= 2:
                            self.w(e, "V", V_PROJ(j - 2))
                        p = PQ[j % 2]
                        for k in range(DC):
                            ins = e.matmul(p[:, 0:512],
                                           xn[:, k, ti * 128:(ti + 1) * 128],
                                           wb_b[:, k, 1024:1536],
                                           start=(k == 0), stop=(k == DC - 1))
                        self.inc(ins, "T")
                        j += 1
                blk.tensor(tproj)

                def vproj(e, grp=grp):
                    j = 0
                    for oc in range(4):
                        for (off, sz) in c.TT:
                            self.w(e, "T", T_PROJ(j))
                            p = PQ[j % 2]
                            self.inc(e.tensor_scalar(
                                q_sb[:, oc, off:off + sz], p[:, 0:sz],
                                cst_s[:, C_EIGHTH:C_EIGHTH + 1], None,
                                op0=ALU.mult), "V")
                            j += 1
                    for oc in range(4):
                        for (off, sz) in c.TTK:
                            self.w(e, "T", T_PROJ(j))
                            p = PQ[j % 2]
                            self.inc(e.tensor_copy(
                                k_sb[:, oc, off:off + sz], p[:, 0:sz]), "V")
                            j += 1
                    for ti in range(NTK):
                        self.w(e, "T", T_PROJ(j))
                        p = PQ[j % 2]
                        for hg in range(8):
                            ins = e.tensor_copy(v_tok[:, ti, hg, 0:64],
                                                p[:, hg * 64:(hg + 1) * 64])
                        self.inc(ins, "V")
                        j += 1
                blk.vector(vproj)

                ta0, aa0, va0 = self.cnt["T"], self.cnt["A"], self.cnt["V"]
                T_SC = lambda qt, h: ta0 + 16 * qt + 2 * h + 1
                T_AV = lambda qt, h: ta0 + 16 * qt + 2 * h + 2
                A_EXP = lambda qt, h: aa0 + 8 * qt + h + 1
                V_MSK = lambda qt, h: va0 + 24 * qt + 3 * h + 1
                V_RC = lambda qt, h: va0 + 24 * qt + 3 * h + 2
                V_EV = lambda qt, h: va0 + 24 * qt + 3 * h + 3

                def tatt(e, grp=grp):
                    for qt in range(NT):
                        for h in range(8):
                            z = h % 2
                            idx = qt * 8 + h
                            i = idx % 2
                            if idx >= 2:
                                pq, ph = divmod(idx - 2, 8)
                                self.w(e, "A", A_EXP(pq, ph))
                            for j, kc in enumerate((qt, qt + 1, qt + 2)):
                                ins = e.matmul(
                                    PS_[i][:, j * 128:(j + 1) * 128],
                                    k_sb[z * 64:z * 64 + 64, h // 2,
                                         kc * 128:(kc + 1) * 128],
                                    q_sb[z * 64:z * 64 + 64, h // 2,
                                         qt * 128:(qt + 1) * 128],
                                    start=True, stop=True)
                            self.inc(ins, "T")
                            self.w(e, "A", A_EXP(qt, h))
                            self.w(e, "V", V_MSK(qt, h))
                            if idx >= 2:
                                pq, ph = divmod(idx - 2, 8)
                                self.w(e, "V", V_EV(pq, ph))
                            for j, kc in enumerate((qt, qt + 1, qt + 2)):
                                ins = e.matmul(
                                    PY[i][:, 0:66],
                                    exp_sb[:, i, j * 128:(j + 1) * 128],
                                    v_tok[:, kc, h, 0:66],
                                    start=(j == 0), stop=(j == 2))
                            self.inc(ins, "T")
                blk.tensor(tatt)

                def aatt(e, grp=grp):
                    for qt in range(NT):
                        for h in range(8):
                            idx = qt * 8 + h
                            i = idx % 2
                            self.w(e, "T", T_SC(qt, h))
                            if idx >= 2:
                                pq, ph = divmod(idx - 2, 8)
                                self.w(e, "V", V_MSK(pq, ph))
                                self.w(e, "T", T_AV(pq, ph))
                            e.activation(er[:, i, 0, :], PS_[i][:, 0:128],
                                         AF.Exp)
                            e.activation(exp_sb[:, i, 128:256],
                                         PS_[i][:, 128:256], AF.Exp)
                            self.inc(e.activation(er[:, i, 1, :],
                                                  PS_[i][:, 256:384],
                                                  AF.Exp), "A")
                blk.scalar(aatt)

                def vatt(e, grp=grp):
                    for qt in range(NT):
                        for h in range(8):
                            idx = qt * 8 + h
                            i = idx % 2
                            self.w(e, "A", A_EXP(qt, h))
                            if idx >= 2:
                                pq, ph = divmod(idx - 2, 8)
                                self.w(e, "T", T_AV(pq, ph))
                            if qt == 0:
                                ins = e.tensor_scalar(
                                    exp_sb[:, i, 0:128], er[:, i, 0, :],
                                    pcst_s[:, 4:5], None, op0=ALU.mult)
                            else:
                                ins = e.tensor_copy(exp_sb[:, i, 0:128],
                                                    er[:, i, 0, :])
                            if qt == NT - 1:
                                ins = e.tensor_scalar(
                                    exp_sb[:, i, 256:384], er[:, i, 1, :],
                                    pcst_s[:, 5:6], None, op0=ALU.mult)
                            else:
                                ins = e.tensor_copy(exp_sb[:, i, 256:384],
                                                    er[:, i, 1, :])
                            self.inc(ins, "V")
                            self.w(e, "T", T_AV(qt, h))
                            self.inc(e.reciprocal(rc_sb[:, i, :],
                                                  PY[i][:, 64:65]), "V")
                            self.w(e, "V", V_RC(qt, h))
                            hh = grp * 8 + h
                            self.inc(e.scalar_tensor_tensor(
                                x[:, qt + 1, hh * 64:(hh + 1) * 64],
                                PY[i][:, 0:64], rc_sb[:, i, :],
                                x[:, qt + 1, hh * 64:(hh + 1) * 64],
                                op0=ALU.mult, op1=ALU.add), "V")
                blk.vector(vatt)
                self._b_t_end = self.cnt["T"]
                self._b_v_end = self.cnt["V"]




def _prep_inputs(cfg, inputs):
    c = cfg
    T, D, S = c.T, c.D, c.S
    f32 = np.float32
    x = np.asarray(inputs["L0_attn_a_x"], f32)
    wq = np.ascontiguousarray(np.asarray(inputs["L0_attn_a_Wq"], f32))
    bq = np.asarray(inputs["L0_attn_a_bq"], f32)
    wk = np.asarray(inputs["L0_attn_a_Wk"], f32)
    bk = np.asarray(inputs["L0_attn_a_bk"], f32)
    wv = np.ascontiguousarray(np.asarray(inputs["L0_attn_a_Wv"], f32))
    bv = np.asarray(inputs["L0_attn_a_bv"], f32)
    wo = np.ascontiguousarray(np.asarray(inputs["L0_attn_a_Wo"], f32))
    bo = np.asarray(inputs["L0_attn_a_bo"], f32)
    wqkv = np.ascontiguousarray(np.asarray(inputs["L0_attn_b_w_qkv"], f32))
    gate_w = np.asarray(inputs["L0_ffn_gate_W"], f32)
    wg = np.asarray(inputs["L0_ffn_W_gate"], f32)
    wu = np.asarray(inputs["L0_ffn_W_up"], f32)
    wd = np.asarray(inputs["L0_ffn_W_down"], f32)

    wk_dup = np.ascontiguousarray(
        np.concatenate([wk, wk[:, 64:128], wk[:, 0:64]], axis=1))
    bk_dup = np.stack([bk, np.concatenate([bk[64:128], bk[0:64]])], axis=1)
    bk_dup = np.ascontiguousarray(bk_dup)
    bq_t = np.ascontiguousarray(bq.reshape(c.DC, 128).T)
    gate_wt = np.ascontiguousarray(gate_w.T)
    wg_t = np.ascontiguousarray(wg.transpose(0, 2, 1))
    wu_t = np.ascontiguousarray(wu.transpose(0, 2, 1))
    wd_t = np.ascontiguousarray(wd.transpose(0, 2, 1))

    u = np.arange(128)[:, None]
    tq = np.arange(128)[None, :]
    tri = np.concatenate([(u >= tq), (u < tq)], axis=1).astype(BFH)
    ident_b = np.eye(128, dtype=BFH)
    ident_f = np.eye(128, dtype=f32)
    onesf = np.ones((1, 128), f32)
    cstv = np.zeros((128, NCST), f32)
    cstv[:, C_EPS] = 1e-5
    cstv[:, C_NEGHALF] = -0.5
    cstv[:, C_NEG20] = -1.0 / 0.05
    cstv[:, C_EIGHTH] = 64.0 ** -0.5
    cstv[:, C_ONE] = 1.0
    cstv[:, C_NEG1E30] = -1e30
    pp = np.arange(128)[:, None]
    esel = ((pp % 16) // 2 == np.arange(8)[None, :]).astype(f32)
    eselT = np.ascontiguousarray(esel.T)
    tsel = ((pp // 16) * 2 + (pp % 2) == np.arange(16)[None, :]).astype(f32)
    tselT = np.ascontiguousarray(tsel.T)

    shared = dict(wq=wq, wk_dup=wk_dup, wv=wv, wo=wo, wqkv=wqkv,
                  gate_wt=gate_wt, wg_t=wg_t, wu_t=wu_t, wd_t=wd_t,
                  bq_t=bq_t, bk_dupi=bk_dup, bo_row=bo[None, :].copy(),
                  bv_row=bv[None, :].copy(), tri=tri, ident_b=ident_b,
                  ident_f=ident_f, onesf=onesf, cst=cstv, esel=esel,
                  eselT=eselT, tsel=tsel, tselT=tselT)

    in_maps = []
    npc = c.NCORE // c.B
    for core in range(c.NCORE):
        seq, ch = divmod(core, npc)
        t0 = ch * T
        xi = np.zeros((T + 256, D), f32)
        lo, hi = t0 - 128, t0 + T + 128
        slo, shi = max(0, lo), min(S, hi)
        xi[slo - lo: shi - lo] = x[seq, slo:shi]
        first = 1.0 if ch > 0 else 0.0
        last = 1.0 if ch < npc - 1 else 0.0
        pc = np.zeros((128, 6), f32)
        pc[:, 0] = first
        pc[:, 1] = 1.0 - first
        pc[:, 2] = last
        pc[:, 3] = 1.0 - last
        pc[:, 4] = first
        pc[:, 5] = last
        sh = np.zeros((128, 32), f32)
        if ch > 0:
            sh[:, 2 * (core - 1) + 1] = 1.0
        if ch < npc - 1:
            sh[:, 16 + 2 * (core + 1)] = 1.0
        m = dict(shared)
        m["x_in"] = xi
        m["pcst"] = pc
        m["sel_halo"] = sh
        in_maps.append(m)
    return in_maps


_NC_CACHE = {}


def _get_nc(cfg_key):
    if cfg_key not in _NC_CACHE:
        T, MF, n_sink = cfg_key
        _NC_CACHE[cfg_key] = MK(Cfg(T=T, MF=MF, n_sink=n_sink)).build()
    return _NC_CACHE[cfg_key]


def kernel(**inputs):
    cfg = Cfg()
    in_maps = _prep_inputs(cfg, inputs)
    nc = _get_nc((cfg.T, cfg.MF, cfg.n_sink))
    res = run_bass_kernel_spmd(nc, in_maps, core_ids=list(range(cfg.NCORE)))
    out = np.empty((cfg.B, cfg.S, cfg.D), np.float32)
    npc = cfg.NCORE // cfg.B
    for core in range(cfg.NCORE):
        seq, ch = divmod(core, npc)
        out[seq, ch * cfg.T:(ch + 1) * cfg.T] = res.results[core]["out"]
    return out



# version 05
# speedup vs baseline: 1.1162x; 1.1162x over previous
import numpy as np
from contextlib import ExitStack

import concourse.bass as bass
import concourse.mybir as mybir
from concourse.bass_utils import run_bass_kernel_spmd
import ml_dtypes

BFH = ml_dtypes.bfloat16

F32 = mybir.dt.float32
BF16 = mybir.dt.bfloat16
AF = mybir.ActivationFunctionType
ALU = mybir.AluOpType
AX = mybir.AxisListType

C_EPS, C_NEGHALF, C_NEG20, C_EIGHTH, C_ONE, C_NEG1E30, C_ZERO = range(7)
NCST = 7


class Cfg:
    def __init__(self, T=2048, MF=32, n_sink=20):
        self.NCORE = 8
        self.B = 2
        self.D = 1024
        self.DC = 8
        self.T = T
        self.NT = T // 128
        self.NTK = self.NT + 2
        self.S = (self.NCORE // self.B) * T
        self.HA = 16
        self.HB = 16
        self.E = 8
        self.MF = MF
        self.F = MF * 128
        self.GROUPS = [list(range(i, min(i + 8, MF)))
                       for i in range(0, MF, 8)]
        self.GMAX = max(len(g) for g in self.GROUPS)
        self.n_sink = n_sink
        self.TA = self.NCORE * T
        self.JW = self.TA // 16
        self.TT = self._tiles(T, 512)
        self.TTK = self._tiles(T + 256, 512)

    @staticmethod
    def _tiles(total, sz):
        out, off = [], 0
        while off < total:
            n = min(sz, total - off)
            out.append((off, n))
            off += n
        return out


class MK:
    def __init__(self, cfg):
        self.c = cfg
        self.nc = bass.Bass("TRN2", target_bir_lowering=False)
        self.cnt = {}
        self.sem = {}
        self.uid = 0

    def mksem(self, name):
        self.sem[name] = self.ctx.enter_context(self.nc.semaphore(name))
        self.cnt[name] = 0

    def inc(self, instr, name, n=1):
        instr.then_inc(self.sem[name], n)
        self.cnt[name] += n
        return self.cnt[name]

    def w(self, eng, name, val):
        if val > 0:
            eng.wait_ge(self.sem[name], val)

    def fence(self, eng, instr):
        n = self.inc(instr, "F")
        eng.wait_ge(self.sem["F"], n)

    def barrier(self):
        snap = dict(self.cnt)

        def body(e):
            for nm in ("T", "V", "A", "G", "Y", "C", "F",
                       "Wa", "Wd0", "Wd1", "Wg0", "Wg1", "Wc"):
                self.w(e, nm, snap[nm])
        for en in ("tensor", "vector", "scalar", "gpsimd", "sync"):
            getattr(self.block, en)(body)

    def sb(self, ctx, name, shape, dt):
        self.uid += 1
        return ctx.enter_context(
            self.nc.sbuf_tensor(f"{name}_{self.uid}", shape, dt))

    def ps(self, ctx, name, shape, dt):
        self.uid += 1
        return ctx.enter_context(
            self.nc.psum_tensor(f"{name}_{self.uid}", shape, dt))

    def build(self):
        c = self.c
        nc = self.nc
        D, DC, T, NT, NTK = c.D, c.DC, c.T, c.NT, c.NTK

        dtens = lambda n, s, d, k: nc.dram_tensor(n, s, d, kind=k)
        x_in = dtens("x_in", [T + 256, D], F32, "ExternalInput")
        wq = dtens("wq", [D, D], F32, "ExternalInput")
        wk_dup = dtens("wk_dup", [D, 256], F32, "ExternalInput")
        wv = dtens("wv", [D, 128], F32, "ExternalInput")
        wo = dtens("wo", [D, D], F32, "ExternalInput")
        wqkv = dtens("wqkv", [D, 3 * D], F32, "ExternalInput")
        gate_wt = dtens("gate_wt", [D, 8], F32, "ExternalInput")
        wg_t = dtens("wg_t", [c.E, D, c.F], F32, "ExternalInput")
        wu_t = dtens("wu_t", [c.E, D, c.F], F32, "ExternalInput")
        wd_t = dtens("wd_t", [c.E, c.F, D], F32, "ExternalInput")
        bq_t = dtens("bq_t", [128, DC], F32, "ExternalInput")
        bk_dupi = dtens("bk_dupi", [128, 2], F32, "ExternalInput")
        bo_row = dtens("bo_row", [1, D], F32, "ExternalInput")
        bv_row = dtens("bv_row", [1, 128], F32, "ExternalInput")
        tri = dtens("tri", [128, 256], BF16, "ExternalInput")
        tri4 = dtens("tri4", [128, 1024], BF16, "ExternalInput")
        ones_r = dtens("ones_r", [1, 128], mybir.dt.float32r,
                       "ExternalInput")
        ident_b = dtens("ident_b", [128, 128], BF16, "ExternalInput")
        ident_f = dtens("ident_f", [128, 128], F32, "ExternalInput")
        onesf = dtens("onesf", [1, 128], F32, "ExternalInput")
        cst = dtens("cst", [128, NCST], F32, "ExternalInput")
        pcst = dtens("pcst", [128, 6], F32, "ExternalInput")
        sel_halo = dtens("sel_halo", [128, 32], F32, "ExternalInput")
        esel = dtens("esel", [128, 8], F32, "ExternalInput")
        eselT = dtens("eselT", [8, 128], F32, "ExternalInput")
        tsel = dtens("tsel", [128, 16], F32, "ExternalInput")
        tselT = dtens("tselT", [16, 128], F32, "ExternalInput")
        out_t = dtens("out", [T, D], F32, "ExternalOutput")
        cc_la_in = nc.dram_tensor("cc_la_in", [8, T], F32)
        cc_la_out = nc.dram_tensor("cc_la_out", [8 * 8, T], F32)
        cc_h_in = nc.dram_tensor("cc_h_in", [256, D], F32)
        cc_h_out = nc.dram_tensor("cc_h_out", [8 * 256, D], F32)

        with ExitStack() as ctx:
            self.ctx = ctx
            for s in ("T", "V", "A", "G", "Y", "C", "F",
                      "Wa", "Wd0", "Wd1", "Wg0", "Wg1", "Wc"):
                self.mksem(s)
            sb, ps = self.sb, self.ps
            x = sb(ctx, "x", [128, NTK, D], F32)
            cst_s = sb(ctx, "cst_s", [128, NCST], F32)
            pcst_s = sb(ctx, "pcst_s", [128, 6], F32)
            selh_s = sb(ctx, "selh_s", [128, 32], F32)
            tri_s = sb(ctx, "tri_s", [128, 256], BF16)
            tri4_s = sb(ctx, "tri4_s", [128, 1024], BF16)
            idb_s = sb(ctx, "idb_s", [128, 128], BF16)
            idf_s = sb(ctx, "idf_s", [128, 128], F32)
            onesf_s = sb(ctx, "onesf_s", [1, 128], F32)
            bq_s = sb(ctx, "bq_s", [128, DC], F32)
            bkd_s = sb(ctx, "bkd_s", [128, 2], F32)
            esel_s = sb(ctx, "esel_s", [128, 8], F32)
            eselT_s = sb(ctx, "eselT_s", [8, 128], F32)
            tsel_s = sb(ctx, "tsel_s", [128, 16], F32)
            tselT_s = sb(ctx, "tselT_s", [16, 128], F32)
            gw_s = sb(ctx, "gw_s", [128, DC, 8], BF16)
            W_sel = sb(ctx, "W_sel", [128, NT, 8], F32)
            la1own = sb(ctx, "la1own", [128, NT, 8], F32)
            colb = sb(ctx, "colb", [128, c.n_sink, 8], F32)
            sk_t = sb(ctx, "sk_t", [128, NT, 8], F32)
            sk_m = sb(ctx, "sk_m", [128, NT, 2], F32)
            pi_s = sb(ctx, "pi_s", [128, NT, 8], F32)
            m1_s = sb(ctx, "m1_s", [128, NT, 8], F32)
            m2_s = sb(ctx, "m2_s", [128, NT, 8], F32)
            ln_st = sb(ctx, "ln_st", [128, 2, 12], F32)
            ln_mv = sb(ctx, "ln_mv", [128, 2, 2], F32)
            ln_r = sb(ctx, "ln_r", [128, 2, 2], F32)
            ln_tm = sb(ctx, "ln_tm", [128, 2, D], BF16)
            xn = sb(ctx, "xn", [128, DC, T + 256], BF16)

            self.t = dict(x=x, cst=cst_s, pcst=pcst_s, selh=selh_s, tri=tri_s,
                          tri4=tri4_s,
                          idb=idb_s, idf=idf_s, onesf=onesf_s, bq=bq_s,
                          bkd=bkd_s, esel=esel_s,
                          eselT=eselT_s, tsel=tsel_s, tselT=tselT_s, gw=gw_s,
                          bo_dram=bo_row, bv_dram=bv_row,
                          W_sel=W_sel, la1own=la1own, colb=colb, sk_t=sk_t,
                          sk_m=sk_m, pi=pi_s, m1=m1_s, m2=m2_s,
                          ln_st=ln_st, ln_mv=ln_mv,
                          ln_r=ln_r, ln_tm=ln_tm, xn=xn)
            self.dram = dict(wq=wq, wk_dup=wk_dup, wv=wv, wo=wo, wqkv=wqkv,
                             ones_r=ones_r,
                             gate_wt=gate_wt, wg_t=wg_t, wu_t=wu_t, wd_t=wd_t,
                             cc_la_in=cc_la_in, cc_la_out=cc_la_out,
                             cc_h_in=cc_h_in, cc_h_out=cc_h_out)

            self.block = ctx.enter_context(nc.Block())
            blk = self.block

            def p0_sync(e):
                for dst, src in ((cst_s, cst), (pcst_s, pcst), (selh_s, sel_halo),
                                 (tri_s, tri), (tri4_s, tri4),
                                 (idb_s, ident_b), (idf_s, ident_f),
                                 (onesf_s, onesf), (bq_s, bq_t), (bkd_s, bk_dupi),
                                 (esel_s, esel), (eselT_s, eselT),
                                 (tsel_s, tsel), (tselT_s, tselT)):
                    self.inc(e.dma_start(dst[:], src[:]), "Y", 16)
                self.inc(e.dma_start(
                    x[:], x_in.ap().rearrange("(t p) d -> p t d", p=128)), "Y", 16)
            blk.sync(p0_sync)

            def p0_gps(e):
                self.inc(e.dma_start(gw_s[:], gate_wt.ap().rearrange(
                    "(kc p) e -> p kc e", p=128)), "Wa", 16)
            blk.gpsimd(p0_gps)
            self.barrier()

            self._ln(range(NTK), toff=0)
            self._attn_a()
            self.barrier()

            self._ln(range(1, NT + 1), toff=128)
            self._sinkhorn()
            self._ffn()
            self.barrier()

            self._halo()
            self.barrier()

            self._ln(range(NTK), toff=0)
            self._attn_b()
            self.barrier()

            self._ln(range(1, NT + 1), toff=128)
            self._sinkhorn()
            self._ffn()
            self.barrier()

            def pout(e):
                self.inc(e.dma_start(
                    out_t.ap().rearrange("(t p) d -> p t d", p=128),
                    x[:, 1:NT + 1, :]), "Y", 16)
                self.w(e, "Y", self.cnt["Y"])
            blk.sync(pout)

        return nc

    def _ln(self, tiles, toff):
        c = self.c
        t = self.t
        tiles = list(tiles)
        with ExitStack() as pctx:
            pt = [self.ps(pctx, "ln_pt0", [128, 1024], BF16),
                  self.ps(pctx, "ln_pt1", [128, 1024], BF16)]
            x, xn = t["x"], t["xn"]
            ln_st, ln_mv, ln_r, ln_tm = t["ln_st"], t["ln_mv"], t["ln_r"], t["ln_tm"]
            cst_s, idb_s = t["cst"], t["idb"]

            v0, a0, t0 = self.cnt["V"], self.cnt["A"], self.cnt["T"]
            V_STATS = lambda i: v0 + 4 * i + 1
            V_AGGR = lambda i: v0 + 4 * i + 2
            V_NORM = lambda i: v0 + 4 * i + 3
            V_COPY = lambda i: v0 + 4 * i + 4
            A_LN = lambda i: a0 + 2 * i + 1
            A_EXP = lambda i: a0 + 2 * i + 2
            T_TR = lambda i: t0 + i + 1

            def vbody(e):
                for i, ti in enumerate(tiles):
                    r = i % 2
                    if i >= 2:
                        self.w(e, "A", A_EXP(i - 2))
                    e.bn_stats(ln_st[:, r, 0:6], x[:, ti, 0:512])
                    self.inc(e.bn_stats(ln_st[:, r, 6:12], x[:, ti, 512:1024]), "V")
                    self.w(e, "V", V_STATS(i))
                    self.inc(e.bn_aggr(ln_mv[:, r, :], ln_st[:, r, :]), "V")
                    self.w(e, "A", A_EXP(i))
                    if i >= 2:
                        self.w(e, "T", T_TR(i - 2))
                    self.inc(e.tensor_scalar(
                        ln_tm[:, r, :], x[:, ti, :], ln_mv[:, r, 0:1],
                        ln_r[:, r, 1:2], op0=ALU.subtract, op1=ALU.mult), "V")
                    self.w(e, "T", T_TR(i))
                    for dc in range(c.DC):
                        ins = e.tensor_copy(
                            xn[:, dc, toff + i * 128: toff + (i + 1) * 128],
                            pt[r][:, dc * 128:(dc + 1) * 128])
                    self.inc(ins, "V")

            def abody(e):
                for i, ti in enumerate(tiles):
                    r = i % 2
                    self.w(e, "V", V_AGGR(i))
                    ins = e.activation(ln_r[:, r, 0:1], ln_mv[:, r, 1:2],
                                       AF.Ln, bias=cst_s[:, C_EPS:C_EPS + 1])
                    self.inc(ins, "A")
                    self.w(e, "A", A_LN(i))
                    self.inc(e.activation(
                        ln_r[:, r, 1:2], ln_r[:, r, 0:1], AF.Exp,
                        scale=cst_s[:, C_NEGHALF:C_NEGHALF + 1]), "A")

            def tbody(e):
                for i, ti in enumerate(tiles):
                    r = i % 2
                    self.w(e, "V", V_NORM(i))
                    if i >= 2:
                        self.w(e, "V", V_COPY(i - 2))
                    for dc in range(c.DC):
                        ins = e.transpose(pt[r][:, dc * 128:(dc + 1) * 128],
                                          ln_tm[:, r, dc * 128:(dc + 1) * 128],
                                          idb_s[:])
                    self.inc(ins, "T")

            self.block.vector(vbody)
            self.block.scalar(abody)
            self.block.tensor(tbody)

    def _attn_a(self):
        c = self.c
        t = self.t
        NT, DC, T, NTK = c.NT, c.DC, c.T, c.NTK
        blk = self.block
        x, xn = t["x"], t["xn"]
        with ExitStack() as pctx:
            sb, ps = self.sb, self.ps
            q_sb = sb(pctx, "q_sb", [128, 8, T], BF16)
            k_sb = sb(pctx, "k_sb", [128, 2, T + 256], BF16)
            v_tok = sb(pctx, "v_tok", [128, NTK, 2, 66], BF16)
            ao_fm = xn[:, :, 0:T]
            exp_sb = sb(pctx, "exp_sb", [128, 2, 3, 512], BF16)
            F32R = mybir.dt.float32r
            rc16 = sb(pctx, "rc16", [1, 2, 512], F32R)
            rc64 = sb(pctx, "rc64", [64, 2, 512], F32)
            ones64 = sb(pctx, "ones64", [1, 128], F32R)
            bo_s = sb(pctx, "bo_s", [1, c.D], F32)
            bv_s = sb(pctx, "bv_s", [1, 128], F32)
            ictx = ExitStack()
            wq_b = sb(ictx, "wq_b", [128, DC, c.D], BF16)
            wkd_b = sb(ictx, "wkd_b", [128, DC, 256], BF16)
            wv_b = sb(ictx, "wv_b", [128, DC, 128], BF16)
            pool = ps(pctx, "a_pool", [128, 8, 512], F32)
            PQ = [pool[:, 0, :], pool[:, 1, :]]

            cst_s, pcst_s, tri_s = t["cst"], t["pcst"], t["tri"]
            idb_s, onesf_s = t["idb"], t["onesf"]
            bq_s, bkd_s = t["bq"], t["bkd"]
            dr = self.dram

            w0 = self.cnt["Wa"]

            def gbody(e):
                for dst, src in ((wq_b, dr["wq"]), (wkd_b, dr["wk_dup"]),
                                 (wv_b, dr["wv"])):
                    self.inc(e.dma_start(dst[:], src.ap().rearrange(
                        "(kc p) f -> p kc f", p=128)), "Wa", 16)
                self.inc(e.dma_start(bo_s[:], t["bo_dram"][:, :]), "Wa", 16)
                self.inc(e.dma_start(bv_s[:], t["bv_dram"][:, :]), "Wa", 16)
                self.inc(e.dma_start(ones64[:], self.dram["ones_r"][:, :]),
                         "Wa", 16)
                self.inc(e.memset(v_tok[:, :, :, 64:66], 1.0), "G")
            blk.gpsimd(gbody)
            W_QKV = w0 + 96
            G_ONES = self.cnt["G"]
            v_ln_done = self.cnt["V"]

            t0, v0 = self.cnt["T"], self.cnt["V"]
            T_PROJ = lambda j: t0 + j + 1
            V_PROJ = lambda j: v0 + j + 1
            nQ, nK, nV = DC * len(c.TT), 2 * len(c.TTK), NTK

            def tproj(e):
                self.w(e, "Wa", W_QKV)
                self.w(e, "V", v_ln_done)
                j = 0
                for oc in range(DC):
                    for (off, sz) in c.TT:
                        if j >= 2:
                            self.w(e, "V", V_PROJ(j - 2))
                        b = j % 2
                        for k in range(DC):
                            ins = e.matmul(pool[:, b, 0:sz],
                                           wq_b[:, k, oc * 128:(oc + 1) * 128],
                                           xn[:, k, 128 + off:128 + off + sz],
                                           start=(k == 0), stop=(k == DC - 1))
                        self.inc(ins, "T")
                        j += 1
                for oc2 in range(2):
                    for (off, sz) in c.TTK:
                        if j >= 2:
                            self.w(e, "V", V_PROJ(j - 2))
                        b = j % 2
                        for k in range(DC):
                            ins = e.matmul(pool[:, b, 0:sz],
                                           wkd_b[:, k, oc2 * 128:(oc2 + 1) * 128],
                                           xn[:, k, off:off + sz],
                                           start=(k == 0), stop=(k == DC - 1))
                        self.inc(ins, "T")
                        j += 1
                self.w(e, "G", G_ONES)
                for ti in range(NTK):
                    if j >= 2:
                        self.w(e, "V", V_PROJ(j - 2))
                    b = j % 2
                    for k in range(DC):
                        e.matmul(pool[:, b, 0:128],
                                 xn[:, k, ti * 128:(ti + 1) * 128],
                                 wv_b[:, k, 0:128], start=(k == 0), stop=False)
                    ins = e.matmul(pool[:, b, 0:128], onesf_s[:], bv_s[:],
                                   start=False, stop=True)
                    self.inc(ins, "T")
                    j += 1
            blk.tensor(tproj)

            def vproj(e):
                j = 0
                for oc in range(DC):
                    for (off, sz) in c.TT:
                        self.w(e, "T", T_PROJ(j))
                        b = j % 2
                        for z in range(2):
                            ins = e.tensor_scalar(
                                q_sb[z * 64:z * 64 + 64, oc, off:off + sz],
                                pool[z * 64:z * 64 + 64, b, 0:sz],
                                bq_s[z * 64:z * 64 + 64, oc:oc + 1], None,
                                op0=ALU.add)
                        self.inc(ins, "V")
                        j += 1
                for oc2 in range(2):
                    for (off, sz) in c.TTK:
                        self.w(e, "T", T_PROJ(j))
                        b = j % 2
                        self.inc(e.tensor_scalar(
                            k_sb[:, oc2, off:off + sz], pool[:, b, 0:sz],
                            bkd_s[:, oc2:oc2 + 1], None, op0=ALU.add), "V")
                        j += 1
                for ti in range(NTK):
                    self.w(e, "T", T_PROJ(j))
                    b = j % 2
                    for kv in range(2):
                        dst = v_tok[:, ti, kv, 0:64]
                        src = pool[:, b, kv * 64:kv * 64 + 64]
                        if ti == 0:
                            ins = e.tensor_scalar(dst, src, pcst_s[:, 0:1],
                                                  None, op0=ALU.mult)
                        elif ti == NTK - 1:
                            ins = e.tensor_scalar(dst, src, pcst_s[:, 2:3],
                                                  None, op0=ALU.mult)
                        else:
                            ins = e.tensor_copy(dst, src)
                    self.inc(ins, "V")
                    j += 1
            blk.vector(vproj)
            t_proj_end = self.cnt["T"]
            ictx.close()
            wo_b = sb(pctx, "wo_b", [128, DC, c.D], BF16)
            wo_w0 = self.cnt["Wa"]

            def gbody2(e):
                self.w(e, "T", t_proj_end)
                self.inc(e.dma_start(wo_b[:], dr["wo"].ap().rearrange(
                    "(kc p) f -> p kc f", p=128)), "Wa", 16)
            blk.gpsimd(gbody2)
            W_O = wo_w0 + 16

            v_proj_end = self.cnt["V"]
            U = NT * 4
            T_SC, T_AV, T_BC = {}, {}, {}
            A_EXP, A_RC = {}, {}
            V_MSK, V_RC, V_AO = {}, {}, {}
            cT, cA, cV = self.cnt["T"], self.cnt["A"], self.cnt["V"]
            cT += 1
            T_SC[0] = cT
            for u in range(U):
                if u + 1 < U:
                    cT += 1
                    T_SC[u + 1] = cT
                cT += 1
                T_AV[u] = cT
                cT += 1
                T_BC[u] = cT
            for u in range(U):
                cA += 1
                A_EXP[u] = cA
                if u >= 1:
                    cA += 1
                    A_RC[u - 1] = cA
            cA += 1
            A_RC[U - 1] = cA
            for u in range(U):
                cV += 1
                V_MSK[u] = cV
                cV += 1
                V_RC[u] = cV
                if u >= 1:
                    cV += 1
                    V_AO[u - 1] = cV
            cV += 1
            V_AO[U - 1] = cV
            tri4_s = t["tri4"]

            def tatt(e):
                def sc(u):
                    qt, iu = divmod(u, 4)
                    z, kv = iu % 2, iu // 2
                    i = u % 2
                    if u >= 2:
                        self.w(e, "A", A_EXP[u - 2])
                        self.w(e, "A", A_RC[u - 2])
                    for j, kc in enumerate((qt, qt + 1, qt + 2)):
                        ins = e.matmul(
                            pool[:, 2 + 3 * i + j, :],
                            k_sb[z * 64:z * 64 + 64, kv ^ z,
                                 kc * 128:(kc + 1) * 128],
                            q_sb[z * 64:z * 64 + 64, kv * 4:kv * 4 + 4,
                                 qt * 128:(qt + 1) * 128],
                            start=True, stop=True)
                    self.inc(ins, "T")
                sc(0)
                for u in range(U):
                    qt, iu = divmod(u, 4)
                    z, kv = iu % 2, iu // 2
                    i = u % 2
                    if u + 1 < U:
                        sc(u + 1)
                    self.w(e, "V", V_MSK[u])
                    self.w(e, "V", V_AO[u - 2] if u >= 2 else v_proj_end)
                    for j, kc in enumerate((qt, qt + 1, qt + 2)):
                        ins = e.matmul(pool[0:66, i, :],
                                       v_tok[:, kc, kv, 0:66],
                                       exp_sb[:, i, j, :],
                                       start=(j == 0), stop=(j == 2))
                    self.inc(ins, "T")
                    self.w(e, "V", V_RC[u])
                    ins = e.matmul(pool[:, 2 + 3 * i, :], ones64[:],
                                   rc16[:, i, :], start=True, stop=True)
                    self.inc(ins, "T")
            blk.tensor(tatt)

            def aatt(e):
                def rc_copy(u):
                    i = u % 2
                    self.w(e, "T", T_BC[u])
                    if u >= 2:
                        self.w(e, "V", V_AO[u - 2])
                    self.inc(e.activation(rc64[:, i, :],
                                          pool[0:64, 2 + 3 * i, :],
                                          AF.Copy), "A")
                for u in range(U):
                    i = u % 2
                    self.w(e, "T", T_SC[u])
                    if u >= 2:
                        self.w(e, "T", T_AV[u - 2])
                    for j in range(3):
                        ins = e.activation(exp_sb[:, i, j, :],
                                           pool[:, 2 + 3 * i + j, :], AF.Exp)
                    self.inc(ins, "A")
                    if u >= 1:
                        rc_copy(u - 1)
                rc_copy(U - 1)
            blk.scalar(aatt)

            def vatt(e):
                def ao(u):
                    qt, iu = divmod(u, 4)
                    z, kv = iu % 2, iu // 2
                    i = u % 2
                    self.w(e, "A", A_RC[u])
                    ins = e.tensor_tensor(
                        xn[z * 64:z * 64 + 64, kv * 4:kv * 4 + 4,
                           qt * 128:(qt + 1) * 128],
                        pool[0:64, i, :].rearrange("p (a b) -> p a b", b=128),
                        rc64[:, i, :].rearrange("p (a b) -> p a b", b=128),
                        op=ALU.mult)
                    self.inc(ins, "V")
                for u in range(U):
                    qt, iu = divmod(u, 4)
                    i = u % 2
                    self.w(e, "A", A_EXP[u])
                    if qt == 0:
                        ins = e.tensor_scalar(exp_sb[:, i, 0, :],
                                              exp_sb[:, i, 0, :],
                                              pcst_s[:, 0:1], pcst_s[:, 1:2],
                                              op0=ALU.mult, op1=ALU.add)
                        self.fence(e, ins)
                    e.tensor_tensor(exp_sb[:, i, 0, :], exp_sb[:, i, 0, :],
                                    tri4_s[:, 0:512], op=ALU.mult)
                    if qt == NT - 1:
                        ins = e.tensor_scalar(exp_sb[:, i, 2, :],
                                              exp_sb[:, i, 2, :],
                                              pcst_s[:, 2:3], pcst_s[:, 3:4],
                                              op0=ALU.mult, op1=ALU.add)
                        self.fence(e, ins)
                    ins = e.tensor_tensor(exp_sb[:, i, 2, :],
                                          exp_sb[:, i, 2, :],
                                          tri4_s[:, 512:1024], op=ALU.mult)
                    self.inc(ins, "V")
                    self.w(e, "T", T_AV[u])
                    if u >= 2:
                        self.w(e, "T", T_BC[u - 2])
                    with self.nc.allow_low_precision(
                            reason="fp16 recip broadcast conduit"):
                        ins = e.reciprocal(rc16[:, i, :], pool[64:65, i, :])
                    self.inc(ins, "V")
                    if u >= 1:
                        ao(u - 1)
                ao(U - 1)
            blk.vector(vatt)
            v_att_end = self.cnt["V"]
            a_att_end = self.cnt["A"]

            to0, vo0 = self.cnt["T"], self.cnt["V"]
            T_OP = lambda j: to0 + j + 1
            V_OP = lambda j: vo0 + j + 1

            def topr(e):
                self.w(e, "Wa", W_O)
                self.w(e, "V", v_att_end)
                self.w(e, "A", a_att_end)
                j = 0
                for qt in range(NT):
                    for dh in range(2):
                        if j >= 2:
                            self.w(e, "V", V_OP(j - 2))
                        b = j % 2
                        for k in range(DC):
                            e.matmul(pool[:, b, 0:512],
                                     ao_fm[:, k, qt * 128:(qt + 1) * 128],
                                     wo_b[:, k, dh * 512:(dh + 1) * 512],
                                     start=(k == 0), stop=False)
                        ins = e.matmul(pool[:, b, 0:512], onesf_s[:],
                                       bo_s[0:1, dh * 512:(dh + 1) * 512],
                                       start=False, stop=True)
                        self.inc(ins, "T")
                        j += 1
            blk.tensor(topr)

            def vopr(e):
                j = 0
                for qt in range(NT):
                    for dh in range(2):
                        self.w(e, "T", T_OP(j))
                        b = j % 2
                        self.inc(e.scalar_tensor_tensor(
                            x[:, qt + 1, dh * 512:(dh + 1) * 512],
                            pool[:, b, 0:512],
                            cst_s[:, C_ONE:C_ONE + 1],
                            x[:, qt + 1, dh * 512:(dh + 1) * 512],
                            op0=ALU.mult, op1=ALU.add), "V")
                        j += 1
            blk.vector(vopr)

    def _sinkhorn(self):
        c = self.c
        t = self.t
        NT, NS, DC = c.NT, c.n_sink, c.DC
        blk = self.block
        xn, gw_s = t["xn"], t["gw"]
        la1own, colb, W_sel = t["la1own"], t["colb"], t["W_sel"]
        sk_t, sk_m, pi_s, m1_s, m2_s = t["sk_t"], t["sk_m"], t["pi"], t["m1"], t["m2"]
        cst_s, idf_s, onesf_s = t["cst"], t["idf"], t["onesf"]
        esel_s, eselT_s, tsel_s, tselT_s = t["esel"], t["eselT"], t["tsel"], t["tselT"]
        dr = self.dram
        jw = c.JW
        njt = (jw + 511) // 512

        with ExitStack() as pctx:
            la_em = self.sb(pctx, "la_em", [8, c.T], F32)
            la_full = self.sb(pctx, "la_full", [128, c.JW], F32)
            exp_f = self.sb(pctx, "exp_f", [128, c.JW], F32)
            psums = self.sb(pctx, "psums", [128, 1], F32)
            rls = self.sb(pctx, "rls", [16, c.JW], F32)
            clsb = self.sb(pctx, "clsb", [8, 1], F32)
            clrow = self.sb(pctx, "clrow", [1, c.n_sink * 8], F32)
            P1 = self.ps(pctx, "sk_p1", [128, 512], F32)
            P2 = self.ps(pctx, "sk_p2", [128, 512], F32)
            jw_pad = max(512, c.JW)
            P3 = self.ps(pctx, "sk_p3", [16, jw_pad], F32)
            P4 = self.ps(pctx, "sk_p4", [128, jw_pad], F32)

            t0, v0, a0 = self.cnt["T"], self.cnt["V"], self.cnt["A"]
            T_LG = lambda i: t0 + 2 * i + 1
            T_TR = lambda i: t0 + 2 * i + 2
            V_RAW = lambda i: v0 + 6 * i + 1
            V_RMX = lambda i: v0 + 6 * i + 2
            V_TT1 = lambda i: v0 + 6 * i + 3
            V_RSM = lambda i: v0 + 6 * i + 4
            V_LA1 = lambda i: v0 + 6 * i + 5
            V_CEM = lambda i: v0 + 6 * i + 6
            A_EXP = lambda i: a0 + 2 * i + 1
            A_LSE = lambda i: a0 + 2 * i + 2

            def tlg(e):
                self.w(e, "V", v0)
                for i in range(NT):
                    if i >= 2:
                        self.w(e, "V", V_RAW(i - 2))
                    p = P1 if i % 2 == 0 else P2
                    for k in range(DC):
                        ins = e.matmul(p[:, 0:8],
                                       xn[:, k, 128 + i * 128:128 + (i + 1) * 128],
                                       gw_s[:, k, :], start=(k == 0),
                                       stop=(k == DC - 1))
                    self.inc(ins, "T")
                    self.w(e, "V", V_LA1(i))
                    if i >= 2:
                        self.w(e, "V", V_CEM(i - 2))
                    ins = e.transpose(P4[0:8, (i % 2) * 128:(i % 2) * 128 + 128],
                                      la1own[:, i, :], idf_s[:])
                    self.inc(ins, "T")
            blk.tensor(tlg)

            def alg(e):
                for i in range(NT):
                    self.w(e, "V", V_TT1(i))
                    self.inc(e.activation(pi_s[:, i, :], la1own[:, i, :],
                                          AF.Exp), "A")
                    self.w(e, "V", V_RSM(i))
                    self.inc(e.activation(sk_m[:, i, 0:1], sk_m[:, i, 1:2],
                                          AF.Ln), "A")
            blk.scalar(alg)

            def vlg(e):
                for i in range(NT):
                    self.w(e, "T", T_LG(i))
                    p = P1 if i % 2 == 0 else P2
                    self.inc(e.tensor_scalar(sk_t[:, i, :], p[:, 0:8],
                                             cst_s[:, C_NEG20:C_NEG20 + 1],
                                             None, op0=ALU.mult), "V")
                    self.w(e, "V", V_RAW(i))
                    self.inc(e.tensor_reduce(sk_m[:, i, 0:1], sk_t[:, i, :],
                                             axis=AX.X, op=ALU.max), "V")
                    self.w(e, "V", V_RMX(i))
                    self.inc(e.tensor_tensor(
                        la1own[:, i, :], sk_t[:, i, :],
                        sk_m[:, i, 0:1].broadcast_to([128, 8]),
                        op=ALU.subtract), "V")
                    self.w(e, "A", A_EXP(i))
                    self.inc(e.tensor_reduce(sk_m[:, i, 1:2], pi_s[:, i, :],
                                             axis=AX.X, op=ALU.add), "V")
                    self.w(e, "A", A_LSE(i))
                    self.inc(e.tensor_tensor(
                        la1own[:, i, :], la1own[:, i, :],
                        sk_m[:, i, 0:1].broadcast_to([128, 8]),
                        op=ALU.subtract), "V")
                    self.w(e, "T", T_TR(i))
                    self.inc(e.tensor_copy(la_em[:, i * 128:(i + 1) * 128],
                                           P4[0:8, (i % 2) * 128:(i % 2) * 128 + 128]),
                             "V")
            blk.vector(vlg)
            v_em_done = self.cnt["V"]

            def gcc(e):
                self.w(e, "V", v_em_done)
                self.inc(e.dma_start(dr["cc_la_in"][:, :], la_em[:]), "Wc", 16)
                self.w(e, "Wc", self.cnt["Wc"])
                self.inc(e.collective_compute(
                    "AllGather", ALU.bypass, replica_groups=[list(range(8))],
                    ins=[dr["cc_la_in"].ap().opt()],
                    outs=[dr["cc_la_out"].ap().opt()]), "C")
                self.w(e, "C", self.cnt["C"])
                self.inc(e.dma_start(
                    la_full[:],
                    dr["cc_la_out"].ap().rearrange(
                        "(r e) (h j) -> (r e h) j", e=8, h=2)), "Wc", 16)
            blk.gpsimd(gcc)
            w_lafull = self.cnt["Wc"]

            seq = [("col", 0)]
            for it in range(1, NS):
                seq += [("row", it), ("col", it)]
            cA, cV, cT = self.cnt["A"], self.cnt["V"], self.cnt["T"]
            marks = []
            for kind, it in seq:
                m = {"kind": kind, "it": it}
                if kind == "col":
                    cA += 1; m["exp"] = cA
                    cV += 1; m["red"] = cV
                    cT += 1; m["mm1"] = cT
                    cA += 1; m["ln"] = cA
                    cT += 1; m["mm2"] = cT
                    cV += 1; m["sub"] = cV
                    cV += 1; m["crow"] = cV
                else:
                    cA += 1; m["exp"] = cA
                    cT += 1; m["mm1"] = cT
                    cA += 1; m["ln"] = cA
                    cT += 1; m["mm2"] = cT
                    cV += 1; m["sub"] = cV
                marks.append(m)

            def tb(e):
                prev_crow = 0
                self.w(e, "Wc", w_lafull)
                for idx, m in enumerate(marks):
                    if m["kind"] == "col":
                        self.w(e, "V", m["red"])
                        if prev_crow:
                            self.w(e, "V", prev_crow)
                        ins = e.matmul(P1[0:8, 0:1], esel_s[:], psums[:],
                                       start=True, stop=True)
                        ins.then_inc(self.sem["T"], 1)
                        self.w(e, "A", m["ln"])
                        e.matmul(P2[0:128, 0:1], eselT_s[:], clsb[:],
                                 start=True, stop=True)
                        ins = e.transpose(P1[0:1, 8:16], clsb[:],
                                          idf_s[0:8, 0:8])
                        ins.then_inc(self.sem["T"], 1)
                        prev_crow = m["crow"]
                    else:
                        self.w(e, "A", m["exp"])
                        for jt in range(njt):
                            o = jt * 512
                            sz = min(512, jw - o)
                            ins = e.matmul(P3[:, o:o + sz], tsel_s[:],
                                           exp_f[:, o:o + sz],
                                           start=True, stop=True)
                        ins.then_inc(self.sem["T"], 1)
                        self.w(e, "A", m["ln"])
                        for jt in range(njt):
                            o = jt * 512
                            sz = min(512, jw - o)
                            ins = e.matmul(P4[:, o:o + sz], tselT_s[:],
                                           rls[:, o:o + sz],
                                           start=True, stop=True)
                        ins.then_inc(self.sem["T"], 1)

            def ab(e):
                for idx, m in enumerate(marks):
                    if idx == 0:
                        self.w(e, "Wc", w_lafull)
                    else:
                        self.w(e, "V", marks[idx - 1]["sub"])
                    ins = e.activation(exp_f[:], la_full[:], AF.Exp)
                    ins.then_inc(self.sem["A"], 1)
                    self.w(e, "T", m["mm1"])
                    if m["kind"] == "col":
                        ins = e.activation(clsb[:], P1[0:8, 0:1], AF.Ln)
                    else:
                        ins = e.activation(rls[:, 0:jw], P3[:, 0:jw], AF.Ln)
                    ins.then_inc(self.sem["A"], 1)

            def vb(e):
                for idx, m in enumerate(marks):
                    if m["kind"] == "col":
                        self.w(e, "A", m["exp"])
                        ins = e.tensor_reduce(psums[:], exp_f[:], axis=AX.X,
                                              op=ALU.add)
                        ins.then_inc(self.sem["V"], 1)
                        self.w(e, "T", m["mm2"])
                        ins = e.tensor_scalar(la_full[:], la_full[:],
                                              P2[0:128, 0:1], None,
                                              op0=ALU.subtract)
                        ins.then_inc(self.sem["V"], 1)
                        ins = e.tensor_copy(
                            clrow[0:1, m["it"] * 8:(m["it"] + 1) * 8],
                            P1[0:1, 8:16])
                        ins.then_inc(self.sem["V"], 1)
                    else:
                        self.w(e, "T", m["mm2"])
                        ins = e.tensor_tensor(la_full[:], la_full[:],
                                              P4[:, 0:jw], op=ALU.subtract)
                        ins.then_inc(self.sem["V"], 1)

            blk.tensor(tb)
            blk.scalar(ab)
            blk.vector(vb)
            self.cnt["T"], self.cnt["V"], self.cnt["A"] = cT, cV, cA
            v_phA = self.cnt["V"]

            def tcolb(e):
                self.w(e, "V", v_phA)
                for it in range(NS):
                    ins = e.matmul(P4[:, it * 8:(it + 1) * 8], onesf_s[:],
                                   clrow[0:1, it * 8:(it + 1) * 8],
                                   start=True, stop=True)
                self.inc(ins, "T")
            blk.tensor(tcolb)
            t_colb = self.cnt["T"]

            def vcolb(e):
                self.w(e, "T", t_colb)
                self.inc(e.tensor_copy(
                    colb[:].rearrange("p a b -> p (a b)"),
                    P4[:, 0:NS * 8]), "V")
            blk.vector(vcolb)
            self.sink_psum_free = (self.cnt["V"], self.cnt["A"], self.cnt["T"])

            a0b = self.cnt["A"]
            A_BEXP = lambda it: a0b + 2 * it + 1
            A_BLN = lambda it: a0b + 2 * it + 2
            A_PI = a0b + 2 * (NS - 1) + 1
            f_tt1 = {}
            f_rsum = {}
            f_colsub_last = [0]
            v_colb_done = self.cnt["V"]

            def vphB(e):
                self.w(e, "V", v_colb_done)
                for it in range(NS):
                    cb = colb[:, it:it + 1, :].broadcast_to([128, NT, 8])
                    ins = e.tensor_tensor(la1own[:], la1own[:], cb,
                                          op=ALU.subtract)
                    self.fence(e, ins)
                    if it == NS - 1:
                        f_colsub_last[0] = self.cnt["F"]
                    if it < NS - 1:
                        self.fence(e, e.tensor_reduce(
                            sk_m[:, :, 0:1], la1own[:], axis=AX.X, op=ALU.max))
                        ins = e.tensor_tensor(
                            sk_t[:], la1own[:],
                            sk_m[:, :, 0:1].broadcast_to([128, NT, 8]),
                            op=ALU.subtract)
                        self.fence(e, ins)
                        f_tt1[it] = self.cnt["F"]
                        self.w(e, "A", A_BEXP(it))
                        self.fence(e, e.tensor_reduce(
                            sk_m[:, :, 1:2], pi_s[:], axis=AX.X, op=ALU.add))
                        f_rsum[it] = self.cnt["F"]
                        self.w(e, "A", A_BLN(it))
                        self.fence(e, e.tensor_tensor(
                            la1own[:], sk_t[:],
                            sk_m[:, :, 0:1].broadcast_to([128, NT, 8]),
                            op=ALU.subtract))
                self.w(e, "A", A_PI)
                self.fence(e, e.tensor_reduce(sk_m[:, :, 0:1], pi_s[:],
                                              axis=AX.X, op=ALU.max))
                self.fence(e, e.tensor_tensor(
                    m1_s[:], pi_s[:],
                    sk_m[:, :, 0:1].broadcast_to([128, NT, 8]),
                    op=ALU.is_equal))
                self.fence(e, e.scalar_tensor_tensor(
                    sk_t[:], m1_s[:], cst_s[:, C_NEG1E30:C_NEG1E30 + 1],
                    pi_s[:], op0=ALU.mult, op1=ALU.add))
                self.fence(e, e.tensor_reduce(sk_m[:, :, 1:2], sk_t[:],
                                              axis=AX.X, op=ALU.max))
                self.fence(e, e.tensor_tensor(
                    m2_s[:], sk_t[:],
                    sk_m[:, :, 1:2].broadcast_to([128, NT, 8]),
                    op=ALU.is_equal))
                self.fence(e, e.tensor_tensor(
                    sk_t[:], m1_s[:],
                    pi_s[:, :, 0:1].broadcast_to([128, NT, 8]), op=ALU.mult))
                self.fence(e, e.tensor_tensor(
                    m1_s[:], m2_s[:],
                    pi_s[:, :, 1:2].broadcast_to([128, NT, 8]), op=ALU.mult))
                self.inc(e.tensor_tensor(W_sel[:], sk_t[:], m1_s[:],
                                         op=ALU.add), "V")
            blk.vector(vphB)
            self.v_wsel = self.cnt["V"]

            la_fl = la1own[:].rearrange("p a b -> p (a b)")
            skt_fl = sk_t[:].rearrange("p a b -> p (a b)")
            pi_fl = pi_s[:].rearrange("p a b -> p (a b)")

            def aphB(e):
                for it in range(NS - 1):
                    self.w(e, "F", f_tt1[it])
                    self.inc(e.activation(pi_fl, skt_fl, AF.Exp), "A")
                    self.w(e, "F", f_rsum[it])
                    self.inc(e.activation(sk_m[:, :, 0:1], sk_m[:, :, 1:2],
                                          AF.Ln), "A")
                self.w(e, "F", f_colsub_last[0])
                self.inc(e.activation(pi_fl, la_fl, AF.Exp), "A")
            blk.scalar(aphB)

    def _ffn(self):
        c = self.c
        t = self.t
        NT, DC, T = c.NT, c.DC, c.T
        blk = self.block
        x, xn, W_sel = t["x"], t["xn"], t["W_sel"]
        cst_s = t["cst"]
        dr = self.dram
        ntt = len(c.TT)

        with ExitStack() as pctx:
            h_b = self.sb(pctx, "h_b", [128, c.GMAX, T], BF16)
            sg_b = self.sb(pctx, "sg_b", [128, 2, 512], BF16)
            wgu_b = self.sb(pctx, "wgu_b", [128, 2, 2, DC, 256], BF16)
            wd_b = self.sb(pctx, "wd_b", [128, 2, c.GMAX, c.D], BF16)
            PG = [self.ps(pctx, "f_pg0", [128, 512], F32),
                  self.ps(pctx, "f_pg1", [128, 512], F32)]
            PU = [self.ps(pctx, "f_pu0", [128, 512], F32),
                  self.ps(pctx, "f_pu1", [128, 512], F32)]
            PYF = [self.ps(pctx, "f_py0", [128, 512], F32),
                   self.ps(pctx, "f_py1", [128, 512], F32)]

            t0, a0, v0 = self.cnt["T"], self.cnt["A"], self.cnt["V"]
            units = []
            for e_i in range(c.E):
                for gi, G in enumerate(c.GROUPS):
                    for mi in range(len(G)):
                        for tti in range(ntt):
                            units.append(("p1", e_i, gi, mi, tti))
                    for tt2 in range(NT):
                        for dh in range(2):
                            units.append(("p2", e_i, gi, tt2, dh))
            T_U = {}
            A_U = {}
            V_U = {}
            cT, cA, cV = t0, a0, v0
            p1list, p2list = [], []
            for u in units:
                cT += 1
                T_U[u] = cT
                if u[0] == "p1":
                    cA += 1
                    A_U[u] = cA
                    p1list.append(u)
                cV += 1
                V_U[u] = cV
                if u[0] == "p2":
                    p2list.append(u)
            pair_of = {}
            PAIR_B = {}
            PAIR_SEQ = []
            gp = 0
            for e_i in range(c.E):
                for gi, G in enumerate(c.GROUPS):
                    nm = len(G)
                    for pi2 in range((nm + 1) // 2):
                        pair_of[(e_i, gi, pi2)] = list(range(2 * pi2,
                                                             min(2 * pi2 + 2, nm)))
                        PAIR_B[(e_i, gi, pi2)] = gp % 2
                        PAIR_SEQ.append((e_i, gi, pi2))
                        gp += 1
            T_PAIR_DONE = {}
            for (e_i, gi, pi2), mis in pair_of.items():
                T_PAIR_DONE[(e_i, gi, pi2)] = T_U[("p1", e_i, gi, mis[-1],
                                                   ntt - 1)]
            T_P2_DONE = {}
            V_P2_DONE = {}
            for e_i in range(c.E):
                for gi in range(len(c.GROUPS)):
                    T_P2_DONE[(e_i, gi)] = T_U[("p2", e_i, gi, NT - 1, 1)]
                    V_P2_DONE[(e_i, gi)] = V_U[("p2", e_i, gi, NT - 1, 1)]
            V_P1_DONE = {}
            for e_i in range(c.E):
                for gi, G in enumerate(c.GROUPS):
                    V_P1_DONE[(e_i, gi)] = V_U[("p1", e_i, gi, len(G) - 1,
                                                ntt - 1)]

            W_WD = {}
            W_PAIR = {}
            cWd = {0: self.cnt["Wd0"], 1: self.cnt["Wd1"]}
            cWg = {0: self.cnt["Wg0"], 1: self.cnt["Wg1"]}
            negl = len(c.GROUPS)
            for e_i in range(c.E):
                for gi, G in enumerate(c.GROUPS):
                    eg = e_i * negl + gi
                    bd = eg % 2
                    cWd[bd] += 16
                    W_WD[(e_i, gi)] = (bd, cWd[bd])
                    for pi2 in range((len(G) + 1) // 2):
                        b = PAIR_B[(e_i, gi, pi2)]
                        cWg[b] += 32
                        W_PAIR[(e_i, gi, pi2)] = (b, cWg[b])

            def prev_eg(e_i, gi):
                k = e_i * len(c.GROUPS) + gi - 1
                if k < 0:
                    return None
                return divmod(k, len(c.GROUPS))

            v_psum_free, a_psum_free, t_psum_free = self.sink_psum_free

            def gstream(e):
                for e_i in range(c.E):
                    wg_e = dr["wg_t"].ap()[e_i].rearrange("(kc p) f -> p kc f", p=128)
                    wu_e = dr["wu_t"].ap()[e_i].rearrange("(kc p) f -> p kc f", p=128)
                    wd_e = dr["wd_t"].ap()[e_i].rearrange("(mc p) d -> p mc d", p=128)
                    for gi, G in enumerate(c.GROUPS):
                        eg = e_i * len(c.GROUPS) + gi
                        if eg >= 2:
                            pv2 = divmod(eg - 2, len(c.GROUPS))
                            self.w(e, "T", T_P2_DONE[pv2])
                        self.inc(e.dma_start(
                            wd_b[:, eg % 2, 0:len(G), :],
                            wd_e[:, G[0]:G[0] + len(G), :],
                            max_dma_last_dim=8192), f"Wd{eg % 2}", 16)
                        npair = (len(G) + 1) // 2
                        for pi2 in range(npair):
                            mis = pair_of[(e_i, gi, pi2)]
                            mw = len(mis) * 128
                            f0 = (G[0] + mis[0]) * 128
                            gidx = PAIR_SEQ.index((e_i, gi, pi2))
                            if gidx >= 2:
                                self.w(e, "T", T_PAIR_DONE[PAIR_SEQ[gidx - 2]])
                            b = PAIR_B[(e_i, gi, pi2)]
                            self.inc(e.dma_start(
                                wgu_b[:, b, 0, :, 0:mw],
                                wg_e[:, :, f0:f0 + mw],
                                max_dma_last_dim=8192), f"Wg{b}", 16)
                            self.inc(e.dma_start(
                                wgu_b[:, b, 1, :, 0:mw],
                                wu_e[:, :, f0:f0 + mw],
                                max_dma_last_dim=8192), f"Wg{b}", 16)
            blk.gpsimd(gstream)


            def tffn(e):
                self.w(e, "V", v_psum_free)
                self.w(e, "A", a_psum_free)
                p1i = 0
                p2i = 0
                for e_i in range(c.E):
                    for gi, G in enumerate(c.GROUPS):
                        for mi in range(len(G)):
                            pi2, z = divmod(mi, 2)
                            b = PAIR_B[(e_i, gi, pi2)]
                            for tti, (off, sz) in enumerate(c.TT):
                                u = ("p1", e_i, gi, mi, tti)
                                if tti == 0 and z == 0:
                                    wb_, wv_ = W_PAIR[(e_i, gi, pi2)]
                                    self.w(e, f"Wg{wb_}", wv_)
                                ii = p1i % 2
                                if p1i >= 2:
                                    pu = p1list[p1i - 2]
                                    self.w(e, "A", A_U[pu])
                                    self.w(e, "V", V_U[pu])
                                for k in range(DC):
                                    e.matmul(PG[ii][:, 0:sz],
                                             wgu_b[:, b, 0, k,
                                                   z * 128:(z + 1) * 128],
                                             xn[:, k, 128 + off:128 + off + sz],
                                             start=(k == 0), stop=(k == DC - 1))
                                for k in range(DC):
                                    ins = e.matmul(
                                        PU[ii][:, 0:sz],
                                        wgu_b[:, b, 1, k, z * 128:(z + 1) * 128],
                                        xn[:, k, 128 + off:128 + off + sz],
                                        start=(k == 0), stop=(k == DC - 1))
                                self.inc(ins, "T")
                                assert self.cnt["T"] == T_U[u]
                                p1i += 1
                        eg = e_i * len(c.GROUPS) + gi
                        wdb_, wdv_ = W_WD[(e_i, gi)]
                        self.w(e, f"Wd{wdb_}", wdv_)
                        for tt2 in range(NT):
                            for dh in range(2):
                                u = ("p2", e_i, gi, tt2, dh)
                                jj = p2i % 2
                                if p2i >= 2:
                                    self.w(e, "V", V_U[p2list[p2i - 2]])
                                if tt2 == 0 and dh == 0:
                                    self.w(e, "V", V_P1_DONE[(e_i, gi)])
                                for mi in range(len(G)):
                                    ins = e.matmul(
                                        PYF[jj][:, 0:512],
                                        h_b[:, mi, tt2 * 128:(tt2 + 1) * 128],
                                        wd_b[:, eg % 2, mi,
                                             dh * 512:(dh + 1) * 512],
                                        start=(mi == 0),
                                        stop=(mi == len(G) - 1))
                                self.inc(ins, "T")
                                assert self.cnt["T"] == T_U[u]
                                p2i += 1
            blk.tensor(tffn)

            def affn(e):
                p1i = 0
                for e_i in range(c.E):
                    for gi, G in enumerate(c.GROUPS):
                        for mi in range(len(G)):
                            for tti, (off, sz) in enumerate(c.TT):
                                u = ("p1", e_i, gi, mi, tti)
                                ii = p1i % 2
                                self.w(e, "T", T_U[u])
                                if p1i >= 2:
                                    self.w(e, "V", V_U[p1list[p1i - 2]])
                                self.inc(e.activation(sg_b[:, ii, 0:sz],
                                                      PG[ii][:, 0:sz],
                                                      AF.Silu), "A")
                                assert self.cnt["A"] == A_U[u]
                                p1i += 1
            blk.scalar(affn)

            wsel_fence = self.v_wsel
            prev_evac = {}

            def vffn(e):
                self.w(e, "V", wsel_fence)
                self.w(e, "A", a_psum_free)
                self.w(e, "T", t_psum_free)
                p1i = 0
                p2i = 0
                for e_i in range(c.E):
                    for gi, G in enumerate(c.GROUPS):
                        for mi in range(len(G)):
                            for tti, (off, sz) in enumerate(c.TT):
                                u = ("p1", e_i, gi, mi, tti)
                                ii = p1i % 2
                                self.w(e, "A", A_U[u])
                                if mi == 0 and tti == 0:
                                    pv = prev_eg(e_i, gi)
                                    if pv is not None:
                                        self.w(e, "T", T_P2_DONE[pv])
                                self.inc(e.tensor_tensor(
                                    h_b[:, mi, off:off + sz],
                                    sg_b[:, ii, 0:sz], PU[ii][:, 0:sz],
                                    op=ALU.mult), "V")
                                assert self.cnt["V"] == V_U[u]
                                p1i += 1
                        for tt2 in range(NT):
                            for dh in range(2):
                                u = ("p2", e_i, gi, tt2, dh)
                                jj = p2i % 2
                                self.w(e, "T", T_U[u])
                                key = (tt2, dh)
                                if key in prev_evac:
                                    self.w(e, "V", prev_evac[key])
                                self.inc(e.scalar_tensor_tensor(
                                    x[:, tt2 + 1, dh * 512:(dh + 1) * 512],
                                    PYF[jj][:, 0:512],
                                    W_sel[:, tt2, e_i:e_i + 1],
                                    x[:, tt2 + 1, dh * 512:(dh + 1) * 512],
                                    op0=ALU.mult, op1=ALU.add), "V")
                                prev_evac[key] = self.cnt["V"]
                                assert self.cnt["V"] == V_U[u]
                                p2i += 1
            blk.vector(vffn)

    def _halo(self):
        c = self.c
        t = self.t
        blk = self.block
        x, selh_s = t["x"], t["selh"]
        dr = self.dram
        NT = c.NT

        with ExitStack() as pctx:
            G_sb = self.sb(pctx, "G_sb", [128, 16, c.D], F32)

            y0 = self.cnt["Y"]

            def ysync(e):
                self.inc(e.dma_start(dr["cc_h_in"][0:128, :], x[:, 1, :]),
                         "Y", 16)
                self.inc(e.dma_start(dr["cc_h_in"][128:256, :], x[:, NT, :]),
                         "Y", 16)
            blk.sync(ysync)

            def gcc(e):
                self.w(e, "Y", y0 + 32)
                self.inc(e.collective_compute(
                    "AllGather", ALU.bypass, replica_groups=[list(range(8))],
                    ins=[dr["cc_h_in"].ap().opt()],
                    outs=[dr["cc_h_out"].ap().opt()]), "C")
                self.w(e, "C", self.cnt["C"])
                self.inc(e.dma_start(G_sb[:], dr["cc_h_out"].ap().rearrange(
                    "(t p) d -> p t d", p=128)), "Wc", 16)
            blk.gpsimd(gcc)
            w_g = self.cnt["Wc"]

            def vsel(e):
                self.w(e, "Wc", w_g)
                for side, dst_ti, col0 in ((0, 0, 0), (1, c.NTK - 1, 16)):
                    ins = e.tensor_scalar(x[:, dst_ti, :], G_sb[:, 0, :],
                                          selh_s[:, col0:col0 + 1], None,
                                          op0=ALU.mult)
                    self.fence(e, ins)
                    for ti in range(1, 16):
                        ins = e.scalar_tensor_tensor(
                            x[:, dst_ti, :], G_sb[:, ti, :],
                            selh_s[:, col0 + ti:col0 + ti + 1],
                            x[:, dst_ti, :], op0=ALU.mult, op1=ALU.add)
                        self.fence(e, ins)
            blk.vector(vsel)

    def _attn_b(self):
        c = self.c
        t = self.t
        NT, DC, T, NTK, HB = c.NT, c.DC, c.T, c.NTK, c.HB
        blk = self.block
        x, xn = t["x"], t["xn"]
        cst_s, pcst_s, idb_s = t["cst"], t["pcst"], t["idb"]
        dr = self.dram

        with ExitStack() as pctx:
            sb, ps = self.sb, self.ps
            wb_b = sb(pctx, "wb_b", [128, DC, 1536], BF16)
            q_sb = sb(pctx, "q_sb_b", [128, 4, T], BF16)
            k_sb = sb(pctx, "k_sb_b", [128, 4, T + 256], BF16)
            v_tok = sb(pctx, "v_tok_b", [128, NTK, 8, 66], BF16)
            er = sb(pctx, "er_b", [128, 2, 2, 128], BF16)
            exp_sb = sb(pctx, "exp_sb_b", [128, 2, 384], BF16)
            rc_sb = sb(pctx, "rc_sb_b", [128, 2, 1], F32)
            PQ = [ps(pctx, "b_pq0", [128, 512], F32),
                  ps(pctx, "b_pq1", [128, 512], F32)]
            PS_ = [ps(pctx, "b_ps0", [128, 512], F32),
                   ps(pctx, "b_ps1", [128, 512], F32)]
            PY = [ps(pctx, "b_py0", [128, 512], F32),
                  ps(pctx, "b_py1", [128, 512], F32)]

            v_ln_done = self.cnt["V"]
            for grp in range(2):
                w0 = self.cnt["Wa"]

                def gw(e, grp=grp):
                    if grp > 0:
                        self.w(e, "T", self._b_t_end)
                        self.w(e, "V", self._b_v_end)
                    for part in range(3):
                        src = dr["wqkv"].ap().rearrange(
                            "(kc p) f -> p kc f", p=128)[
                            :, :, part * c.D + grp * 512:
                            part * c.D + grp * 512 + 512]
                        self.inc(e.dma_start(wb_b[:, :, part * 512:
                                                   part * 512 + 512], src),
                                 "Wa", 16)
                    self.inc(e.memset(v_tok[:, :, :, 64:66], 1.0), "G")
                blk.gpsimd(gw)
                W_G = w0 + 48
                G_ONES = self.cnt["G"]

                t0, v0 = self.cnt["T"], self.cnt["V"]
                T_PROJ = lambda j: t0 + j + 1
                V_PROJ = lambda j: v0 + j + 1

                def tproj(e, grp=grp):
                    self.w(e, "Wa", W_G)
                    self.w(e, "V", v_ln_done)
                    self.w(e, "G", G_ONES)
                    j = 0
                    for oc in range(4):
                        for (off, sz) in c.TT:
                            if j >= 2:
                                self.w(e, "V", V_PROJ(j - 2))
                            p = PQ[j % 2]
                            for k in range(DC):
                                ins = e.matmul(
                                    p[:, 0:sz],
                                    wb_b[:, k, oc * 128:(oc + 1) * 128],
                                    xn[:, k, 128 + off:128 + off + sz],
                                    start=(k == 0), stop=(k == DC - 1))
                            self.inc(ins, "T")
                            j += 1
                    for oc in range(4):
                        for (off, sz) in c.TTK:
                            if j >= 2:
                                self.w(e, "V", V_PROJ(j - 2))
                            p = PQ[j % 2]
                            for k in range(DC):
                                ins = e.matmul(
                                    p[:, 0:sz],
                                    wb_b[:, k, 512 + oc * 128:512 + (oc + 1) * 128],
                                    xn[:, k, off:off + sz],
                                    start=(k == 0), stop=(k == DC - 1))
                            self.inc(ins, "T")
                            j += 1
                    for ti in range(NTK):
                        if j >= 2:
                            self.w(e, "V", V_PROJ(j - 2))
                        p = PQ[j % 2]
                        for k in range(DC):
                            ins = e.matmul(p[:, 0:512],
                                           xn[:, k, ti * 128:(ti + 1) * 128],
                                           wb_b[:, k, 1024:1536],
                                           start=(k == 0), stop=(k == DC - 1))
                        self.inc(ins, "T")
                        j += 1
                blk.tensor(tproj)

                def vproj(e, grp=grp):
                    j = 0
                    for oc in range(4):
                        for (off, sz) in c.TT:
                            self.w(e, "T", T_PROJ(j))
                            p = PQ[j % 2]
                            self.inc(e.tensor_scalar(
                                q_sb[:, oc, off:off + sz], p[:, 0:sz],
                                cst_s[:, C_EIGHTH:C_EIGHTH + 1], None,
                                op0=ALU.mult), "V")
                            j += 1
                    for oc in range(4):
                        for (off, sz) in c.TTK:
                            self.w(e, "T", T_PROJ(j))
                            p = PQ[j % 2]
                            self.inc(e.tensor_copy(
                                k_sb[:, oc, off:off + sz], p[:, 0:sz]), "V")
                            j += 1
                    for ti in range(NTK):
                        self.w(e, "T", T_PROJ(j))
                        p = PQ[j % 2]
                        for hg in range(8):
                            ins = e.tensor_copy(v_tok[:, ti, hg, 0:64],
                                                p[:, hg * 64:(hg + 1) * 64])
                        self.inc(ins, "V")
                        j += 1
                blk.vector(vproj)

                ta0, aa0, va0 = self.cnt["T"], self.cnt["A"], self.cnt["V"]
                T_SC = lambda qt, h: ta0 + 16 * qt + 2 * h + 1
                T_AV = lambda qt, h: ta0 + 16 * qt + 2 * h + 2
                A_EXP = lambda qt, h: aa0 + 8 * qt + h + 1
                V_MSK = lambda qt, h: va0 + 24 * qt + 3 * h + 1
                V_RC = lambda qt, h: va0 + 24 * qt + 3 * h + 2
                V_EV = lambda qt, h: va0 + 24 * qt + 3 * h + 3

                def tatt(e, grp=grp):
                    for qt in range(NT):
                        for h in range(8):
                            z = h % 2
                            idx = qt * 8 + h
                            i = idx % 2
                            if idx >= 2:
                                pq, ph = divmod(idx - 2, 8)
                                self.w(e, "A", A_EXP(pq, ph))
                            for j, kc in enumerate((qt, qt + 1, qt + 2)):
                                ins = e.matmul(
                                    PS_[i][:, j * 128:(j + 1) * 128],
                                    k_sb[z * 64:z * 64 + 64, h // 2,
                                         kc * 128:(kc + 1) * 128],
                                    q_sb[z * 64:z * 64 + 64, h // 2,
                                         qt * 128:(qt + 1) * 128],
                                    start=True, stop=True)
                            self.inc(ins, "T")
                            self.w(e, "A", A_EXP(qt, h))
                            self.w(e, "V", V_MSK(qt, h))
                            if idx >= 2:
                                pq, ph = divmod(idx - 2, 8)
                                self.w(e, "V", V_EV(pq, ph))
                            for j, kc in enumerate((qt, qt + 1, qt + 2)):
                                ins = e.matmul(
                                    PY[i][:, 0:66],
                                    exp_sb[:, i, j * 128:(j + 1) * 128],
                                    v_tok[:, kc, h, 0:66],
                                    start=(j == 0), stop=(j == 2))
                            self.inc(ins, "T")
                blk.tensor(tatt)

                def aatt(e, grp=grp):
                    for qt in range(NT):
                        for h in range(8):
                            idx = qt * 8 + h
                            i = idx % 2
                            self.w(e, "T", T_SC(qt, h))
                            if idx >= 2:
                                pq, ph = divmod(idx - 2, 8)
                                self.w(e, "V", V_MSK(pq, ph))
                                self.w(e, "T", T_AV(pq, ph))
                            e.activation(er[:, i, 0, :], PS_[i][:, 0:128],
                                         AF.Exp)
                            e.activation(exp_sb[:, i, 128:256],
                                         PS_[i][:, 128:256], AF.Exp)
                            self.inc(e.activation(er[:, i, 1, :],
                                                  PS_[i][:, 256:384],
                                                  AF.Exp), "A")
                blk.scalar(aatt)

                def vatt(e, grp=grp):
                    for qt in range(NT):
                        for h in range(8):
                            idx = qt * 8 + h
                            i = idx % 2
                            self.w(e, "A", A_EXP(qt, h))
                            if idx >= 2:
                                pq, ph = divmod(idx - 2, 8)
                                self.w(e, "T", T_AV(pq, ph))
                            if qt == 0:
                                ins = e.tensor_scalar(
                                    exp_sb[:, i, 0:128], er[:, i, 0, :],
                                    pcst_s[:, 4:5], None, op0=ALU.mult)
                            else:
                                ins = e.tensor_copy(exp_sb[:, i, 0:128],
                                                    er[:, i, 0, :])
                            if qt == NT - 1:
                                ins = e.tensor_scalar(
                                    exp_sb[:, i, 256:384], er[:, i, 1, :],
                                    pcst_s[:, 5:6], None, op0=ALU.mult)
                            else:
                                ins = e.tensor_copy(exp_sb[:, i, 256:384],
                                                    er[:, i, 1, :])
                            self.inc(ins, "V")
                            self.w(e, "T", T_AV(qt, h))
                            self.inc(e.reciprocal(rc_sb[:, i, :],
                                                  PY[i][:, 64:65]), "V")
                            self.w(e, "V", V_RC(qt, h))
                            hh = grp * 8 + h
                            self.inc(e.scalar_tensor_tensor(
                                x[:, qt + 1, hh * 64:(hh + 1) * 64],
                                PY[i][:, 0:64], rc_sb[:, i, :],
                                x[:, qt + 1, hh * 64:(hh + 1) * 64],
                                op0=ALU.mult, op1=ALU.add), "V")
                blk.vector(vatt)
                self._b_t_end = self.cnt["T"]
                self._b_v_end = self.cnt["V"]




def _prep_inputs(cfg, inputs):
    c = cfg
    T, D, S = c.T, c.D, c.S
    f32 = np.float32
    x = np.asarray(inputs["L0_attn_a_x"], f32)
    wq = np.ascontiguousarray(np.asarray(inputs["L0_attn_a_Wq"], f32))
    bq = np.asarray(inputs["L0_attn_a_bq"], f32)
    wk = np.asarray(inputs["L0_attn_a_Wk"], f32)
    bk = np.asarray(inputs["L0_attn_a_bk"], f32)
    wv = np.ascontiguousarray(np.asarray(inputs["L0_attn_a_Wv"], f32))
    bv = np.asarray(inputs["L0_attn_a_bv"], f32)
    wo = np.ascontiguousarray(np.asarray(inputs["L0_attn_a_Wo"], f32))
    bo = np.asarray(inputs["L0_attn_a_bo"], f32)
    wqkv = np.ascontiguousarray(np.asarray(inputs["L0_attn_b_w_qkv"], f32))
    gate_w = np.asarray(inputs["L0_ffn_gate_W"], f32)
    wg = np.asarray(inputs["L0_ffn_W_gate"], f32)
    wu = np.asarray(inputs["L0_ffn_W_up"], f32)
    wd = np.asarray(inputs["L0_ffn_W_down"], f32)

    wk_dup = np.ascontiguousarray(
        np.concatenate([wk, wk[:, 64:128], wk[:, 0:64]], axis=1))
    bk_dup = np.stack([bk, np.concatenate([bk[64:128], bk[0:64]])], axis=1)
    bk_dup = np.ascontiguousarray(bk_dup)
    bq_t = np.ascontiguousarray(bq.reshape(c.DC, 128).T)
    gate_wt = np.ascontiguousarray(gate_w.T)
    wg_t = np.ascontiguousarray(wg.transpose(0, 2, 1))
    wu_t = np.ascontiguousarray(wu.transpose(0, 2, 1))
    wd_t = np.ascontiguousarray(wd.transpose(0, 2, 1))

    u = np.arange(128)[:, None]
    tq = np.arange(128)[None, :]
    tri = np.concatenate([(u >= tq), (u < tq)], axis=1).astype(BFH)
    tri4 = np.concatenate([np.tile(u >= tq, (1, 4)),
                           np.tile(u < tq, (1, 4))], axis=1).astype(BFH)
    ident_b = np.eye(128, dtype=BFH)
    ident_f = np.eye(128, dtype=f32)
    onesf = np.ones((1, 128), f32)
    cstv = np.zeros((128, NCST), f32)
    cstv[:, C_EPS] = 1e-5
    cstv[:, C_NEGHALF] = -0.5
    cstv[:, C_NEG20] = -1.0 / 0.05
    cstv[:, C_EIGHTH] = 64.0 ** -0.5
    cstv[:, C_ONE] = 1.0
    cstv[:, C_NEG1E30] = -1e30
    pp = np.arange(128)[:, None]
    esel = ((pp % 16) // 2 == np.arange(8)[None, :]).astype(f32)
    eselT = np.ascontiguousarray(esel.T)
    tsel = ((pp // 16) * 2 + (pp % 2) == np.arange(16)[None, :]).astype(f32)
    tselT = np.ascontiguousarray(tsel.T)

    shared = dict(wq=wq, wk_dup=wk_dup, wv=wv, wo=wo, wqkv=wqkv,
                  gate_wt=gate_wt, wg_t=wg_t, wu_t=wu_t, wd_t=wd_t,
                  bq_t=bq_t, bk_dupi=bk_dup, bo_row=bo[None, :].copy(),
                  bv_row=bv[None, :].copy(), tri=tri, tri4=tri4,
                  ident_b=ident_b, ones_r=np.ones((1, 128), f32),
                  ident_f=ident_f, onesf=onesf, cst=cstv, esel=esel,
                  eselT=eselT, tsel=tsel, tselT=tselT)

    in_maps = []
    npc = c.NCORE // c.B
    for core in range(c.NCORE):
        seq, ch = divmod(core, npc)
        t0 = ch * T
        xi = np.zeros((T + 256, D), f32)
        lo, hi = t0 - 128, t0 + T + 128
        slo, shi = max(0, lo), min(S, hi)
        xi[slo - lo: shi - lo] = x[seq, slo:shi]
        first = 1.0 if ch > 0 else 0.0
        last = 1.0 if ch < npc - 1 else 0.0
        pc = np.zeros((128, 6), f32)
        pc[:, 0] = first
        pc[:, 1] = 1.0 - first
        pc[:, 2] = last
        pc[:, 3] = 1.0 - last
        pc[:, 4] = first
        pc[:, 5] = last
        sh = np.zeros((128, 32), f32)
        if ch > 0:
            sh[:, 2 * (core - 1) + 1] = 1.0
        if ch < npc - 1:
            sh[:, 16 + 2 * (core + 1)] = 1.0
        m = dict(shared)
        m["x_in"] = xi
        m["pcst"] = pc
        m["sel_halo"] = sh
        in_maps.append(m)
    return in_maps


_NC_CACHE = {}


def _get_nc(cfg_key):
    if cfg_key not in _NC_CACHE:
        T, MF, n_sink = cfg_key
        _NC_CACHE[cfg_key] = MK(Cfg(T=T, MF=MF, n_sink=n_sink)).build()
    return _NC_CACHE[cfg_key]


def kernel(**inputs):
    cfg = Cfg()
    in_maps = _prep_inputs(cfg, inputs)
    nc = _get_nc((cfg.T, cfg.MF, cfg.n_sink))
    res = run_bass_kernel_spmd(nc, in_maps, core_ids=list(range(cfg.NCORE)))
    out = np.empty((cfg.B, cfg.S, cfg.D), np.float32)
    npc = cfg.NCORE // cfg.B
    for core in range(cfg.NCORE):
        seq, ch = divmod(core, npc)
        out[seq, ch * cfg.T:(ch + 1) * cfg.T] = res.results[core]["out"]
    return out



# version 08
# speedup vs baseline: 1.1239x; 1.0068x over previous
import numpy as np
from contextlib import ExitStack

import concourse.bass as bass
import concourse.mybir as mybir
from concourse.bass_utils import run_bass_kernel_spmd
import ml_dtypes

BFH = ml_dtypes.bfloat16

F32 = mybir.dt.float32
BF16 = mybir.dt.bfloat16
AF = mybir.ActivationFunctionType
ALU = mybir.AluOpType
AX = mybir.AxisListType

(C_EPS, C_NEGHALF, C_NEG20, C_EIGHTH, C_ONE, C_NEG1E30, C_ZERO,
 C_NEGONE) = range(8)
NCST = 8


class Cfg:
    def __init__(self, T=2048, MF=32, n_sink=20):
        self.NCORE = 8
        self.B = 2
        self.D = 1024
        self.DC = 8
        self.T = T
        self.NT = T // 128
        self.NTK = self.NT + 2
        self.S = (self.NCORE // self.B) * T
        self.HA = 16
        self.HB = 16
        self.E = 8
        self.MF = MF
        self.F = MF * 128
        self.GROUPS = [list(range(i, min(i + 8, MF)))
                       for i in range(0, MF, 8)]
        self.GMAX = max(len(g) for g in self.GROUPS)
        self.n_sink = n_sink
        self.TA = self.NCORE * T
        self.JW = self.TA // 16
        self.TT = self._tiles(T, 512)
        self.TTK = self._tiles(T + 256, 512)

    @staticmethod
    def _tiles(total, sz):
        out, off = [], 0
        while off < total:
            n = min(sz, total - off)
            out.append((off, n))
            off += n
        return out


class MK:
    def __init__(self, cfg):
        self.c = cfg
        self.nc = bass.Bass("TRN2", target_bir_lowering=False)
        self.cnt = {}
        self.sem = {}
        self.uid = 0

    def mksem(self, name):
        self.sem[name] = self.ctx.enter_context(self.nc.semaphore(name))
        self.cnt[name] = 0

    def inc(self, instr, name, n=1):
        instr.then_inc(self.sem[name], n)
        self.cnt[name] += n
        return self.cnt[name]

    def w(self, eng, name, val):
        if val > 0:
            eng.wait_ge(self.sem[name], val)

    def fence(self, eng, instr, sem="F"):
        n = self.inc(instr, sem)
        eng.wait_ge(self.sem[sem], n)

    def barrier(self):
        snap = dict(self.cnt)

        def body(e):
            for nm in ("T", "V", "A", "G", "Y", "C", "F", "F2",
                       "Wa", "Wd0", "Wd1", "Wg0", "Wg1", "Wc"):
                self.w(e, nm, snap[nm])
        for en in ("tensor", "vector", "scalar", "gpsimd", "sync"):
            getattr(self.block, en)(body)

    def sb(self, ctx, name, shape, dt):
        self.uid += 1
        return ctx.enter_context(
            self.nc.sbuf_tensor(f"{name}_{self.uid}", shape, dt))

    def ps(self, ctx, name, shape, dt):
        self.uid += 1
        return ctx.enter_context(
            self.nc.psum_tensor(f"{name}_{self.uid}", shape, dt))

    def build(self):
        c = self.c
        nc = self.nc
        D, DC, T, NT, NTK = c.D, c.DC, c.T, c.NT, c.NTK

        dtens = lambda n, s, d, k: nc.dram_tensor(n, s, d, kind=k)
        x_in = dtens("x_in", [T + 256, D], F32, "ExternalInput")
        wq = dtens("wq", [D, D], F32, "ExternalInput")
        wk_dup = dtens("wk_dup", [D, 256], F32, "ExternalInput")
        wv = dtens("wv", [D, 128], F32, "ExternalInput")
        wo = dtens("wo", [D, D], F32, "ExternalInput")
        wqkv = dtens("wqkv", [D, 3 * D], F32, "ExternalInput")
        gate_wt = dtens("gate_wt", [D, 8], F32, "ExternalInput")
        wg_t = dtens("wg_t", [c.E, D, c.F], F32, "ExternalInput")
        wu_t = dtens("wu_t", [c.E, D, c.F], F32, "ExternalInput")
        wd_t = dtens("wd_t", [c.E, c.F, D], F32, "ExternalInput")
        bq_t = dtens("bq_t", [128, DC], F32, "ExternalInput")
        bk_dupi = dtens("bk_dupi", [128, 2], F32, "ExternalInput")
        bo_row = dtens("bo_row", [1, D], F32, "ExternalInput")
        bv_row = dtens("bv_row", [1, 128], F32, "ExternalInput")
        tri = dtens("tri", [128, 256], BF16, "ExternalInput")
        tri4 = dtens("tri4", [128, 1024], BF16, "ExternalInput")
        ones_r = dtens("ones_r", [1, 128], mybir.dt.float32r,
                       "ExternalInput")
        ident_b = dtens("ident_b", [128, 128], BF16, "ExternalInput")
        ident_f = dtens("ident_f", [128, 128], F32, "ExternalInput")
        onesf = dtens("onesf", [1, 128], F32, "ExternalInput")
        cst = dtens("cst", [128, NCST], F32, "ExternalInput")
        pcst = dtens("pcst", [128, 6], F32, "ExternalInput")
        sel_halo = dtens("sel_halo", [128, 32], F32, "ExternalInput")
        esel = dtens("esel", [128, 8], F32, "ExternalInput")
        eselT = dtens("eselT", [8, 128], F32, "ExternalInput")
        tsel = dtens("tsel", [128, 16], F32, "ExternalInput")
        tselT = dtens("tselT", [16, 128], F32, "ExternalInput")
        out_t = dtens("out", [T, D], F32, "ExternalOutput")
        cc_la_in = nc.dram_tensor("cc_la_in", [8, T], F32)
        cc_la_out = nc.dram_tensor("cc_la_out", [8 * 8, T], F32)
        cc_h_in = nc.dram_tensor("cc_h_in", [256, D], F32)
        cc_h_out = nc.dram_tensor("cc_h_out", [8 * 256, D], F32)

        with ExitStack() as ctx:
            self.ctx = ctx
            for s in ("T", "V", "A", "G", "Y", "C", "F", "F2",
                      "Wa", "Wd0", "Wd1", "Wg0", "Wg1", "Wc"):
                self.mksem(s)
            sb, ps = self.sb, self.ps
            x = sb(ctx, "x", [128, NTK, D], F32)
            cst_s = sb(ctx, "cst_s", [128, NCST], F32)
            pcst_s = sb(ctx, "pcst_s", [128, 6], F32)
            selh_s = sb(ctx, "selh_s", [128, 32], F32)
            tri4_s = sb(ctx, "tri4_s", [128, 1024], BF16)
            idb_s = sb(ctx, "idb_s", [128, 128], BF16)
            idf_s = sb(ctx, "idf_s", [128, 128], F32)
            onesf_s = sb(ctx, "onesf_s", [1, 128], F32)
            bq_s = sb(ctx, "bq_s", [128, DC], F32)
            bkd_s = sb(ctx, "bkd_s", [128, 2], F32)
            esel_s = sb(ctx, "esel_s", [128, 8], F32)
            eselT_s = sb(ctx, "eselT_s", [8, 128], F32)
            tsel_s = sb(ctx, "tsel_s", [128, 16], F32)
            tselT_s = sb(ctx, "tselT_s", [16, 128], F32)
            gw_s = sb(ctx, "gw_s", [128, DC, 8], BF16)
            W_sel = sb(ctx, "W_sel", [128, NT, 8], F32)
            la1own = sb(ctx, "la1own", [128, NT, 8], F32)
            colb = sb(ctx, "colb", [128, c.n_sink, 8], F32)
            sk_t = sb(ctx, "sk_t", [128, NT, 8], F32)
            sk_m = sb(ctx, "sk_m", [128, NT, 2], F32)
            pi_s = sb(ctx, "pi_s", [128, NT, 8], F32)
            m1_s = sb(ctx, "m1_s", [128, NT, 8], F32)
            m2_s = sb(ctx, "m2_s", [128, NT, 8], F32)
            ln_st = sb(ctx, "ln_st", [128, 2, 12], F32)
            ln_mv = sb(ctx, "ln_mv", [128, 2, 2], F32)
            ln_r = sb(ctx, "ln_r", [128, 2, 2], F32)
            ln_tm = sb(ctx, "ln_tm", [128, 2, D], BF16)
            xn = sb(ctx, "xn", [128, DC, T + 256], BF16)

            self.t = dict(x=x, cst=cst_s, pcst=pcst_s, selh=selh_s,
                          tri4=tri4_s,
                          idb=idb_s, idf=idf_s, onesf=onesf_s, bq=bq_s,
                          bkd=bkd_s, esel=esel_s,
                          eselT=eselT_s, tsel=tsel_s, tselT=tselT_s, gw=gw_s,
                          bo_dram=bo_row, bv_dram=bv_row,
                          W_sel=W_sel, la1own=la1own, colb=colb, sk_t=sk_t,
                          sk_m=sk_m, pi=pi_s, m1=m1_s, m2=m2_s,
                          ln_st=ln_st, ln_mv=ln_mv,
                          ln_r=ln_r, ln_tm=ln_tm, xn=xn)
            self.dram = dict(wq=wq, wk_dup=wk_dup, wv=wv, wo=wo, wqkv=wqkv,
                             ones_r=ones_r,
                             gate_wt=gate_wt, wg_t=wg_t, wu_t=wu_t, wd_t=wd_t,
                             cc_la_in=cc_la_in, cc_la_out=cc_la_out,
                             cc_h_in=cc_h_in, cc_h_out=cc_h_out)

            self.block = ctx.enter_context(nc.Block())
            blk = self.block

            def p0_sync(e):
                for dst, src in ((cst_s, cst), (pcst_s, pcst), (selh_s, sel_halo),
                                 (tri4_s, tri4),
                                 (idb_s, ident_b), (idf_s, ident_f),
                                 (onesf_s, onesf), (bq_s, bq_t), (bkd_s, bk_dupi),
                                 (esel_s, esel), (eselT_s, eselT),
                                 (tsel_s, tsel), (tselT_s, tselT)):
                    self.inc(e.dma_start(dst[:], src[:]), "Y", 16)
                self.inc(e.dma_start(
                    x[:], x_in.ap().rearrange("(t p) d -> p t d", p=128)), "Y", 16)
            blk.sync(p0_sync)

            def p0_gps(e):
                self.inc(e.dma_start(gw_s[:], gate_wt.ap().rearrange(
                    "(kc p) e -> p kc e", p=128)), "Wa", 16)
            blk.gpsimd(p0_gps)
            self.barrier()

            self._ln(range(NTK), toff=0)
            self._attn_a()
            self.barrier()

            self._ln(range(1, NT + 1), toff=128)
            self._sinkhorn()
            self._ffn()
            self.barrier()

            self._halo()
            self.barrier()

            self._ln(range(NTK), toff=0)
            self._attn_b()
            self.barrier()

            self._ln(range(1, NT + 1), toff=128)
            self._sinkhorn()
            self._ffn()
            self.barrier()

            def pout(e):
                self.inc(e.dma_start(
                    out_t.ap().rearrange("(t p) d -> p t d", p=128),
                    x[:, 1:NT + 1, :]), "Y", 16)
                self.w(e, "Y", self.cnt["Y"])
            blk.sync(pout)

        return nc

    def _ln(self, tiles, toff):
        c = self.c
        t = self.t
        tiles = list(tiles)
        with ExitStack() as pctx:
            pt = [self.ps(pctx, "ln_pt0", [128, 1024], BF16),
                  self.ps(pctx, "ln_pt1", [128, 1024], BF16)]
            x, xn = t["x"], t["xn"]
            ln_st, ln_mv, ln_r, ln_tm = t["ln_st"], t["ln_mv"], t["ln_r"], t["ln_tm"]
            cst_s, idb_s = t["cst"], t["idb"]

            v0, a0, t0 = self.cnt["V"], self.cnt["A"], self.cnt["T"]
            V_STATS = lambda i: v0 + 4 * i + 1
            V_AGGR = lambda i: v0 + 4 * i + 2
            V_NORM = lambda i: v0 + 4 * i + 3
            V_COPY = lambda i: v0 + 4 * i + 4
            A_LN = lambda i: a0 + 2 * i + 1
            A_EXP = lambda i: a0 + 2 * i + 2
            T_TR = lambda i: t0 + i + 1

            def vbody(e):
                for i, ti in enumerate(tiles):
                    r = i % 2
                    if i >= 2:
                        self.w(e, "A", A_EXP(i - 2))
                    e.bn_stats(ln_st[:, r, 0:6], x[:, ti, 0:512])
                    self.inc(e.bn_stats(ln_st[:, r, 6:12], x[:, ti, 512:1024]), "V")
                    self.w(e, "V", V_STATS(i))
                    self.inc(e.bn_aggr(ln_mv[:, r, :], ln_st[:, r, :]), "V")
                    self.w(e, "A", A_EXP(i))
                    if i >= 2:
                        self.w(e, "T", T_TR(i - 2))
                    self.inc(e.tensor_scalar(
                        ln_tm[:, r, :], x[:, ti, :], ln_mv[:, r, 0:1],
                        ln_r[:, r, 1:2], op0=ALU.subtract, op1=ALU.mult), "V")
                    self.w(e, "T", T_TR(i))
                    for dc in range(c.DC):
                        ins = e.tensor_copy(
                            xn[:, dc, toff + i * 128: toff + (i + 1) * 128],
                            pt[r][:, dc * 128:(dc + 1) * 128])
                    self.inc(ins, "V")

            def abody(e):
                for i, ti in enumerate(tiles):
                    r = i % 2
                    self.w(e, "V", V_AGGR(i))
                    ins = e.activation(ln_r[:, r, 0:1], ln_mv[:, r, 1:2],
                                       AF.Ln, bias=cst_s[:, C_EPS:C_EPS + 1])
                    self.inc(ins, "A")
                    self.w(e, "A", A_LN(i))
                    self.inc(e.activation(
                        ln_r[:, r, 1:2], ln_r[:, r, 0:1], AF.Exp,
                        scale=cst_s[:, C_NEGHALF:C_NEGHALF + 1]), "A")

            def tbody(e):
                for i, ti in enumerate(tiles):
                    r = i % 2
                    self.w(e, "V", V_NORM(i))
                    if i >= 2:
                        self.w(e, "V", V_COPY(i - 2))
                    for dc in range(c.DC):
                        ins = e.transpose(pt[r][:, dc * 128:(dc + 1) * 128],
                                          ln_tm[:, r, dc * 128:(dc + 1) * 128],
                                          idb_s[:])
                    self.inc(ins, "T")

            self.block.vector(vbody)
            self.block.scalar(abody)
            self.block.tensor(tbody)

    def _attn_a(self):
        c = self.c
        t = self.t
        NT, DC, T, NTK = c.NT, c.DC, c.T, c.NTK
        blk = self.block
        x, xn = t["x"], t["xn"]
        with ExitStack() as pctx:
            sb, ps = self.sb, self.ps
            q_sb = sb(pctx, "q_sb", [128, 8, T], BF16)
            k_sb = sb(pctx, "k_sb", [128, 2, T + 256], BF16)
            v_tok = sb(pctx, "v_tok", [128, NTK, 2, 66], BF16)
            ao_fm = xn[:, :, 0:T]
            exp_sb = sb(pctx, "exp_sb", [128, 2, 3, 512], BF16)
            F32R = mybir.dt.float32r
            rc16 = sb(pctx, "rc16", [1, 2, 512], F32R)
            rc64 = sb(pctx, "rc64", [64, 512], F32)
            lnt = sb(pctx, "lnt", [64, 512], F32)
            ones64 = sb(pctx, "ones64", [1, 128], F32R)
            bo_s = sb(pctx, "bo_s", [1, c.D], F32)
            bv_s = sb(pctx, "bv_s", [1, 128], F32)
            ictx = ExitStack()
            wq_b = sb(ictx, "wq_b", [128, DC, c.D], BF16)
            wkd_b = sb(ictx, "wkd_b", [128, DC, 256], BF16)
            wv_b = sb(ictx, "wv_b", [128, DC, 128], BF16)
            pool = ps(pctx, "a_pool", [128, 8, 512], F32)
            PQ = [pool[:, 0, :], pool[:, 1, :]]

            cst_s, pcst_s = t["cst"], t["pcst"]
            idb_s, onesf_s = t["idb"], t["onesf"]
            bq_s, bkd_s = t["bq"], t["bkd"]
            dr = self.dram

            w0 = self.cnt["Wa"]

            def gbody(e):
                for dst, src in ((wq_b, dr["wq"]), (wkd_b, dr["wk_dup"]),
                                 (wv_b, dr["wv"])):
                    self.inc(e.dma_start(dst[:], src.ap().rearrange(
                        "(kc p) f -> p kc f", p=128)), "Wa", 16)
                self.inc(e.dma_start(bo_s[:], t["bo_dram"][:, :]), "Wa", 16)
                self.inc(e.dma_start(bv_s[:], t["bv_dram"][:, :]), "Wa", 16)
                self.inc(e.dma_start(ones64[:], self.dram["ones_r"][:, :]),
                         "Wa", 16)
                self.inc(e.memset(v_tok[:, :, :, 64:66], 1.0), "G")
            blk.gpsimd(gbody)
            W_QKV = w0 + 96
            G_ONES = self.cnt["G"]
            v_ln_done = self.cnt["V"]

            t0, v0 = self.cnt["T"], self.cnt["V"]
            T_PROJ = lambda j: t0 + j + 1
            V_PROJ = lambda j: v0 + j + 1
            nQ, nK, nV = DC * len(c.TT), 2 * len(c.TTK), NTK

            def tproj(e):
                self.w(e, "Wa", W_QKV)
                self.w(e, "V", v_ln_done)
                j = 0
                for oc in range(DC):
                    for (off, sz) in c.TT:
                        if j >= 2:
                            self.w(e, "V", V_PROJ(j - 2))
                        b = j % 2
                        for k in range(DC):
                            ins = e.matmul(pool[:, b, 0:sz],
                                           wq_b[:, k, oc * 128:(oc + 1) * 128],
                                           xn[:, k, 128 + off:128 + off + sz],
                                           start=(k == 0), stop=(k == DC - 1))
                        self.inc(ins, "T")
                        j += 1
                for oc2 in range(2):
                    for (off, sz) in c.TTK:
                        if j >= 2:
                            self.w(e, "V", V_PROJ(j - 2))
                        b = j % 2
                        for k in range(DC):
                            ins = e.matmul(pool[:, b, 0:sz],
                                           wkd_b[:, k, oc2 * 128:(oc2 + 1) * 128],
                                           xn[:, k, off:off + sz],
                                           start=(k == 0), stop=(k == DC - 1))
                        self.inc(ins, "T")
                        j += 1
                self.w(e, "G", G_ONES)
                for ti in range(NTK):
                    if j >= 2:
                        self.w(e, "V", V_PROJ(j - 2))
                    b = j % 2
                    for k in range(DC):
                        e.matmul(pool[:, b, 0:128],
                                 xn[:, k, ti * 128:(ti + 1) * 128],
                                 wv_b[:, k, 0:128], start=(k == 0), stop=False)
                    ins = e.matmul(pool[:, b, 0:128], onesf_s[:], bv_s[:],
                                   start=False, stop=True)
                    self.inc(ins, "T")
                    j += 1
            blk.tensor(tproj)

            def vproj(e):
                j = 0
                for oc in range(DC):
                    for (off, sz) in c.TT:
                        self.w(e, "T", T_PROJ(j))
                        b = j % 2
                        for z in range(2):
                            ins = e.tensor_scalar(
                                q_sb[z * 64:z * 64 + 64, oc, off:off + sz],
                                pool[z * 64:z * 64 + 64, b, 0:sz],
                                bq_s[z * 64:z * 64 + 64, oc:oc + 1], None,
                                op0=ALU.add)
                        self.inc(ins, "V")
                        j += 1
                for oc2 in range(2):
                    for (off, sz) in c.TTK:
                        self.w(e, "T", T_PROJ(j))
                        b = j % 2
                        self.inc(e.tensor_scalar(
                            k_sb[:, oc2, off:off + sz], pool[:, b, 0:sz],
                            bkd_s[:, oc2:oc2 + 1], None, op0=ALU.add), "V")
                        j += 1
                for ti in range(NTK):
                    self.w(e, "T", T_PROJ(j))
                    b = j % 2
                    for kv in range(2):
                        dst = v_tok[:, ti, kv, 0:64]
                        src = pool[:, b, kv * 64:kv * 64 + 64]
                        if ti == 0:
                            ins = e.tensor_scalar(dst, src, pcst_s[:, 0:1],
                                                  None, op0=ALU.mult)
                        elif ti == NTK - 1:
                            ins = e.tensor_scalar(dst, src, pcst_s[:, 2:3],
                                                  None, op0=ALU.mult)
                        else:
                            ins = e.tensor_copy(dst, src)
                    self.inc(ins, "V")
                    j += 1
            blk.vector(vproj)
            t_proj_end = self.cnt["T"]
            ictx.close()
            wo_b = sb(pctx, "wo_b", [128, DC, c.D], BF16)
            wo_w0 = self.cnt["Wa"]

            def gbody2(e):
                self.w(e, "T", t_proj_end)
                self.inc(e.dma_start(wo_b[:], dr["wo"].ap().rearrange(
                    "(kc p) f -> p kc f", p=128)), "Wa", 16)
            blk.gpsimd(gbody2)
            W_O = wo_w0 + 16

            v_proj_end = self.cnt["V"]
            U = NT * 4
            T_SC, T_AV, T_BC = {}, {}, {}
            A_EXP, A_RC = {}, {}
            V_MSK, V_RC, V_AO = {}, {}, {}
            cT, cA, cV = self.cnt["T"], self.cnt["A"], self.cnt["V"]
            cT += 1
            T_SC[0] = cT
            for u in range(U):
                if u + 1 < U:
                    cT += 1
                    T_SC[u + 1] = cT
                cT += 1
                T_AV[u] = cT
                cT += 1
                T_BC[u] = cT
            for u in range(U):
                cA += 1
                A_EXP[u] = cA
                if u >= 1:
                    cA += 1
                    A_RC[u - 1] = cA
            cA += 1
            A_RC[U - 1] = cA
            for u in range(U):
                cV += 1
                V_MSK[u] = cV
                cV += 1
                V_RC[u] = cV
                if u >= 1:
                    cV += 1
                    V_AO[u - 1] = cV
            cV += 1
            V_AO[U - 1] = cV
            tri4_s = t["tri4"]

            def tatt(e):
                def sc(u):
                    qt, iu = divmod(u, 4)
                    z, kv = iu % 2, iu // 2
                    i = u % 2
                    if u >= 2:
                        self.w(e, "A", A_EXP[u - 2])
                        self.w(e, "A", A_RC[u - 2])
                    for j, kc in enumerate((qt, qt + 1, qt + 2)):
                        ins = e.matmul(
                            pool[:, 2 + 3 * i + j, :],
                            k_sb[z * 64:z * 64 + 64, kv ^ z,
                                 kc * 128:(kc + 1) * 128],
                            q_sb[z * 64:z * 64 + 64, kv * 4:kv * 4 + 4,
                                 qt * 128:(qt + 1) * 128],
                            start=True, stop=True)
                    self.inc(ins, "T")
                sc(0)
                for u in range(U):
                    qt, iu = divmod(u, 4)
                    z, kv = iu % 2, iu // 2
                    i = u % 2
                    if u + 1 < U:
                        sc(u + 1)
                    self.w(e, "V", V_MSK[u])
                    self.w(e, "V", V_AO[u - 2] if u >= 2 else v_proj_end)
                    for j, kc in enumerate((qt, qt + 1, qt + 2)):
                        ins = e.matmul(pool[0:66, i, :],
                                       v_tok[:, kc, kv, 0:66],
                                       exp_sb[:, i, j, :],
                                       start=(j == 0), stop=(j == 2))
                    self.inc(ins, "T")
                    self.w(e, "V", V_RC[u])
                    ins = e.matmul(pool[:, 2 + 3 * i, :], ones64[:],
                                   rc16[:, i, :], start=True, stop=True)
                    self.inc(ins, "T")
            blk.tensor(tatt)

            def aatt(e):
                def rc_copy(u):
                    i = u % 2
                    self.w(e, "T", T_BC[u])
                    ins = e.activation(lnt[:], pool[0:64, 2 + 3 * i, :],
                                       AF.Ln)
                    self.fence(e, ins, "F2")
                    if u >= 1:
                        self.w(e, "V", V_AO[u - 1])
                    self.inc(e.activation(
                        rc64[:], lnt[:], AF.Exp,
                        scale=cst_s[0:64, C_NEGONE:C_NEGONE + 1]), "A")
                for u in range(U):
                    i = u % 2
                    self.w(e, "T", T_SC[u])
                    if u >= 2:
                        self.w(e, "T", T_AV[u - 2])
                    for j in range(3):
                        ins = e.activation(exp_sb[:, i, j, :],
                                           pool[:, 2 + 3 * i + j, :], AF.Exp)
                    self.inc(ins, "A")
                    if u >= 1:
                        rc_copy(u - 1)
                rc_copy(U - 1)
            blk.scalar(aatt)

            def vatt(e):
                def ao(u):
                    qt, iu = divmod(u, 4)
                    z, kv = iu % 2, iu // 2
                    i = u % 2
                    self.w(e, "A", A_RC[u])
                    ins = e.tensor_tensor(
                        xn[z * 64:z * 64 + 64, kv * 4:kv * 4 + 4,
                           qt * 128:(qt + 1) * 128],
                        pool[0:64, i, :].rearrange("p (a b) -> p a b", b=128),
                        rc64[:].rearrange("p (a b) -> p a b", b=128),
                        op=ALU.mult)
                    self.inc(ins, "V")
                for u in range(U):
                    qt, iu = divmod(u, 4)
                    i = u % 2
                    self.w(e, "A", A_EXP[u])
                    if qt == 0:
                        ins = e.tensor_scalar(exp_sb[:, i, 0, :],
                                              exp_sb[:, i, 0, :],
                                              pcst_s[:, 0:1], pcst_s[:, 1:2],
                                              op0=ALU.mult, op1=ALU.add)
                        self.fence(e, ins)
                    e.tensor_tensor(exp_sb[:, i, 0, :], exp_sb[:, i, 0, :],
                                    tri4_s[:, 0:512], op=ALU.mult)
                    if qt == NT - 1:
                        ins = e.tensor_scalar(exp_sb[:, i, 2, :],
                                              exp_sb[:, i, 2, :],
                                              pcst_s[:, 2:3], pcst_s[:, 3:4],
                                              op0=ALU.mult, op1=ALU.add)
                        self.fence(e, ins)
                    ins = e.tensor_tensor(exp_sb[:, i, 2, :],
                                          exp_sb[:, i, 2, :],
                                          tri4_s[:, 512:1024], op=ALU.mult)
                    self.inc(ins, "V")
                    self.w(e, "T", T_AV[u])
                    if u >= 2:
                        self.w(e, "T", T_BC[u - 2])
                    with self.nc.allow_low_precision(
                            reason="f32r denom broadcast conduit"):
                        ins = e.tensor_copy(rc16[:, i, :], pool[64:65, i, :])
                    self.inc(ins, "V")
                    if u >= 1:
                        ao(u - 1)
                ao(U - 1)
            blk.vector(vatt)
            v_att_end = self.cnt["V"]
            a_att_end = self.cnt["A"]

            to0, vo0 = self.cnt["T"], self.cnt["V"]
            T_OP = lambda j: to0 + j + 1
            V_OP = lambda j: vo0 + j + 1

            def topr(e):
                self.w(e, "Wa", W_O)
                self.w(e, "V", v_att_end)
                self.w(e, "A", a_att_end)
                j = 0
                for qt in range(NT):
                    for dh in range(2):
                        if j >= 2:
                            self.w(e, "V", V_OP(j - 2))
                        b = j % 2
                        for k in range(DC):
                            e.matmul(pool[:, b, 0:512],
                                     ao_fm[:, k, qt * 128:(qt + 1) * 128],
                                     wo_b[:, k, dh * 512:(dh + 1) * 512],
                                     start=(k == 0), stop=False)
                        ins = e.matmul(pool[:, b, 0:512], onesf_s[:],
                                       bo_s[0:1, dh * 512:(dh + 1) * 512],
                                       start=False, stop=True)
                        self.inc(ins, "T")
                        j += 1
            blk.tensor(topr)

            def vopr(e):
                j = 0
                for qt in range(NT):
                    for dh in range(2):
                        self.w(e, "T", T_OP(j))
                        b = j % 2
                        self.inc(e.scalar_tensor_tensor(
                            x[:, qt + 1, dh * 512:(dh + 1) * 512],
                            pool[:, b, 0:512],
                            cst_s[:, C_ONE:C_ONE + 1],
                            x[:, qt + 1, dh * 512:(dh + 1) * 512],
                            op0=ALU.mult, op1=ALU.add), "V")
                        j += 1
            blk.vector(vopr)

    def _sinkhorn(self):
        c = self.c
        t = self.t
        NT, NS, DC = c.NT, c.n_sink, c.DC
        blk = self.block
        xn, gw_s = t["xn"], t["gw"]
        la1own, colb, W_sel = t["la1own"], t["colb"], t["W_sel"]
        sk_t, sk_m, pi_s, m1_s, m2_s = t["sk_t"], t["sk_m"], t["pi"], t["m1"], t["m2"]
        cst_s, idf_s, onesf_s = t["cst"], t["idf"], t["onesf"]
        esel_s, eselT_s, tsel_s, tselT_s = t["esel"], t["eselT"], t["tsel"], t["tselT"]
        dr = self.dram
        jw = c.JW
        njt = (jw + 511) // 512

        with ExitStack() as pctx:
            la_em = self.sb(pctx, "la_em", [8, c.T], F32)
            la_full = self.sb(pctx, "la_full", [128, c.JW], F32)
            exp_f = self.sb(pctx, "exp_f", [128, c.JW], F32)
            psums = self.sb(pctx, "psums", [128, 1], F32)
            rls = self.sb(pctx, "rls", [16, c.JW], F32)
            clsb = self.sb(pctx, "clsb", [8, 1], F32)
            clrow = self.sb(pctx, "clrow", [1, c.n_sink * 8], F32)
            P1 = self.ps(pctx, "sk_p1", [128, 512], F32)
            P2 = self.ps(pctx, "sk_p2", [128, 512], F32)
            jw_pad = max(512, c.JW)
            P3 = self.ps(pctx, "sk_p3", [16, jw_pad], F32)
            P4 = self.ps(pctx, "sk_p4", [128, jw_pad], F32)

            t0, v0, a0 = self.cnt["T"], self.cnt["V"], self.cnt["A"]
            T_LG = lambda i: t0 + 2 * i + 1
            T_TR = lambda i: t0 + 2 * i + 2
            V_RAW = lambda i: v0 + 6 * i + 1
            V_RMX = lambda i: v0 + 6 * i + 2
            V_TT1 = lambda i: v0 + 6 * i + 3
            V_RSM = lambda i: v0 + 6 * i + 4
            V_LA1 = lambda i: v0 + 6 * i + 5
            V_CEM = lambda i: v0 + 6 * i + 6
            A_EXP = lambda i: a0 + 2 * i + 1
            A_LSE = lambda i: a0 + 2 * i + 2

            def tlg(e):
                self.w(e, "V", v0)
                for i in range(NT):
                    if i >= 2:
                        self.w(e, "V", V_RAW(i - 2))
                    p = P1 if i % 2 == 0 else P2
                    for k in range(DC):
                        ins = e.matmul(p[:, 0:8],
                                       xn[:, k, 128 + i * 128:128 + (i + 1) * 128],
                                       gw_s[:, k, :], start=(k == 0),
                                       stop=(k == DC - 1))
                    self.inc(ins, "T")
                    self.w(e, "V", V_LA1(i))
                    if i >= 2:
                        self.w(e, "V", V_CEM(i - 2))
                    ins = e.transpose(P4[0:8, (i % 2) * 128:(i % 2) * 128 + 128],
                                      la1own[:, i, :], idf_s[:])
                    self.inc(ins, "T")
            blk.tensor(tlg)

            def alg(e):
                for i in range(NT):
                    self.w(e, "V", V_TT1(i))
                    self.inc(e.activation(pi_s[:, i, :], la1own[:, i, :],
                                          AF.Exp), "A")
                    self.w(e, "V", V_RSM(i))
                    self.inc(e.activation(sk_m[:, i, 0:1], sk_m[:, i, 1:2],
                                          AF.Ln), "A")
            blk.scalar(alg)

            def vlg(e):
                for i in range(NT):
                    self.w(e, "T", T_LG(i))
                    p = P1 if i % 2 == 0 else P2
                    self.inc(e.tensor_scalar(sk_t[:, i, :], p[:, 0:8],
                                             cst_s[:, C_NEG20:C_NEG20 + 1],
                                             None, op0=ALU.mult), "V")
                    self.w(e, "V", V_RAW(i))
                    self.inc(e.tensor_reduce(sk_m[:, i, 0:1], sk_t[:, i, :],
                                             axis=AX.X, op=ALU.max), "V")
                    self.w(e, "V", V_RMX(i))
                    self.inc(e.tensor_tensor(
                        la1own[:, i, :], sk_t[:, i, :],
                        sk_m[:, i, 0:1].broadcast_to([128, 8]),
                        op=ALU.subtract), "V")
                    self.w(e, "A", A_EXP(i))
                    self.inc(e.tensor_reduce(sk_m[:, i, 1:2], pi_s[:, i, :],
                                             axis=AX.X, op=ALU.add), "V")
                    self.w(e, "A", A_LSE(i))
                    self.inc(e.tensor_tensor(
                        la1own[:, i, :], la1own[:, i, :],
                        sk_m[:, i, 0:1].broadcast_to([128, 8]),
                        op=ALU.subtract), "V")
                    self.w(e, "T", T_TR(i))
                    self.inc(e.tensor_copy(la_em[:, i * 128:(i + 1) * 128],
                                           P4[0:8, (i % 2) * 128:(i % 2) * 128 + 128]),
                             "V")
            blk.vector(vlg)
            v_em_done = self.cnt["V"]

            def gcc(e):
                self.w(e, "V", v_em_done)
                self.inc(e.dma_start(dr["cc_la_in"][:, :], la_em[:]), "Wc", 16)
                self.w(e, "Wc", self.cnt["Wc"])
                self.inc(e.collective_compute(
                    "AllGather", ALU.bypass, replica_groups=[list(range(8))],
                    ins=[dr["cc_la_in"].ap().opt()],
                    outs=[dr["cc_la_out"].ap().opt()]), "C")
                self.w(e, "C", self.cnt["C"])
                self.inc(e.dma_start(
                    la_full[:],
                    dr["cc_la_out"].ap().rearrange(
                        "(r e) (h j) -> (r e h) j", e=8, h=2)), "Wc", 16)
            blk.gpsimd(gcc)
            w_lafull = self.cnt["Wc"]

            seq = [("col", 0)]
            for it in range(1, NS):
                seq += [("row", it), ("col", it)]
            cA, cV, cT = self.cnt["A"], self.cnt["V"], self.cnt["T"]
            marks = []
            for kind, it in seq:
                m = {"kind": kind, "it": it}
                if kind == "col":
                    cA += 1; m["exp"] = cA
                    cV += 1; m["red"] = cV
                    cT += 1; m["mm1"] = cT
                    cA += 1; m["ln"] = cA
                    cT += 1; m["mm2"] = cT
                    cV += 1; m["sub"] = cV
                    cV += 1; m["crow"] = cV
                else:
                    cA += 1; m["exp"] = cA
                    cT += 1; m["mm1"] = cT
                    cA += 1; m["ln"] = cA
                    cT += 1; m["mm2"] = cT
                    cV += 1; m["sub"] = cV
                marks.append(m)

            def tb(e):
                prev_crow = 0
                self.w(e, "Wc", w_lafull)
                for idx, m in enumerate(marks):
                    if m["kind"] == "col":
                        self.w(e, "V", m["red"])
                        if prev_crow:
                            self.w(e, "V", prev_crow)
                        ins = e.matmul(P1[0:8, 0:1], esel_s[:], psums[:],
                                       start=True, stop=True)
                        ins.then_inc(self.sem["T"], 1)
                        self.w(e, "A", m["ln"])
                        e.matmul(P2[0:128, 0:1], eselT_s[:], clsb[:],
                                 start=True, stop=True)
                        ins = e.transpose(P1[0:1, 8:16], clsb[:],
                                          idf_s[0:8, 0:8])
                        ins.then_inc(self.sem["T"], 1)
                        prev_crow = m["crow"]
                    else:
                        self.w(e, "A", m["exp"])
                        for jt in range(njt):
                            o = jt * 512
                            sz = min(512, jw - o)
                            ins = e.matmul(P3[:, o:o + sz], tsel_s[:],
                                           exp_f[:, o:o + sz],
                                           start=True, stop=True)
                        ins.then_inc(self.sem["T"], 1)
                        self.w(e, "A", m["ln"])
                        for jt in range(njt):
                            o = jt * 512
                            sz = min(512, jw - o)
                            ins = e.matmul(P4[:, o:o + sz], tselT_s[:],
                                           rls[:, o:o + sz],
                                           start=True, stop=True)
                        ins.then_inc(self.sem["T"], 1)

            def ab(e):
                for idx, m in enumerate(marks):
                    if idx == 0:
                        self.w(e, "Wc", w_lafull)
                    else:
                        self.w(e, "V", marks[idx - 1]["sub"])
                    ins = e.activation(exp_f[:], la_full[:], AF.Exp)
                    ins.then_inc(self.sem["A"], 1)
                    self.w(e, "T", m["mm1"])
                    if m["kind"] == "col":
                        ins = e.activation(clsb[:], P1[0:8, 0:1], AF.Ln)
                    else:
                        ins = e.activation(rls[:, 0:jw], P3[:, 0:jw], AF.Ln)
                    ins.then_inc(self.sem["A"], 1)

            def vb(e):
                for idx, m in enumerate(marks):
                    if m["kind"] == "col":
                        self.w(e, "A", m["exp"])
                        ins = e.tensor_reduce(psums[:], exp_f[:], axis=AX.X,
                                              op=ALU.add)
                        ins.then_inc(self.sem["V"], 1)
                        self.w(e, "T", m["mm2"])
                        ins = e.tensor_scalar(la_full[:], la_full[:],
                                              P2[0:128, 0:1], None,
                                              op0=ALU.subtract)
                        ins.then_inc(self.sem["V"], 1)
                        ins = e.tensor_copy(
                            clrow[0:1, m["it"] * 8:(m["it"] + 1) * 8],
                            P1[0:1, 8:16])
                        ins.then_inc(self.sem["V"], 1)
                    else:
                        self.w(e, "T", m["mm2"])
                        ins = e.tensor_tensor(la_full[:], la_full[:],
                                              P4[:, 0:jw], op=ALU.subtract)
                        ins.then_inc(self.sem["V"], 1)

            blk.tensor(tb)
            blk.scalar(ab)
            blk.vector(vb)
            self.cnt["T"], self.cnt["V"], self.cnt["A"] = cT, cV, cA
            v_phA = self.cnt["V"]

            def tcolb(e):
                self.w(e, "V", v_phA)
                for it in range(NS):
                    ins = e.matmul(P4[:, it * 8:(it + 1) * 8], onesf_s[:],
                                   clrow[0:1, it * 8:(it + 1) * 8],
                                   start=True, stop=True)
                self.inc(ins, "T")
            blk.tensor(tcolb)
            t_colb = self.cnt["T"]

            def vcolb(e):
                self.w(e, "T", t_colb)
                self.inc(e.tensor_copy(
                    colb[:].rearrange("p a b -> p (a b)"),
                    P4[:, 0:NS * 8]), "V")
            blk.vector(vcolb)
            self.sink_psum_free = (self.cnt["V"], self.cnt["A"], self.cnt["T"])

            a0b = self.cnt["A"]
            A_BEXP = lambda it: a0b + 2 * it + 1
            A_BLN = lambda it: a0b + 2 * it + 2
            A_PI = a0b + 2 * (NS - 1) + 1
            f_tt1 = {}
            f_rsum = {}
            f_colsub_last = [0]
            v_colb_done = self.cnt["V"]

            def vphB(e):
                self.w(e, "V", v_colb_done)
                for it in range(NS):
                    cb = colb[:, it:it + 1, :].broadcast_to([128, NT, 8])
                    ins = e.tensor_tensor(la1own[:], la1own[:], cb,
                                          op=ALU.subtract)
                    self.fence(e, ins)
                    if it == NS - 1:
                        f_colsub_last[0] = self.cnt["F"]
                    if it < NS - 1:
                        self.fence(e, e.tensor_reduce(
                            sk_m[:, :, 0:1], la1own[:], axis=AX.X, op=ALU.max))
                        ins = e.tensor_tensor(
                            sk_t[:], la1own[:],
                            sk_m[:, :, 0:1].broadcast_to([128, NT, 8]),
                            op=ALU.subtract)
                        self.fence(e, ins)
                        f_tt1[it] = self.cnt["F"]
                        self.w(e, "A", A_BEXP(it))
                        self.fence(e, e.tensor_reduce(
                            sk_m[:, :, 1:2], pi_s[:], axis=AX.X, op=ALU.add))
                        f_rsum[it] = self.cnt["F"]
                        self.w(e, "A", A_BLN(it))
                        self.fence(e, e.tensor_tensor(
                            la1own[:], sk_t[:],
                            sk_m[:, :, 0:1].broadcast_to([128, NT, 8]),
                            op=ALU.subtract))
                self.w(e, "A", A_PI)
                self.fence(e, e.tensor_reduce(sk_m[:, :, 0:1], pi_s[:],
                                              axis=AX.X, op=ALU.max))
                self.fence(e, e.tensor_tensor(
                    m1_s[:], pi_s[:],
                    sk_m[:, :, 0:1].broadcast_to([128, NT, 8]),
                    op=ALU.is_equal))
                self.fence(e, e.scalar_tensor_tensor(
                    sk_t[:], m1_s[:], cst_s[:, C_NEG1E30:C_NEG1E30 + 1],
                    pi_s[:], op0=ALU.mult, op1=ALU.add))
                self.fence(e, e.tensor_reduce(sk_m[:, :, 1:2], sk_t[:],
                                              axis=AX.X, op=ALU.max))
                self.fence(e, e.tensor_tensor(
                    m2_s[:], sk_t[:],
                    sk_m[:, :, 1:2].broadcast_to([128, NT, 8]),
                    op=ALU.is_equal))
                self.fence(e, e.tensor_tensor(
                    sk_t[:], m1_s[:],
                    pi_s[:, :, 0:1].broadcast_to([128, NT, 8]), op=ALU.mult))
                self.fence(e, e.tensor_tensor(
                    m1_s[:], m2_s[:],
                    pi_s[:, :, 1:2].broadcast_to([128, NT, 8]), op=ALU.mult))
                self.inc(e.tensor_tensor(W_sel[:], sk_t[:], m1_s[:],
                                         op=ALU.add), "V")
            blk.vector(vphB)
            self.v_wsel = self.cnt["V"]

            la_fl = la1own[:].rearrange("p a b -> p (a b)")
            skt_fl = sk_t[:].rearrange("p a b -> p (a b)")
            pi_fl = pi_s[:].rearrange("p a b -> p (a b)")

            def aphB(e):
                for it in range(NS - 1):
                    self.w(e, "F", f_tt1[it])
                    self.inc(e.activation(pi_fl, skt_fl, AF.Exp), "A")
                    self.w(e, "F", f_rsum[it])
                    self.inc(e.activation(sk_m[:, :, 0:1], sk_m[:, :, 1:2],
                                          AF.Ln), "A")
                self.w(e, "F", f_colsub_last[0])
                self.inc(e.activation(pi_fl, la_fl, AF.Exp), "A")
            blk.scalar(aphB)

    def _ffn(self):
        c = self.c
        t = self.t
        NT, DC, T = c.NT, c.DC, c.T
        blk = self.block
        x, xn, W_sel = t["x"], t["xn"], t["W_sel"]
        cst_s = t["cst"]
        dr = self.dram
        ntt = len(c.TT)

        with ExitStack() as pctx:
            h_b = self.sb(pctx, "h_b", [128, c.GMAX, T], BF16)
            sg_b = self.sb(pctx, "sg_b", [128, 2, 512], BF16)
            wgu_b = self.sb(pctx, "wgu_b", [128, 2, 2, DC, 256], BF16)
            wd_b = self.sb(pctx, "wd_b", [128, 2, c.GMAX, c.D], BF16)
            PG = [self.ps(pctx, "f_pg0", [128, 512], F32),
                  self.ps(pctx, "f_pg1", [128, 512], F32)]
            PU = [self.ps(pctx, "f_pu0", [128, 512], F32),
                  self.ps(pctx, "f_pu1", [128, 512], F32)]
            PYF = [self.ps(pctx, "f_py0", [128, 512], F32),
                   self.ps(pctx, "f_py1", [128, 512], F32)]

            t0, a0, v0 = self.cnt["T"], self.cnt["A"], self.cnt["V"]
            units = []
            for e_i in range(c.E):
                for gi, G in enumerate(c.GROUPS):
                    for mi in range(len(G)):
                        for tti in range(ntt):
                            units.append(("p1", e_i, gi, mi, tti))
                    for tt2 in range(NT):
                        for dh in range(2):
                            units.append(("p2", e_i, gi, tt2, dh))
            T_U = {}
            A_U = {}
            V_U = {}
            cT, cA, cV = t0, a0, v0
            p1list, p2list = [], []
            for u in units:
                cT += 1
                T_U[u] = cT
                if u[0] == "p1":
                    cA += 1
                    A_U[u] = cA
                    p1list.append(u)
                cV += 1
                V_U[u] = cV
                if u[0] == "p2":
                    p2list.append(u)
            pair_of = {}
            PAIR_B = {}
            PAIR_SEQ = []
            gp = 0
            for e_i in range(c.E):
                for gi, G in enumerate(c.GROUPS):
                    nm = len(G)
                    for pi2 in range((nm + 1) // 2):
                        pair_of[(e_i, gi, pi2)] = list(range(2 * pi2,
                                                             min(2 * pi2 + 2, nm)))
                        PAIR_B[(e_i, gi, pi2)] = gp % 2
                        PAIR_SEQ.append((e_i, gi, pi2))
                        gp += 1
            T_PAIR_DONE = {}
            for (e_i, gi, pi2), mis in pair_of.items():
                T_PAIR_DONE[(e_i, gi, pi2)] = T_U[("p1", e_i, gi, mis[-1],
                                                   ntt - 1)]
            T_P2_DONE = {}
            V_P2_DONE = {}
            for e_i in range(c.E):
                for gi in range(len(c.GROUPS)):
                    T_P2_DONE[(e_i, gi)] = T_U[("p2", e_i, gi, NT - 1, 1)]
                    V_P2_DONE[(e_i, gi)] = V_U[("p2", e_i, gi, NT - 1, 1)]
            V_P1_DONE = {}
            for e_i in range(c.E):
                for gi, G in enumerate(c.GROUPS):
                    V_P1_DONE[(e_i, gi)] = V_U[("p1", e_i, gi, len(G) - 1,
                                                ntt - 1)]

            W_WD = {}
            W_PAIR = {}
            cWd = {0: self.cnt["Wd0"], 1: self.cnt["Wd1"]}
            cWg = {0: self.cnt["Wg0"], 1: self.cnt["Wg1"]}
            negl = len(c.GROUPS)
            for e_i in range(c.E):
                for gi, G in enumerate(c.GROUPS):
                    eg = e_i * negl + gi
                    bd = eg % 2
                    cWd[bd] += 16
                    W_WD[(e_i, gi)] = (bd, cWd[bd])
                    for pi2 in range((len(G) + 1) // 2):
                        b = PAIR_B[(e_i, gi, pi2)]
                        cWg[b] += 32
                        W_PAIR[(e_i, gi, pi2)] = (b, cWg[b])

            def prev_eg(e_i, gi):
                k = e_i * len(c.GROUPS) + gi - 1
                if k < 0:
                    return None
                return divmod(k, len(c.GROUPS))

            v_psum_free, a_psum_free, t_psum_free = self.sink_psum_free

            def gstream(e):
                for e_i in range(c.E):
                    wg_e = dr["wg_t"].ap()[e_i].rearrange("(kc p) f -> p kc f", p=128)
                    wu_e = dr["wu_t"].ap()[e_i].rearrange("(kc p) f -> p kc f", p=128)
                    wd_e = dr["wd_t"].ap()[e_i].rearrange("(mc p) d -> p mc d", p=128)
                    for gi, G in enumerate(c.GROUPS):
                        eg = e_i * len(c.GROUPS) + gi
                        if eg >= 2:
                            pv2 = divmod(eg - 2, len(c.GROUPS))
                            self.w(e, "T", T_P2_DONE[pv2])
                        self.inc(e.dma_start(
                            wd_b[:, eg % 2, 0:len(G), :],
                            wd_e[:, G[0]:G[0] + len(G), :],
                            max_dma_last_dim=8192), f"Wd{eg % 2}", 16)
                        npair = (len(G) + 1) // 2
                        for pi2 in range(npair):
                            mis = pair_of[(e_i, gi, pi2)]
                            mw = len(mis) * 128
                            f0 = (G[0] + mis[0]) * 128
                            gidx = PAIR_SEQ.index((e_i, gi, pi2))
                            if gidx >= 2:
                                self.w(e, "T", T_PAIR_DONE[PAIR_SEQ[gidx - 2]])
                            b = PAIR_B[(e_i, gi, pi2)]
                            self.inc(e.dma_start(
                                wgu_b[:, b, 0, :, 0:mw],
                                wg_e[:, :, f0:f0 + mw],
                                max_dma_last_dim=8192), f"Wg{b}", 16)
                            self.inc(e.dma_start(
                                wgu_b[:, b, 1, :, 0:mw],
                                wu_e[:, :, f0:f0 + mw],
                                max_dma_last_dim=8192), f"Wg{b}", 16)
            blk.gpsimd(gstream)


            def tffn(e):
                self.w(e, "V", v_psum_free)
                self.w(e, "A", a_psum_free)
                p1i = 0
                p2i = 0
                for e_i in range(c.E):
                    for gi, G in enumerate(c.GROUPS):
                        for mi in range(len(G)):
                            pi2, z = divmod(mi, 2)
                            b = PAIR_B[(e_i, gi, pi2)]
                            for tti, (off, sz) in enumerate(c.TT):
                                u = ("p1", e_i, gi, mi, tti)
                                if tti == 0 and z == 0:
                                    wb_, wv_ = W_PAIR[(e_i, gi, pi2)]
                                    self.w(e, f"Wg{wb_}", wv_)
                                ii = p1i % 2
                                if p1i >= 2:
                                    pu = p1list[p1i - 2]
                                    self.w(e, "A", A_U[pu])
                                    self.w(e, "V", V_U[pu])
                                for k in range(DC):
                                    e.matmul(PG[ii][:, 0:sz],
                                             wgu_b[:, b, 0, k,
                                                   z * 128:(z + 1) * 128],
                                             xn[:, k, 128 + off:128 + off + sz],
                                             start=(k == 0), stop=(k == DC - 1))
                                for k in range(DC):
                                    ins = e.matmul(
                                        PU[ii][:, 0:sz],
                                        wgu_b[:, b, 1, k, z * 128:(z + 1) * 128],
                                        xn[:, k, 128 + off:128 + off + sz],
                                        start=(k == 0), stop=(k == DC - 1))
                                self.inc(ins, "T")
                                assert self.cnt["T"] == T_U[u]
                                p1i += 1
                        eg = e_i * len(c.GROUPS) + gi
                        wdb_, wdv_ = W_WD[(e_i, gi)]
                        self.w(e, f"Wd{wdb_}", wdv_)
                        for tt2 in range(NT):
                            for dh in range(2):
                                u = ("p2", e_i, gi, tt2, dh)
                                jj = p2i % 2
                                if p2i >= 2:
                                    self.w(e, "V", V_U[p2list[p2i - 2]])
                                if tt2 == 0 and dh == 0:
                                    self.w(e, "V", V_P1_DONE[(e_i, gi)])
                                for mi in range(len(G)):
                                    ins = e.matmul(
                                        PYF[jj][:, 0:512],
                                        h_b[:, mi, tt2 * 128:(tt2 + 1) * 128],
                                        wd_b[:, eg % 2, mi,
                                             dh * 512:(dh + 1) * 512],
                                        start=(mi == 0),
                                        stop=(mi == len(G) - 1))
                                self.inc(ins, "T")
                                assert self.cnt["T"] == T_U[u]
                                p2i += 1
            blk.tensor(tffn)

            def affn(e):
                p1i = 0
                for e_i in range(c.E):
                    for gi, G in enumerate(c.GROUPS):
                        for mi in range(len(G)):
                            for tti, (off, sz) in enumerate(c.TT):
                                u = ("p1", e_i, gi, mi, tti)
                                ii = p1i % 2
                                self.w(e, "T", T_U[u])
                                if p1i >= 2:
                                    self.w(e, "V", V_U[p1list[p1i - 2]])
                                self.inc(e.activation(sg_b[:, ii, 0:sz],
                                                      PG[ii][:, 0:sz],
                                                      AF.Silu), "A")
                                assert self.cnt["A"] == A_U[u]
                                p1i += 1
            blk.scalar(affn)

            wsel_fence = self.v_wsel
            prev_evac = {}

            def vffn(e):
                self.w(e, "V", wsel_fence)
                self.w(e, "A", a_psum_free)
                self.w(e, "T", t_psum_free)
                p1i = 0
                p2i = 0
                for e_i in range(c.E):
                    for gi, G in enumerate(c.GROUPS):
                        for mi in range(len(G)):
                            for tti, (off, sz) in enumerate(c.TT):
                                u = ("p1", e_i, gi, mi, tti)
                                ii = p1i % 2
                                self.w(e, "A", A_U[u])
                                if mi == 0 and tti == 0:
                                    pv = prev_eg(e_i, gi)
                                    if pv is not None:
                                        self.w(e, "T", T_P2_DONE[pv])
                                self.inc(e.tensor_tensor(
                                    h_b[:, mi, off:off + sz],
                                    sg_b[:, ii, 0:sz], PU[ii][:, 0:sz],
                                    op=ALU.mult), "V")
                                assert self.cnt["V"] == V_U[u]
                                p1i += 1
                        for tt2 in range(NT):
                            for dh in range(2):
                                u = ("p2", e_i, gi, tt2, dh)
                                jj = p2i % 2
                                self.w(e, "T", T_U[u])
                                key = (tt2, dh)
                                if key in prev_evac:
                                    self.w(e, "V", prev_evac[key])
                                self.inc(e.scalar_tensor_tensor(
                                    x[:, tt2 + 1, dh * 512:(dh + 1) * 512],
                                    PYF[jj][:, 0:512],
                                    W_sel[:, tt2, e_i:e_i + 1],
                                    x[:, tt2 + 1, dh * 512:(dh + 1) * 512],
                                    op0=ALU.mult, op1=ALU.add), "V")
                                prev_evac[key] = self.cnt["V"]
                                assert self.cnt["V"] == V_U[u]
                                p2i += 1
            blk.vector(vffn)

    def _halo(self):
        c = self.c
        t = self.t
        blk = self.block
        x, selh_s = t["x"], t["selh"]
        dr = self.dram
        NT = c.NT

        with ExitStack() as pctx:
            G_sb = self.sb(pctx, "G_sb", [128, 16, c.D], F32)

            y0 = self.cnt["Y"]

            def ysync(e):
                self.inc(e.dma_start(dr["cc_h_in"][0:128, :], x[:, 1, :]),
                         "Y", 16)
                self.inc(e.dma_start(dr["cc_h_in"][128:256, :], x[:, NT, :]),
                         "Y", 16)
            blk.sync(ysync)

            def gcc(e):
                self.w(e, "Y", y0 + 32)
                self.inc(e.collective_compute(
                    "AllGather", ALU.bypass, replica_groups=[list(range(8))],
                    ins=[dr["cc_h_in"].ap().opt()],
                    outs=[dr["cc_h_out"].ap().opt()]), "C")
                self.w(e, "C", self.cnt["C"])
                self.inc(e.dma_start(G_sb[:], dr["cc_h_out"].ap().rearrange(
                    "(t p) d -> p t d", p=128)), "Wc", 16)
            blk.gpsimd(gcc)
            w_g = self.cnt["Wc"]

            def vsel(e):
                self.w(e, "Wc", w_g)
                for side, dst_ti, col0 in ((0, 0, 0), (1, c.NTK - 1, 16)):
                    ins = e.tensor_scalar(x[:, dst_ti, :], G_sb[:, 0, :],
                                          selh_s[:, col0:col0 + 1], None,
                                          op0=ALU.mult)
                    self.fence(e, ins)
                    for ti in range(1, 16):
                        ins = e.scalar_tensor_tensor(
                            x[:, dst_ti, :], G_sb[:, ti, :],
                            selh_s[:, col0 + ti:col0 + ti + 1],
                            x[:, dst_ti, :], op0=ALU.mult, op1=ALU.add)
                        self.fence(e, ins)
            blk.vector(vsel)

    def _attn_b(self):
        c = self.c
        t = self.t
        NT, DC, T, NTK, HB = c.NT, c.DC, c.T, c.NTK, c.HB
        blk = self.block
        x, xn = t["x"], t["xn"]
        cst_s, pcst_s, idb_s = t["cst"], t["pcst"], t["idb"]
        dr = self.dram

        with ExitStack() as pctx:
            sb, ps = self.sb, self.ps
            wb_b = sb(pctx, "wb_b", [128, DC, 1536], BF16)
            q_sb = sb(pctx, "q_sb_b", [128, 4, T], BF16)
            k_sb = sb(pctx, "k_sb_b", [128, 4, T + 256], BF16)
            v_tok = sb(pctx, "v_tok_b", [128, NTK, 8, 66], BF16)
            er = sb(pctx, "er_b", [128, 2, 2, 128], BF16)
            exp_sb = sb(pctx, "exp_sb_b", [128, 2, 384], BF16)
            rc_sb = sb(pctx, "rc_sb_b", [128, 2, 1], F32)
            PQ = [ps(pctx, "b_pq0", [128, 512], F32),
                  ps(pctx, "b_pq1", [128, 512], F32)]
            PS_ = [ps(pctx, "b_ps0", [128, 512], F32),
                   ps(pctx, "b_ps1", [128, 512], F32)]
            PY = [ps(pctx, "b_py0", [128, 512], F32),
                  ps(pctx, "b_py1", [128, 512], F32)]

            v_ln_done = self.cnt["V"]
            for grp in range(2):
                w0 = self.cnt["Wa"]

                def gw(e, grp=grp):
                    if grp > 0:
                        self.w(e, "T", self._b_t_end)
                        self.w(e, "V", self._b_v_end)
                    for part in range(3):
                        src = dr["wqkv"].ap().rearrange(
                            "(kc p) f -> p kc f", p=128)[
                            :, :, part * c.D + grp * 512:
                            part * c.D + grp * 512 + 512]
                        self.inc(e.dma_start(wb_b[:, :, part * 512:
                                                   part * 512 + 512], src),
                                 "Wa", 16)
                    self.inc(e.memset(v_tok[:, :, :, 64:66], 1.0), "G")
                blk.gpsimd(gw)
                W_G = w0 + 48
                G_ONES = self.cnt["G"]

                t0, v0 = self.cnt["T"], self.cnt["V"]
                T_PROJ = lambda j: t0 + j + 1
                V_PROJ = lambda j: v0 + j + 1

                def tproj(e, grp=grp):
                    self.w(e, "Wa", W_G)
                    self.w(e, "V", v_ln_done)
                    self.w(e, "G", G_ONES)
                    j = 0
                    for oc in range(4):
                        for (off, sz) in c.TT:
                            if j >= 2:
                                self.w(e, "V", V_PROJ(j - 2))
                            p = PQ[j % 2]
                            for k in range(DC):
                                ins = e.matmul(
                                    p[:, 0:sz],
                                    wb_b[:, k, oc * 128:(oc + 1) * 128],
                                    xn[:, k, 128 + off:128 + off + sz],
                                    start=(k == 0), stop=(k == DC - 1))
                            self.inc(ins, "T")
                            j += 1
                    for oc in range(4):
                        for (off, sz) in c.TTK:
                            if j >= 2:
                                self.w(e, "V", V_PROJ(j - 2))
                            p = PQ[j % 2]
                            for k in range(DC):
                                ins = e.matmul(
                                    p[:, 0:sz],
                                    wb_b[:, k, 512 + oc * 128:512 + (oc + 1) * 128],
                                    xn[:, k, off:off + sz],
                                    start=(k == 0), stop=(k == DC - 1))
                            self.inc(ins, "T")
                            j += 1
                    for ti in range(NTK):
                        if j >= 2:
                            self.w(e, "V", V_PROJ(j - 2))
                        p = PQ[j % 2]
                        for k in range(DC):
                            ins = e.matmul(p[:, 0:512],
                                           xn[:, k, ti * 128:(ti + 1) * 128],
                                           wb_b[:, k, 1024:1536],
                                           start=(k == 0), stop=(k == DC - 1))
                        self.inc(ins, "T")
                        j += 1
                blk.tensor(tproj)

                def vproj(e, grp=grp):
                    j = 0
                    for oc in range(4):
                        for (off, sz) in c.TT:
                            self.w(e, "T", T_PROJ(j))
                            p = PQ[j % 2]
                            self.inc(e.tensor_scalar(
                                q_sb[:, oc, off:off + sz], p[:, 0:sz],
                                cst_s[:, C_EIGHTH:C_EIGHTH + 1], None,
                                op0=ALU.mult), "V")
                            j += 1
                    for oc in range(4):
                        for (off, sz) in c.TTK:
                            self.w(e, "T", T_PROJ(j))
                            p = PQ[j % 2]
                            self.inc(e.tensor_copy(
                                k_sb[:, oc, off:off + sz], p[:, 0:sz]), "V")
                            j += 1
                    for ti in range(NTK):
                        self.w(e, "T", T_PROJ(j))
                        p = PQ[j % 2]
                        for hg in range(8):
                            ins = e.tensor_copy(v_tok[:, ti, hg, 0:64],
                                                p[:, hg * 64:(hg + 1) * 64])
                        self.inc(ins, "V")
                        j += 1
                blk.vector(vproj)

                ta0, aa0, va0 = self.cnt["T"], self.cnt["A"], self.cnt["V"]
                T_SC = lambda qt, h: ta0 + 16 * qt + 2 * h + 1
                T_AV = lambda qt, h: ta0 + 16 * qt + 2 * h + 2
                A_EXP = lambda qt, h: aa0 + 8 * qt + h + 1
                V_MSK = lambda qt, h: va0 + 24 * qt + 3 * h + 1
                V_RC = lambda qt, h: va0 + 24 * qt + 3 * h + 2
                V_EV = lambda qt, h: va0 + 24 * qt + 3 * h + 3

                def tatt(e, grp=grp):
                    for qt in range(NT):
                        for h in range(8):
                            z = h % 2
                            idx = qt * 8 + h
                            i = idx % 2
                            if idx >= 2:
                                pq, ph = divmod(idx - 2, 8)
                                self.w(e, "A", A_EXP(pq, ph))
                            for j, kc in enumerate((qt, qt + 1, qt + 2)):
                                ins = e.matmul(
                                    PS_[i][:, j * 128:(j + 1) * 128],
                                    k_sb[z * 64:z * 64 + 64, h // 2,
                                         kc * 128:(kc + 1) * 128],
                                    q_sb[z * 64:z * 64 + 64, h // 2,
                                         qt * 128:(qt + 1) * 128],
                                    start=True, stop=True)
                            self.inc(ins, "T")
                            self.w(e, "A", A_EXP(qt, h))
                            self.w(e, "V", V_MSK(qt, h))
                            if idx >= 2:
                                pq, ph = divmod(idx - 2, 8)
                                self.w(e, "V", V_EV(pq, ph))
                            for j, kc in enumerate((qt, qt + 1, qt + 2)):
                                ins = e.matmul(
                                    PY[i][:, 0:66],
                                    exp_sb[:, i, j * 128:(j + 1) * 128],
                                    v_tok[:, kc, h, 0:66],
                                    start=(j == 0), stop=(j == 2))
                            self.inc(ins, "T")
                blk.tensor(tatt)

                def aatt(e, grp=grp):
                    for qt in range(NT):
                        for h in range(8):
                            idx = qt * 8 + h
                            i = idx % 2
                            self.w(e, "T", T_SC(qt, h))
                            if idx >= 2:
                                pq, ph = divmod(idx - 2, 8)
                                self.w(e, "V", V_MSK(pq, ph))
                                self.w(e, "T", T_AV(pq, ph))
                            e.activation(er[:, i, 0, :], PS_[i][:, 0:128],
                                         AF.Exp)
                            e.activation(exp_sb[:, i, 128:256],
                                         PS_[i][:, 128:256], AF.Exp)
                            self.inc(e.activation(er[:, i, 1, :],
                                                  PS_[i][:, 256:384],
                                                  AF.Exp), "A")
                blk.scalar(aatt)

                def vatt(e, grp=grp):
                    for qt in range(NT):
                        for h in range(8):
                            idx = qt * 8 + h
                            i = idx % 2
                            self.w(e, "A", A_EXP(qt, h))
                            if idx >= 2:
                                pq, ph = divmod(idx - 2, 8)
                                self.w(e, "T", T_AV(pq, ph))
                            if qt == 0:
                                ins = e.tensor_scalar(
                                    exp_sb[:, i, 0:128], er[:, i, 0, :],
                                    pcst_s[:, 4:5], None, op0=ALU.mult)
                            else:
                                ins = e.tensor_copy(exp_sb[:, i, 0:128],
                                                    er[:, i, 0, :])
                            if qt == NT - 1:
                                ins = e.tensor_scalar(
                                    exp_sb[:, i, 256:384], er[:, i, 1, :],
                                    pcst_s[:, 5:6], None, op0=ALU.mult)
                            else:
                                ins = e.tensor_copy(exp_sb[:, i, 256:384],
                                                    er[:, i, 1, :])
                            self.inc(ins, "V")
                            self.w(e, "T", T_AV(qt, h))
                            self.inc(e.reciprocal(rc_sb[:, i, :],
                                                  PY[i][:, 64:65]), "V")
                            self.w(e, "V", V_RC(qt, h))
                            hh = grp * 8 + h
                            self.inc(e.scalar_tensor_tensor(
                                x[:, qt + 1, hh * 64:(hh + 1) * 64],
                                PY[i][:, 0:64], rc_sb[:, i, :],
                                x[:, qt + 1, hh * 64:(hh + 1) * 64],
                                op0=ALU.mult, op1=ALU.add), "V")
                blk.vector(vatt)
                self._b_t_end = self.cnt["T"]
                self._b_v_end = self.cnt["V"]




def _prep_inputs(cfg, inputs):
    c = cfg
    T, D, S = c.T, c.D, c.S
    f32 = np.float32
    x = np.asarray(inputs["L0_attn_a_x"], f32)
    wq = np.ascontiguousarray(np.asarray(inputs["L0_attn_a_Wq"], f32))
    bq = np.asarray(inputs["L0_attn_a_bq"], f32)
    wk = np.asarray(inputs["L0_attn_a_Wk"], f32)
    bk = np.asarray(inputs["L0_attn_a_bk"], f32)
    wv = np.ascontiguousarray(np.asarray(inputs["L0_attn_a_Wv"], f32))
    bv = np.asarray(inputs["L0_attn_a_bv"], f32)
    wo = np.ascontiguousarray(np.asarray(inputs["L0_attn_a_Wo"], f32))
    bo = np.asarray(inputs["L0_attn_a_bo"], f32)
    wqkv = np.ascontiguousarray(np.asarray(inputs["L0_attn_b_w_qkv"], f32))
    gate_w = np.asarray(inputs["L0_ffn_gate_W"], f32)
    wg = np.asarray(inputs["L0_ffn_W_gate"], f32)
    wu = np.asarray(inputs["L0_ffn_W_up"], f32)
    wd = np.asarray(inputs["L0_ffn_W_down"], f32)

    wk_dup = np.ascontiguousarray(
        np.concatenate([wk, wk[:, 64:128], wk[:, 0:64]], axis=1))
    bk_dup = np.stack([bk, np.concatenate([bk[64:128], bk[0:64]])], axis=1)
    bk_dup = np.ascontiguousarray(bk_dup)
    bq_t = np.ascontiguousarray(bq.reshape(c.DC, 128).T)
    gate_wt = np.ascontiguousarray(gate_w.T)
    wg_t = np.ascontiguousarray(wg.transpose(0, 2, 1))
    wu_t = np.ascontiguousarray(wu.transpose(0, 2, 1))
    wd_t = np.ascontiguousarray(wd.transpose(0, 2, 1))

    u = np.arange(128)[:, None]
    tq = np.arange(128)[None, :]
    tri = np.concatenate([(u >= tq), (u < tq)], axis=1).astype(BFH)
    tri4 = np.concatenate([np.tile(u >= tq, (1, 4)),
                           np.tile(u < tq, (1, 4))], axis=1).astype(BFH)
    ident_b = np.eye(128, dtype=BFH)
    ident_f = np.eye(128, dtype=f32)
    onesf = np.ones((1, 128), f32)
    cstv = np.zeros((128, NCST), f32)
    cstv[:, C_EPS] = 1e-5
    cstv[:, C_NEGHALF] = -0.5
    cstv[:, C_NEG20] = -1.0 / 0.05
    cstv[:, C_EIGHTH] = 64.0 ** -0.5
    cstv[:, C_ONE] = 1.0
    cstv[:, C_NEG1E30] = -1e30
    cstv[:, C_NEGONE] = -1.0
    pp = np.arange(128)[:, None]
    esel = ((pp % 16) // 2 == np.arange(8)[None, :]).astype(f32)
    eselT = np.ascontiguousarray(esel.T)
    tsel = ((pp // 16) * 2 + (pp % 2) == np.arange(16)[None, :]).astype(f32)
    tselT = np.ascontiguousarray(tsel.T)

    shared = dict(wq=wq, wk_dup=wk_dup, wv=wv, wo=wo, wqkv=wqkv,
                  gate_wt=gate_wt, wg_t=wg_t, wu_t=wu_t, wd_t=wd_t,
                  bq_t=bq_t, bk_dupi=bk_dup, bo_row=bo[None, :].copy(),
                  bv_row=bv[None, :].copy(), tri=tri, tri4=tri4,
                  ident_b=ident_b, ones_r=np.ones((1, 128), f32),
                  ident_f=ident_f, onesf=onesf, cst=cstv, esel=esel,
                  eselT=eselT, tsel=tsel, tselT=tselT)

    in_maps = []
    npc = c.NCORE // c.B
    for core in range(c.NCORE):
        seq, ch = divmod(core, npc)
        t0 = ch * T
        xi = np.zeros((T + 256, D), f32)
        lo, hi = t0 - 128, t0 + T + 128
        slo, shi = max(0, lo), min(S, hi)
        xi[slo - lo: shi - lo] = x[seq, slo:shi]
        first = 1.0 if ch > 0 else 0.0
        last = 1.0 if ch < npc - 1 else 0.0
        pc = np.zeros((128, 6), f32)
        pc[:, 0] = first
        pc[:, 1] = 1.0 - first
        pc[:, 2] = last
        pc[:, 3] = 1.0 - last
        pc[:, 4] = first
        pc[:, 5] = last
        sh = np.zeros((128, 32), f32)
        if ch > 0:
            sh[:, 2 * (core - 1) + 1] = 1.0
        if ch < npc - 1:
            sh[:, 16 + 2 * (core + 1)] = 1.0
        m = dict(shared)
        m["x_in"] = xi
        m["pcst"] = pc
        m["sel_halo"] = sh
        in_maps.append(m)
    return in_maps


_NC_CACHE = {}


def _get_nc(cfg_key):
    if cfg_key not in _NC_CACHE:
        T, MF, n_sink = cfg_key
        _NC_CACHE[cfg_key] = MK(Cfg(T=T, MF=MF, n_sink=n_sink)).build()
    return _NC_CACHE[cfg_key]


def kernel(**inputs):
    cfg = Cfg()
    in_maps = _prep_inputs(cfg, inputs)
    nc = _get_nc((cfg.T, cfg.MF, cfg.n_sink))
    res = run_bass_kernel_spmd(nc, in_maps, core_ids=list(range(cfg.NCORE)))
    out = np.empty((cfg.B, cfg.S, cfg.D), np.float32)
    npc = cfg.NCORE // cfg.B
    for core in range(cfg.NCORE):
        seq, ch = divmod(core, npc)
        out[seq, ch * cfg.T:(ch + 1) * cfg.T] = res.results[core]["out"]
    return out



# version 09
# speedup vs baseline: 1.1370x; 1.0116x over previous
import numpy as np
from contextlib import ExitStack

import concourse.bass as bass
import concourse.mybir as mybir
from concourse.bass_utils import run_bass_kernel_spmd
import ml_dtypes

BFH = ml_dtypes.bfloat16

F32 = mybir.dt.float32
BF16 = mybir.dt.bfloat16
AF = mybir.ActivationFunctionType
ALU = mybir.AluOpType
AX = mybir.AxisListType

(C_EPS, C_NEGHALF, C_NEG20, C_EIGHTH, C_ONE, C_NEG1E30, C_ZERO,
 C_NEGONE) = range(8)
NCST = 8


class Cfg:
    def __init__(self, T=2048, MF=32, n_sink=20):
        self.NCORE = 8
        self.B = 2
        self.D = 1024
        self.DC = 8
        self.T = T
        self.NT = T // 128
        self.NTK = self.NT + 2
        self.S = (self.NCORE // self.B) * T
        self.HA = 16
        self.HB = 16
        self.E = 8
        self.MF = MF
        self.F = MF * 128
        self.GROUPS = [list(range(i, min(i + 8, MF)))
                       for i in range(0, MF, 8)]
        self.GMAX = max(len(g) for g in self.GROUPS)
        self.n_sink = n_sink
        self.TA = self.NCORE * T
        self.JW = self.TA // 16
        self.TT = self._tiles(T, 512)
        self.TTK = self._tiles(T + 256, 512)

    @staticmethod
    def _tiles(total, sz):
        out, off = [], 0
        while off < total:
            n = min(sz, total - off)
            out.append((off, n))
            off += n
        return out


class MK:
    def __init__(self, cfg):
        self.c = cfg
        self.nc = bass.Bass("TRN2", target_bir_lowering=False)
        self.cnt = {}
        self.sem = {}
        self.uid = 0

    def mksem(self, name):
        self.sem[name] = self.ctx.enter_context(self.nc.semaphore(name))
        self.cnt[name] = 0

    def inc(self, instr, name, n=1):
        instr.then_inc(self.sem[name], n)
        self.cnt[name] += n
        return self.cnt[name]

    def w(self, eng, name, val):
        if val > 0:
            eng.wait_ge(self.sem[name], val)

    def fence(self, eng, instr, sem="F"):
        n = self.inc(instr, sem)
        eng.wait_ge(self.sem[sem], n)

    def barrier(self):
        snap = dict(self.cnt)

        def body(e):
            for nm in ("T", "V", "A", "G", "Y", "C", "F", "F2",
                       "Wa", "Wd0", "Wd1", "Wg0", "Wg1", "Wc"):
                self.w(e, nm, snap[nm])
        for en in ("tensor", "vector", "scalar", "gpsimd", "sync"):
            getattr(self.block, en)(body)

    def sb(self, ctx, name, shape, dt):
        self.uid += 1
        return ctx.enter_context(
            self.nc.sbuf_tensor(f"{name}_{self.uid}", shape, dt))

    def ps(self, ctx, name, shape, dt):
        self.uid += 1
        return ctx.enter_context(
            self.nc.psum_tensor(f"{name}_{self.uid}", shape, dt))

    def build(self):
        c = self.c
        nc = self.nc
        D, DC, T, NT, NTK = c.D, c.DC, c.T, c.NT, c.NTK

        dtens = lambda n, s, d, k: nc.dram_tensor(n, s, d, kind=k)
        x_in = dtens("x_in", [T + 256, D], F32, "ExternalInput")
        wq = dtens("wq", [D, D], F32, "ExternalInput")
        wk_dup = dtens("wk_dup", [D, 256], F32, "ExternalInput")
        wv = dtens("wv", [D, 128], F32, "ExternalInput")
        wo = dtens("wo", [D, D], F32, "ExternalInput")
        wqkv = dtens("wqkv", [D, 3 * D], F32, "ExternalInput")
        gate_wt = dtens("gate_wt", [D, 8], F32, "ExternalInput")
        wg_t = dtens("wg_t", [c.E, D, c.F], F32, "ExternalInput")
        wu_t = dtens("wu_t", [c.E, D, c.F], F32, "ExternalInput")
        wd_t = dtens("wd_t", [c.E, c.F, D], F32, "ExternalInput")
        bq_t = dtens("bq_t", [128, DC], F32, "ExternalInput")
        bk_dupi = dtens("bk_dupi", [128, 2], F32, "ExternalInput")
        bo_row = dtens("bo_row", [1, D], F32, "ExternalInput")
        bv_row = dtens("bv_row", [1, 128], F32, "ExternalInput")
        tri = dtens("tri", [128, 256], BF16, "ExternalInput")
        tri4 = dtens("tri4", [128, 1024], BF16, "ExternalInput")
        ones_r = dtens("ones_r", [1, 128], mybir.dt.float32r,
                       "ExternalInput")
        ident_b = dtens("ident_b", [128, 128], BF16, "ExternalInput")
        ident_f = dtens("ident_f", [128, 128], F32, "ExternalInput")
        onesf = dtens("onesf", [1, 128], F32, "ExternalInput")
        cst = dtens("cst", [128, NCST], F32, "ExternalInput")
        pcst = dtens("pcst", [128, 6], F32, "ExternalInput")
        sel_halo = dtens("sel_halo", [128, 32], F32, "ExternalInput")
        esel = dtens("esel", [128, 8], F32, "ExternalInput")
        eselT = dtens("eselT", [8, 128], F32, "ExternalInput")
        tsel = dtens("tsel", [128, 16], F32, "ExternalInput")
        tselT = dtens("tselT", [16, 128], F32, "ExternalInput")
        out_t = dtens("out", [T, D], F32, "ExternalOutput")
        cc_la_in = nc.dram_tensor("cc_la_in", [8, T], F32)
        cc_la_out = nc.dram_tensor("cc_la_out", [8 * 8, T], F32)
        cc_h_in = nc.dram_tensor("cc_h_in", [256, D], F32)
        cc_h_out = nc.dram_tensor("cc_h_out", [8 * 256, D], F32)

        with ExitStack() as ctx:
            self.ctx = ctx
            for s in ("T", "V", "A", "G", "Y", "C", "F", "F2",
                      "Wa", "Wd0", "Wd1", "Wg0", "Wg1", "Wc"):
                self.mksem(s)
            sb, ps = self.sb, self.ps
            x = sb(ctx, "x", [128, NTK, D], F32)
            cst_s = sb(ctx, "cst_s", [128, NCST], F32)
            pcst_s = sb(ctx, "pcst_s", [128, 6], F32)
            selh_s = sb(ctx, "selh_s", [128, 32], F32)
            tri4_s = sb(ctx, "tri4_s", [128, 1024], BF16)
            idb_s = sb(ctx, "idb_s", [128, 128], BF16)
            idf_s = sb(ctx, "idf_s", [128, 128], F32)
            onesf_s = sb(ctx, "onesf_s", [1, 128], F32)
            bq_s = sb(ctx, "bq_s", [128, DC], F32)
            bkd_s = sb(ctx, "bkd_s", [128, 2], F32)
            esel_s = sb(ctx, "esel_s", [128, 8], F32)
            eselT_s = sb(ctx, "eselT_s", [8, 128], F32)
            tsel_s = sb(ctx, "tsel_s", [128, 16], F32)
            tselT_s = sb(ctx, "tselT_s", [16, 128], F32)
            gw_s = sb(ctx, "gw_s", [128, DC, 8], BF16)
            W_sel = sb(ctx, "W_sel", [128, NT, 8], F32)
            la1own = sb(ctx, "la1own", [128, NT, 8], F32)
            colb = sb(ctx, "colb", [128, c.n_sink, 8], F32)
            sk_t = sb(ctx, "sk_t", [128, NT, 8], F32)
            sk_m = sb(ctx, "sk_m", [128, NT, 2], F32)
            pi_s = sb(ctx, "pi_s", [128, NT, 8], F32)
            m1_s = sb(ctx, "m1_s", [128, NT, 8], F32)
            m2_s = sb(ctx, "m2_s", [128, NT, 8], F32)
            ln_st = sb(ctx, "ln_st", [128, 2, 12], F32)
            ln_mv = sb(ctx, "ln_mv", [128, 2, 2], F32)
            ln_r = sb(ctx, "ln_r", [128, 2, 2], F32)
            ln_tm = sb(ctx, "ln_tm", [128, 2, D], BF16)
            xn = sb(ctx, "xn", [128, DC, T + 256], BF16)

            self.t = dict(x=x, cst=cst_s, pcst=pcst_s, selh=selh_s,
                          tri4=tri4_s,
                          idb=idb_s, idf=idf_s, onesf=onesf_s, bq=bq_s,
                          bkd=bkd_s, esel=esel_s,
                          eselT=eselT_s, tsel=tsel_s, tselT=tselT_s, gw=gw_s,
                          bo_dram=bo_row, bv_dram=bv_row,
                          W_sel=W_sel, la1own=la1own, colb=colb, sk_t=sk_t,
                          sk_m=sk_m, pi=pi_s, m1=m1_s, m2=m2_s,
                          ln_st=ln_st, ln_mv=ln_mv,
                          ln_r=ln_r, ln_tm=ln_tm, xn=xn)
            self.dram = dict(wq=wq, wk_dup=wk_dup, wv=wv, wo=wo, wqkv=wqkv,
                             ones_r=ones_r,
                             gate_wt=gate_wt, wg_t=wg_t, wu_t=wu_t, wd_t=wd_t,
                             cc_la_in=cc_la_in, cc_la_out=cc_la_out,
                             cc_h_in=cc_h_in, cc_h_out=cc_h_out)

            self.block = ctx.enter_context(nc.Block())
            blk = self.block

            def p0_sync(e):
                for dst, src in ((cst_s, cst), (pcst_s, pcst), (selh_s, sel_halo),
                                 (tri4_s, tri4),
                                 (idb_s, ident_b), (idf_s, ident_f),
                                 (onesf_s, onesf), (bq_s, bq_t), (bkd_s, bk_dupi),
                                 (esel_s, esel), (eselT_s, eselT),
                                 (tsel_s, tsel), (tselT_s, tselT)):
                    self.inc(e.dma_start(dst[:], src[:]), "Y", 16)
                self.inc(e.dma_start(
                    x[:], x_in.ap().rearrange("(t p) d -> p t d", p=128)), "Y", 16)
            blk.sync(p0_sync)

            def p0_gps(e):
                self.inc(e.dma_start(gw_s[:], gate_wt.ap().rearrange(
                    "(kc p) e -> p kc e", p=128)), "Wa", 16)
            blk.gpsimd(p0_gps)
            self.barrier()

            self._ln(range(NTK), toff=0)
            self._attn_a()
            self.barrier()

            self._ln(range(1, NT + 1), toff=128)
            self._sinkhorn()
            self._ffn()
            self.barrier()

            self._halo()
            self.barrier()

            self._ln(range(NTK), toff=0)
            self._attn_b()
            self.barrier()

            self._ln(range(1, NT + 1), toff=128)
            self._sinkhorn()
            self._ffn()
            self.barrier()

            def pout(e):
                self.inc(e.dma_start(
                    out_t.ap().rearrange("(t p) d -> p t d", p=128),
                    x[:, 1:NT + 1, :]), "Y", 16)
                self.w(e, "Y", self.cnt["Y"])
            blk.sync(pout)

        return nc

    def _ln(self, tiles, toff):
        c = self.c
        t = self.t
        tiles = list(tiles)
        with ExitStack() as pctx:
            pt = [self.ps(pctx, "ln_pt0", [128, 1024], BF16),
                  self.ps(pctx, "ln_pt1", [128, 1024], BF16)]
            x, xn = t["x"], t["xn"]
            ln_st, ln_mv, ln_r, ln_tm = t["ln_st"], t["ln_mv"], t["ln_r"], t["ln_tm"]
            cst_s, idb_s = t["cst"], t["idb"]

            v0, a0, t0 = self.cnt["V"], self.cnt["A"], self.cnt["T"]
            V_STATS = lambda i: v0 + 4 * i + 1
            V_AGGR = lambda i: v0 + 4 * i + 2
            V_NORM = lambda i: v0 + 4 * i + 3
            V_COPY = lambda i: v0 + 4 * i + 4
            A_LN = lambda i: a0 + 2 * i + 1
            A_EXP = lambda i: a0 + 2 * i + 2
            T_TR = lambda i: t0 + i + 1

            def vbody(e):
                for i, ti in enumerate(tiles):
                    r = i % 2
                    if i >= 2:
                        self.w(e, "A", A_EXP(i - 2))
                    e.bn_stats(ln_st[:, r, 0:6], x[:, ti, 0:512])
                    self.inc(e.bn_stats(ln_st[:, r, 6:12], x[:, ti, 512:1024]), "V")
                    self.w(e, "V", V_STATS(i))
                    self.inc(e.bn_aggr(ln_mv[:, r, :], ln_st[:, r, :]), "V")
                    self.w(e, "A", A_EXP(i))
                    if i >= 2:
                        self.w(e, "T", T_TR(i - 2))
                    self.inc(e.tensor_scalar(
                        ln_tm[:, r, :], x[:, ti, :], ln_mv[:, r, 0:1],
                        ln_r[:, r, 1:2], op0=ALU.subtract, op1=ALU.mult), "V")
                    self.w(e, "T", T_TR(i))
                    for dc in range(c.DC):
                        ins = e.tensor_copy(
                            xn[:, dc, toff + i * 128: toff + (i + 1) * 128],
                            pt[r][:, dc * 128:(dc + 1) * 128])
                    self.inc(ins, "V")

            def abody(e):
                for i, ti in enumerate(tiles):
                    r = i % 2
                    self.w(e, "V", V_AGGR(i))
                    ins = e.activation(ln_r[:, r, 0:1], ln_mv[:, r, 1:2],
                                       AF.Ln, bias=cst_s[:, C_EPS:C_EPS + 1])
                    self.inc(ins, "A")
                    self.w(e, "A", A_LN(i))
                    self.inc(e.activation(
                        ln_r[:, r, 1:2], ln_r[:, r, 0:1], AF.Exp,
                        scale=cst_s[:, C_NEGHALF:C_NEGHALF + 1]), "A")

            def tbody(e):
                for i, ti in enumerate(tiles):
                    r = i % 2
                    self.w(e, "V", V_NORM(i))
                    if i >= 2:
                        self.w(e, "V", V_COPY(i - 2))
                    for dc in range(c.DC):
                        ins = e.transpose(pt[r][:, dc * 128:(dc + 1) * 128],
                                          ln_tm[:, r, dc * 128:(dc + 1) * 128],
                                          idb_s[:])
                    self.inc(ins, "T")

            self.block.vector(vbody)
            self.block.scalar(abody)
            self.block.tensor(tbody)

    def _attn_a(self):
        c = self.c
        t = self.t
        NT, DC, T, NTK = c.NT, c.DC, c.T, c.NTK
        blk = self.block
        x, xn = t["x"], t["xn"]
        with ExitStack() as pctx:
            sb, ps = self.sb, self.ps
            q_sb = sb(pctx, "q_sb", [128, 8, T], BF16)
            k_sb = sb(pctx, "k_sb", [128, 2, T + 256], BF16)
            v_tok = sb(pctx, "v_tok", [128, NTK, 2, 66], BF16)
            ao_fm = xn[:, :, 0:T]
            exp_sb = sb(pctx, "exp_sb", [128, 2, 3, 512], BF16)
            F32R = mybir.dt.float32r
            rc16 = sb(pctx, "rc16", [1, 2, 512], F32R)
            rc64 = sb(pctx, "rc64", [64, 512], F32)
            lnt = sb(pctx, "lnt", [64, 512], F32)
            ones64 = sb(pctx, "ones64", [1, 128], F32R)
            bo_s = sb(pctx, "bo_s", [1, c.D], F32)
            bv_s = sb(pctx, "bv_s", [1, 128], F32)
            ictx = ExitStack()
            wq_b = sb(ictx, "wq_b", [128, DC, c.D], BF16)
            wkd_b = sb(ictx, "wkd_b", [128, DC, 256], BF16)
            wv_b = sb(ictx, "wv_b", [128, DC, 128], BF16)
            pool = ps(pctx, "a_pool", [128, 8, 512], F32)
            PQ = [pool[:, 0, :], pool[:, 1, :]]

            cst_s, pcst_s = t["cst"], t["pcst"]
            idb_s, onesf_s = t["idb"], t["onesf"]
            bq_s, bkd_s = t["bq"], t["bkd"]
            dr = self.dram

            w0 = self.cnt["Wa"]

            def gbody(e):
                for dst, src in ((wq_b, dr["wq"]), (wkd_b, dr["wk_dup"]),
                                 (wv_b, dr["wv"])):
                    self.inc(e.dma_start(dst[:], src.ap().rearrange(
                        "(kc p) f -> p kc f", p=128)), "Wa", 16)
                self.inc(e.dma_start(bo_s[:], t["bo_dram"][:, :]), "Wa", 16)
                self.inc(e.dma_start(bv_s[:], t["bv_dram"][:, :]), "Wa", 16)
                self.inc(e.dma_start(ones64[:], self.dram["ones_r"][:, :]),
                         "Wa", 16)
                self.inc(e.memset(v_tok[:, :, :, 64:66], 1.0), "G")
            blk.gpsimd(gbody)
            W_QKV = w0 + 96
            G_ONES = self.cnt["G"]
            v_ln_done = self.cnt["V"]

            t0, v0 = self.cnt["T"], self.cnt["V"]
            T_PROJ = lambda j: t0 + j + 1
            V_PROJ = lambda j: v0 + j + 1
            nQ, nK, nV = DC * len(c.TT), 2 * len(c.TTK), NTK

            def tproj(e):
                self.w(e, "Wa", W_QKV)
                self.w(e, "V", v_ln_done)
                j = 0
                for oc in range(DC):
                    for (off, sz) in c.TT:
                        if j >= 2:
                            self.w(e, "V", V_PROJ(j - 2))
                        b = j % 2
                        for k in range(DC):
                            ins = e.matmul(pool[:, b, 0:sz],
                                           wq_b[:, k, oc * 128:(oc + 1) * 128],
                                           xn[:, k, 128 + off:128 + off + sz],
                                           start=(k == 0), stop=(k == DC - 1))
                        self.inc(ins, "T")
                        j += 1
                for oc2 in range(2):
                    for (off, sz) in c.TTK:
                        if j >= 2:
                            self.w(e, "V", V_PROJ(j - 2))
                        b = j % 2
                        for k in range(DC):
                            ins = e.matmul(pool[:, b, 0:sz],
                                           wkd_b[:, k, oc2 * 128:(oc2 + 1) * 128],
                                           xn[:, k, off:off + sz],
                                           start=(k == 0), stop=(k == DC - 1))
                        self.inc(ins, "T")
                        j += 1
                self.w(e, "G", G_ONES)
                for ti in range(NTK):
                    if j >= 2:
                        self.w(e, "V", V_PROJ(j - 2))
                    b = j % 2
                    for k in range(DC):
                        e.matmul(pool[:, b, 0:128],
                                 xn[:, k, ti * 128:(ti + 1) * 128],
                                 wv_b[:, k, 0:128], start=(k == 0), stop=False)
                    ins = e.matmul(pool[:, b, 0:128], onesf_s[:], bv_s[:],
                                   start=False, stop=True)
                    self.inc(ins, "T")
                    j += 1
            blk.tensor(tproj)

            def vproj(e):
                j = 0
                for oc in range(DC):
                    for (off, sz) in c.TT:
                        self.w(e, "T", T_PROJ(j))
                        b = j % 2
                        for z in range(2):
                            ins = e.tensor_scalar(
                                q_sb[z * 64:z * 64 + 64, oc, off:off + sz],
                                pool[z * 64:z * 64 + 64, b, 0:sz],
                                bq_s[z * 64:z * 64 + 64, oc:oc + 1], None,
                                op0=ALU.add)
                        self.inc(ins, "V")
                        j += 1
                for oc2 in range(2):
                    for (off, sz) in c.TTK:
                        self.w(e, "T", T_PROJ(j))
                        b = j % 2
                        self.inc(e.tensor_scalar(
                            k_sb[:, oc2, off:off + sz], pool[:, b, 0:sz],
                            bkd_s[:, oc2:oc2 + 1], None, op0=ALU.add), "V")
                        j += 1
                for ti in range(NTK):
                    self.w(e, "T", T_PROJ(j))
                    b = j % 2
                    for kv in range(2):
                        dst = v_tok[:, ti, kv, 0:64]
                        src = pool[:, b, kv * 64:kv * 64 + 64]
                        if ti == 0:
                            ins = e.tensor_scalar(dst, src, pcst_s[:, 0:1],
                                                  None, op0=ALU.mult)
                        elif ti == NTK - 1:
                            ins = e.tensor_scalar(dst, src, pcst_s[:, 2:3],
                                                  None, op0=ALU.mult)
                        else:
                            ins = e.tensor_copy(dst, src)
                    self.inc(ins, "V")
                    j += 1
            blk.vector(vproj)
            t_proj_end = self.cnt["T"]
            ictx.close()
            wo_b = sb(pctx, "wo_b", [128, DC, c.D], BF16)
            wo_w0 = self.cnt["Wa"]

            def gbody2(e):
                self.w(e, "T", t_proj_end)
                self.inc(e.dma_start(wo_b[:], dr["wo"].ap().rearrange(
                    "(kc p) f -> p kc f", p=128)), "Wa", 16)
            blk.gpsimd(gbody2)
            W_O = wo_w0 + 16

            v_proj_end = self.cnt["V"]
            U = NT * 4
            T_SC, T_AV, T_BC = {}, {}, {}
            A_EXP, A_RC = {}, {}
            V_MSK, V_RC, V_AO = {}, {}, {}
            cT, cA, cV = self.cnt["T"], self.cnt["A"], self.cnt["V"]
            cT += 1
            T_SC[0] = cT
            for u in range(U):
                if u + 1 < U:
                    cT += 1
                    T_SC[u + 1] = cT
                cT += 1
                T_AV[u] = cT
                cT += 1
                T_BC[u] = cT
            for u in range(U):
                cA += 1
                A_EXP[u] = cA
                if u >= 1:
                    cA += 1
                    A_RC[u - 1] = cA
            cA += 1
            A_RC[U - 1] = cA
            for u in range(U):
                cV += 1
                V_MSK[u] = cV
                cV += 1
                V_RC[u] = cV
                if u >= 1:
                    cV += 1
                    V_AO[u - 1] = cV
            cV += 1
            V_AO[U - 1] = cV
            tri4_s = t["tri4"]

            def tatt(e):
                def sc(u):
                    qt, iu = divmod(u, 4)
                    z, kv = iu % 2, iu // 2
                    i = u % 2
                    if u >= 2:
                        self.w(e, "A", A_EXP[u - 2])
                        self.w(e, "A", A_RC[u - 2])
                    for j, kc in enumerate((qt, qt + 1, qt + 2)):
                        ins = e.matmul(
                            pool[:, 2 + 3 * i + j, :],
                            k_sb[z * 64:z * 64 + 64, kv ^ z,
                                 kc * 128:(kc + 1) * 128],
                            q_sb[z * 64:z * 64 + 64, kv * 4:kv * 4 + 4,
                                 qt * 128:(qt + 1) * 128],
                            start=True, stop=True)
                    self.inc(ins, "T")
                sc(0)
                for u in range(U):
                    qt, iu = divmod(u, 4)
                    z, kv = iu % 2, iu // 2
                    i = u % 2
                    if u + 1 < U:
                        sc(u + 1)
                    self.w(e, "V", V_MSK[u])
                    self.w(e, "V", V_AO[u - 2] if u >= 2 else v_proj_end)
                    for j, kc in enumerate((qt, qt + 1, qt + 2)):
                        ins = e.matmul(pool[0:66, i, :],
                                       v_tok[:, kc, kv, 0:66],
                                       exp_sb[:, i, j, :],
                                       start=(j == 0), stop=(j == 2))
                    self.inc(ins, "T")
                    self.w(e, "V", V_RC[u])
                    ins = e.matmul(pool[:, 2 + 3 * i, :], ones64[:],
                                   rc16[:, i, :], start=True, stop=True)
                    self.inc(ins, "T")
            blk.tensor(tatt)

            def aatt(e):
                def rc_copy(u):
                    i = u % 2
                    self.w(e, "T", T_BC[u])
                    ins = e.activation(lnt[:], pool[0:64, 2 + 3 * i, :],
                                       AF.Ln)
                    self.fence(e, ins, "F2")
                    if u >= 1:
                        self.w(e, "V", V_AO[u - 1])
                    self.inc(e.activation(
                        rc64[:], lnt[:], AF.Exp,
                        scale=cst_s[0:64, C_NEGONE:C_NEGONE + 1]), "A")
                for u in range(U):
                    i = u % 2
                    self.w(e, "T", T_SC[u])
                    if u >= 2:
                        self.w(e, "T", T_AV[u - 2])
                    for j in range(3):
                        ins = e.activation(exp_sb[:, i, j, :],
                                           pool[:, 2 + 3 * i + j, :], AF.Exp)
                    self.inc(ins, "A")
                    if u >= 1:
                        rc_copy(u - 1)
                rc_copy(U - 1)
            blk.scalar(aatt)

            def vatt(e):
                def ao(u):
                    qt, iu = divmod(u, 4)
                    z, kv = iu % 2, iu // 2
                    i = u % 2
                    self.w(e, "A", A_RC[u])
                    ins = e.tensor_tensor(
                        xn[z * 64:z * 64 + 64, kv * 4:kv * 4 + 4,
                           qt * 128:(qt + 1) * 128],
                        pool[0:64, i, :].rearrange("p (a b) -> p a b", b=128),
                        rc64[:].rearrange("p (a b) -> p a b", b=128),
                        op=ALU.mult)
                    self.inc(ins, "V")
                for u in range(U):
                    qt, iu = divmod(u, 4)
                    i = u % 2
                    self.w(e, "A", A_EXP[u])
                    if qt == 0:
                        ins = e.tensor_scalar(exp_sb[:, i, 0, :],
                                              exp_sb[:, i, 0, :],
                                              pcst_s[:, 0:1], pcst_s[:, 1:2],
                                              op0=ALU.mult, op1=ALU.add)
                        self.fence(e, ins)
                    e.tensor_tensor(exp_sb[:, i, 0, :], exp_sb[:, i, 0, :],
                                    tri4_s[:, 0:512], op=ALU.mult)
                    if qt == NT - 1:
                        ins = e.tensor_scalar(exp_sb[:, i, 2, :],
                                              exp_sb[:, i, 2, :],
                                              pcst_s[:, 2:3], pcst_s[:, 3:4],
                                              op0=ALU.mult, op1=ALU.add)
                        self.fence(e, ins)
                    ins = e.tensor_tensor(exp_sb[:, i, 2, :],
                                          exp_sb[:, i, 2, :],
                                          tri4_s[:, 512:1024], op=ALU.mult)
                    self.inc(ins, "V")
                    self.w(e, "T", T_AV[u])
                    if u >= 2:
                        self.w(e, "T", T_BC[u - 2])
                    with self.nc.allow_low_precision(
                            reason="f32r denom broadcast conduit"):
                        ins = e.tensor_copy(rc16[:, i, :], pool[64:65, i, :])
                    self.inc(ins, "V")
                    if u >= 1:
                        ao(u - 1)
                ao(U - 1)
            blk.vector(vatt)
            v_att_end = self.cnt["V"]
            a_att_end = self.cnt["A"]

            to0, vo0 = self.cnt["T"], self.cnt["V"]
            T_OP = lambda j: to0 + j + 1
            V_OP = lambda j: vo0 + j + 1

            def topr(e):
                self.w(e, "Wa", W_O)
                self.w(e, "V", v_att_end)
                self.w(e, "A", a_att_end)
                j = 0
                for qt in range(NT):
                    for dh in range(2):
                        if j >= 2:
                            self.w(e, "V", V_OP(j - 2))
                        b = j % 2
                        for k in range(DC):
                            e.matmul(pool[:, b, 0:512],
                                     ao_fm[:, k, qt * 128:(qt + 1) * 128],
                                     wo_b[:, k, dh * 512:(dh + 1) * 512],
                                     start=(k == 0), stop=False)
                        ins = e.matmul(pool[:, b, 0:512], onesf_s[:],
                                       bo_s[0:1, dh * 512:(dh + 1) * 512],
                                       start=False, stop=True)
                        self.inc(ins, "T")
                        j += 1
            blk.tensor(topr)

            def vopr(e):
                j = 0
                for qt in range(NT):
                    for dh in range(2):
                        self.w(e, "T", T_OP(j))
                        b = j % 2
                        self.inc(e.scalar_tensor_tensor(
                            x[:, qt + 1, dh * 512:(dh + 1) * 512],
                            pool[:, b, 0:512],
                            cst_s[:, C_ONE:C_ONE + 1],
                            x[:, qt + 1, dh * 512:(dh + 1) * 512],
                            op0=ALU.mult, op1=ALU.add), "V")
                        j += 1
            blk.vector(vopr)

    def _sinkhorn(self):
        c = self.c
        t = self.t
        NT, NS, DC = c.NT, c.n_sink, c.DC
        blk = self.block
        xn, gw_s = t["xn"], t["gw"]
        la1own, colb, W_sel = t["la1own"], t["colb"], t["W_sel"]
        sk_t, sk_m, pi_s, m1_s, m2_s = t["sk_t"], t["sk_m"], t["pi"], t["m1"], t["m2"]
        cst_s, idf_s, onesf_s = t["cst"], t["idf"], t["onesf"]
        esel_s, eselT_s, tsel_s, tselT_s = t["esel"], t["eselT"], t["tsel"], t["tselT"]
        dr = self.dram
        jw = c.JW
        njt = (jw + 511) // 512

        with ExitStack() as pctx:
            la_em = self.sb(pctx, "la_em", [8, c.T], F32)
            la_full = self.sb(pctx, "la_full", [128, c.JW], F32)
            exp_f = self.sb(pctx, "exp_f", [128, c.JW], F32)
            psums = self.sb(pctx, "psums", [128, 1], F32)
            rls = self.sb(pctx, "rls", [16, c.JW], F32)
            clsb = self.sb(pctx, "clsb", [8, 1], F32)
            clrow = self.sb(pctx, "clrow", [1, c.n_sink * 8], F32)
            P1 = self.ps(pctx, "sk_p1", [128, 512], F32)
            P2 = self.ps(pctx, "sk_p2", [128, 512], F32)
            jw_pad = max(512, c.JW)
            P3 = self.ps(pctx, "sk_p3", [16, jw_pad], F32)
            P4 = self.ps(pctx, "sk_p4", [128, jw_pad], F32)

            t0, v0, a0 = self.cnt["T"], self.cnt["V"], self.cnt["A"]
            T_LG = lambda i: t0 + 2 * i + 1
            T_TR = lambda i: t0 + 2 * i + 2
            V_RAW = lambda i: v0 + 6 * i + 1
            V_RMX = lambda i: v0 + 6 * i + 2
            V_TT1 = lambda i: v0 + 6 * i + 3
            V_RSM = lambda i: v0 + 6 * i + 4
            V_LA1 = lambda i: v0 + 6 * i + 5
            V_CEM = lambda i: v0 + 6 * i + 6
            A_EXP = lambda i: a0 + 2 * i + 1
            A_LSE = lambda i: a0 + 2 * i + 2

            def tlg(e):
                self.w(e, "V", v0)
                for i in range(NT):
                    if i >= 2:
                        self.w(e, "V", V_RAW(i - 2))
                    p = P1 if i % 2 == 0 else P2
                    for k in range(DC):
                        ins = e.matmul(p[:, 0:8],
                                       xn[:, k, 128 + i * 128:128 + (i + 1) * 128],
                                       gw_s[:, k, :], start=(k == 0),
                                       stop=(k == DC - 1))
                    self.inc(ins, "T")
                    self.w(e, "V", V_LA1(i))
                    if i >= 2:
                        self.w(e, "V", V_CEM(i - 2))
                    ins = e.transpose(P4[0:8, (i % 2) * 128:(i % 2) * 128 + 128],
                                      la1own[:, i, :], idf_s[:])
                    self.inc(ins, "T")
            blk.tensor(tlg)

            def alg(e):
                for i in range(NT):
                    self.w(e, "V", V_TT1(i))
                    self.inc(e.activation(pi_s[:, i, :], la1own[:, i, :],
                                          AF.Exp), "A")
                    self.w(e, "V", V_RSM(i))
                    self.inc(e.activation(sk_m[:, i, 0:1], sk_m[:, i, 1:2],
                                          AF.Ln), "A")
            blk.scalar(alg)

            def vlg(e):
                for i in range(NT):
                    self.w(e, "T", T_LG(i))
                    p = P1 if i % 2 == 0 else P2
                    self.inc(e.tensor_scalar(sk_t[:, i, :], p[:, 0:8],
                                             cst_s[:, C_NEG20:C_NEG20 + 1],
                                             None, op0=ALU.mult), "V")
                    self.w(e, "V", V_RAW(i))
                    self.inc(e.tensor_reduce(sk_m[:, i, 0:1], sk_t[:, i, :],
                                             axis=AX.X, op=ALU.max), "V")
                    self.w(e, "V", V_RMX(i))
                    self.inc(e.tensor_tensor(
                        la1own[:, i, :], sk_t[:, i, :],
                        sk_m[:, i, 0:1].broadcast_to([128, 8]),
                        op=ALU.subtract), "V")
                    self.w(e, "A", A_EXP(i))
                    self.inc(e.tensor_reduce(sk_m[:, i, 1:2], pi_s[:, i, :],
                                             axis=AX.X, op=ALU.add), "V")
                    self.w(e, "A", A_LSE(i))
                    self.inc(e.tensor_tensor(
                        la1own[:, i, :], la1own[:, i, :],
                        sk_m[:, i, 0:1].broadcast_to([128, 8]),
                        op=ALU.subtract), "V")
                    self.w(e, "T", T_TR(i))
                    self.inc(e.tensor_copy(la_em[:, i * 128:(i + 1) * 128],
                                           P4[0:8, (i % 2) * 128:(i % 2) * 128 + 128]),
                             "V")
            blk.vector(vlg)
            v_em_done = self.cnt["V"]

            def gcc(e):
                self.w(e, "V", v_em_done)
                self.inc(e.dma_start(dr["cc_la_in"][:, :], la_em[:]), "Wc", 16)
                self.w(e, "Wc", self.cnt["Wc"])
                self.inc(e.collective_compute(
                    "AllGather", ALU.bypass, replica_groups=[list(range(8))],
                    ins=[dr["cc_la_in"].ap().opt()],
                    outs=[dr["cc_la_out"].ap().opt()]), "C")
                self.w(e, "C", self.cnt["C"])
                self.inc(e.dma_start(
                    la_full[:],
                    dr["cc_la_out"].ap().rearrange(
                        "(r e) (h j) -> (r e h) j", e=8, h=2)), "Wc", 16)
            blk.gpsimd(gcc)
            w_lafull = self.cnt["Wc"]

            seq = [("col", 0)]
            for it in range(1, NS):
                seq += [("row", it), ("col", it)]
            cA, cV, cT = self.cnt["A"], self.cnt["V"], self.cnt["T"]
            marks = []
            for kind, it in seq:
                m = {"kind": kind, "it": it}
                if kind == "col":
                    cA += 1; m["exp"] = cA
                    cV += 1; m["red"] = cV
                    cT += 1; m["mm1"] = cT
                    cA += 1; m["ln"] = cA
                    cT += 1; m["mm2"] = cT
                    cV += 1; m["sub"] = cV
                    cV += 1; m["crow"] = cV
                else:
                    cA += 1; m["exp"] = cA
                    cT += 1; m["mm1"] = cT
                    cA += 1; m["ln"] = cA
                    cT += 1; m["mm2"] = cT
                    cV += 1; m["sub"] = cV
                marks.append(m)

            def tb(e):
                prev_crow = 0
                self.w(e, "Wc", w_lafull)
                for idx, m in enumerate(marks):
                    if m["kind"] == "col":
                        self.w(e, "V", m["red"])
                        if prev_crow:
                            self.w(e, "V", prev_crow)
                        ins = e.matmul(P1[0:8, 0:1], esel_s[:], psums[:],
                                       start=True, stop=True)
                        ins.then_inc(self.sem["T"], 1)
                        self.w(e, "A", m["ln"])
                        e.matmul(P2[0:128, 0:1], eselT_s[:], clsb[:],
                                 start=True, stop=True)
                        ins = e.transpose(P1[0:1, 8:16], clsb[:],
                                          idf_s[0:8, 0:8])
                        ins.then_inc(self.sem["T"], 1)
                        prev_crow = m["crow"]
                    else:
                        self.w(e, "A", m["exp"])
                        for jt in range(njt):
                            o = jt * 512
                            sz = min(512, jw - o)
                            ins = e.matmul(P3[:, o:o + sz], tsel_s[:],
                                           exp_f[:, o:o + sz],
                                           start=True, stop=True)
                        ins.then_inc(self.sem["T"], 1)
                        self.w(e, "A", m["ln"])
                        for jt in range(njt):
                            o = jt * 512
                            sz = min(512, jw - o)
                            ins = e.matmul(P4[:, o:o + sz], tselT_s[:],
                                           rls[:, o:o + sz],
                                           start=True, stop=True)
                        ins.then_inc(self.sem["T"], 1)

            def ab(e):
                for idx, m in enumerate(marks):
                    if idx == 0:
                        self.w(e, "Wc", w_lafull)
                    else:
                        self.w(e, "V", marks[idx - 1]["sub"])
                    ins = e.activation(exp_f[:], la_full[:], AF.Exp)
                    ins.then_inc(self.sem["A"], 1)
                    self.w(e, "T", m["mm1"])
                    if m["kind"] == "col":
                        ins = e.activation(clsb[:], P1[0:8, 0:1], AF.Ln)
                    else:
                        ins = e.activation(rls[:, 0:jw], P3[:, 0:jw], AF.Ln)
                    ins.then_inc(self.sem["A"], 1)

            def vb(e):
                for idx, m in enumerate(marks):
                    if m["kind"] == "col":
                        self.w(e, "A", m["exp"])
                        ins = e.tensor_reduce(psums[:], exp_f[:], axis=AX.X,
                                              op=ALU.add)
                        ins.then_inc(self.sem["V"], 1)
                        self.w(e, "T", m["mm2"])
                        ins = e.tensor_scalar(la_full[:], la_full[:],
                                              P2[0:128, 0:1], None,
                                              op0=ALU.subtract)
                        ins.then_inc(self.sem["V"], 1)
                        ins = e.tensor_copy(
                            clrow[0:1, m["it"] * 8:(m["it"] + 1) * 8],
                            P1[0:1, 8:16])
                        ins.then_inc(self.sem["V"], 1)
                    else:
                        self.w(e, "T", m["mm2"])
                        ins = e.tensor_tensor(la_full[:], la_full[:],
                                              P4[:, 0:jw], op=ALU.subtract)
                        ins.then_inc(self.sem["V"], 1)

            blk.tensor(tb)
            blk.scalar(ab)
            blk.vector(vb)
            self.cnt["T"], self.cnt["V"], self.cnt["A"] = cT, cV, cA
            v_phA = self.cnt["V"]

            def tcolb(e):
                self.w(e, "V", v_phA)
                for it in range(NS):
                    ins = e.matmul(P4[:, it * 8:(it + 1) * 8], onesf_s[:],
                                   clrow[0:1, it * 8:(it + 1) * 8],
                                   start=True, stop=True)
                self.inc(ins, "T")
            blk.tensor(tcolb)
            t_colb = self.cnt["T"]

            def vcolb(e):
                self.w(e, "T", t_colb)
                self.inc(e.tensor_copy(
                    colb[:].rearrange("p a b -> p (a b)"),
                    P4[:, 0:NS * 8]), "V")
            blk.vector(vcolb)
            self.sink_psum_free = (self.cnt["V"], self.cnt["A"], self.cnt["T"])

            a0b = self.cnt["A"]
            A_BEXP = lambda it: a0b + 2 * it + 1
            A_BLN = lambda it: a0b + 2 * it + 2
            A_PI = a0b + 2 * (NS - 1) + 1
            f_tt1 = {}
            f_rsum = {}
            f_colsub_last = [0]
            v_colb_done = self.cnt["V"]

            def vphB(e):
                self.w(e, "V", v_colb_done)
                for it in range(NS):
                    cb = colb[:, it:it + 1, :].broadcast_to([128, NT, 8])
                    ins = e.tensor_tensor(la1own[:], la1own[:], cb,
                                          op=ALU.subtract)
                    self.fence(e, ins)
                    if it == NS - 1:
                        f_colsub_last[0] = self.cnt["F"]
                    if it < NS - 1:
                        self.fence(e, e.tensor_reduce(
                            sk_m[:, :, 0:1], la1own[:], axis=AX.X, op=ALU.max))
                        ins = e.tensor_tensor(
                            sk_t[:], la1own[:],
                            sk_m[:, :, 0:1].broadcast_to([128, NT, 8]),
                            op=ALU.subtract)
                        self.fence(e, ins)
                        f_tt1[it] = self.cnt["F"]
                        self.w(e, "A", A_BEXP(it))
                        self.fence(e, e.tensor_reduce(
                            sk_m[:, :, 1:2], pi_s[:], axis=AX.X, op=ALU.add))
                        f_rsum[it] = self.cnt["F"]
                        self.w(e, "A", A_BLN(it))
                        self.fence(e, e.tensor_tensor(
                            la1own[:], sk_t[:],
                            sk_m[:, :, 0:1].broadcast_to([128, NT, 8]),
                            op=ALU.subtract))
                self.w(e, "A", A_PI)
                self.fence(e, e.tensor_reduce(sk_m[:, :, 0:1], pi_s[:],
                                              axis=AX.X, op=ALU.max))
                self.fence(e, e.tensor_tensor(
                    m1_s[:], pi_s[:],
                    sk_m[:, :, 0:1].broadcast_to([128, NT, 8]),
                    op=ALU.is_equal))
                self.fence(e, e.scalar_tensor_tensor(
                    sk_t[:], m1_s[:], cst_s[:, C_NEG1E30:C_NEG1E30 + 1],
                    pi_s[:], op0=ALU.mult, op1=ALU.add))
                self.fence(e, e.tensor_reduce(sk_m[:, :, 1:2], sk_t[:],
                                              axis=AX.X, op=ALU.max))
                self.fence(e, e.tensor_tensor(
                    m2_s[:], sk_t[:],
                    sk_m[:, :, 1:2].broadcast_to([128, NT, 8]),
                    op=ALU.is_equal))
                self.fence(e, e.tensor_tensor(
                    sk_t[:], m1_s[:],
                    pi_s[:, :, 0:1].broadcast_to([128, NT, 8]), op=ALU.mult))
                self.fence(e, e.tensor_tensor(
                    m1_s[:], m2_s[:],
                    pi_s[:, :, 1:2].broadcast_to([128, NT, 8]), op=ALU.mult))
                self.inc(e.tensor_tensor(W_sel[:], sk_t[:], m1_s[:],
                                         op=ALU.add), "V")
            blk.vector(vphB)
            self.v_wsel = self.cnt["V"]

            la_fl = la1own[:].rearrange("p a b -> p (a b)")
            skt_fl = sk_t[:].rearrange("p a b -> p (a b)")
            pi_fl = pi_s[:].rearrange("p a b -> p (a b)")

            def aphB(e):
                for it in range(NS - 1):
                    self.w(e, "F", f_tt1[it])
                    self.inc(e.activation(pi_fl, skt_fl, AF.Exp), "A")
                    self.w(e, "F", f_rsum[it])
                    self.inc(e.activation(sk_m[:, :, 0:1], sk_m[:, :, 1:2],
                                          AF.Ln), "A")
                self.w(e, "F", f_colsub_last[0])
                self.inc(e.activation(pi_fl, la_fl, AF.Exp), "A")
            blk.scalar(aphB)

    def _ffn(self):
        c = self.c
        t = self.t
        NT, DC, T = c.NT, c.DC, c.T
        blk = self.block
        x, xn, W_sel = t["x"], t["xn"], t["W_sel"]
        cst_s = t["cst"]
        dr = self.dram
        ntt = len(c.TT)

        with ExitStack() as pctx:
            h_b = self.sb(pctx, "h_b", [128, c.GMAX, T], BF16)
            sg_b = self.sb(pctx, "sg_b", [128, 2, 512], BF16)
            wgu_b = self.sb(pctx, "wgu_b", [128, 2, 2, DC, 256], BF16)
            wd_b = self.sb(pctx, "wd_b", [128, 2, c.GMAX, c.D], BF16)
            PG = [self.ps(pctx, "f_pg0", [128, 512], F32),
                  self.ps(pctx, "f_pg1", [128, 512], F32)]
            PU = [self.ps(pctx, "f_pu0", [128, 512], F32),
                  self.ps(pctx, "f_pu1", [128, 512], F32)]
            PYF = [self.ps(pctx, "f_py0", [128, 512], F32),
                   self.ps(pctx, "f_py1", [128, 512], F32)]

            t0, a0, v0 = self.cnt["T"], self.cnt["A"], self.cnt["V"]
            units = []
            for e_i in range(c.E):
                for gi, G in enumerate(c.GROUPS):
                    for mi in range(len(G)):
                        for tti in range(ntt):
                            units.append(("p1", e_i, gi, mi, tti))
                    for tt2 in range(NT):
                        for dh in range(2):
                            units.append(("p2", e_i, gi, tt2, dh))
            T_U = {}
            A_U = {}
            V_U = {}
            cT, cA, cV = t0, a0, v0
            p1list, p2list = [], []
            for u in units:
                cT += 1
                T_U[u] = cT
                if u[0] == "p1":
                    cA += 1
                    A_U[u] = cA
                    p1list.append(u)
                cV += 1
                V_U[u] = cV
                if u[0] == "p2":
                    p2list.append(u)
            pair_of = {}
            PAIR_B = {}
            PAIR_SEQ = []
            gp = 0
            for e_i in range(c.E):
                for gi, G in enumerate(c.GROUPS):
                    nm = len(G)
                    for pi2 in range((nm + 1) // 2):
                        pair_of[(e_i, gi, pi2)] = list(range(2 * pi2,
                                                             min(2 * pi2 + 2, nm)))
                        PAIR_B[(e_i, gi, pi2)] = gp % 2
                        PAIR_SEQ.append((e_i, gi, pi2))
                        gp += 1
            T_PAIR_DONE = {}
            for (e_i, gi, pi2), mis in pair_of.items():
                T_PAIR_DONE[(e_i, gi, pi2)] = T_U[("p1", e_i, gi, mis[-1],
                                                   ntt - 1)]
            T_P2_DONE = {}
            V_P2_DONE = {}
            for e_i in range(c.E):
                for gi in range(len(c.GROUPS)):
                    T_P2_DONE[(e_i, gi)] = T_U[("p2", e_i, gi, NT - 1, 1)]
                    V_P2_DONE[(e_i, gi)] = V_U[("p2", e_i, gi, NT - 1, 1)]
            V_P1_DONE = {}
            for e_i in range(c.E):
                for gi, G in enumerate(c.GROUPS):
                    V_P1_DONE[(e_i, gi)] = V_U[("p1", e_i, gi, len(G) - 1,
                                                ntt - 1)]

            W_WD = {}
            W_PAIR = {}
            cWd = {0: self.cnt["Wd0"], 1: self.cnt["Wd1"]}
            cWg = {0: self.cnt["Wg0"], 1: self.cnt["Wg1"]}
            negl = len(c.GROUPS)
            for e_i in range(c.E):
                for gi, G in enumerate(c.GROUPS):
                    eg = e_i * negl + gi
                    bd = eg % 2
                    cWd[bd] += 16
                    W_WD[(e_i, gi)] = (bd, cWd[bd])
                    for pi2 in range((len(G) + 1) // 2):
                        b = PAIR_B[(e_i, gi, pi2)]
                        cWg[b] += 32
                        W_PAIR[(e_i, gi, pi2)] = (b, cWg[b])

            def prev_eg(e_i, gi):
                k = e_i * len(c.GROUPS) + gi - 1
                if k < 0:
                    return None
                return divmod(k, len(c.GROUPS))

            v_psum_free, a_psum_free, t_psum_free = self.sink_psum_free

            def gstream(e):
                for e_i in range(c.E):
                    wg_e = dr["wg_t"].ap()[e_i].rearrange("(kc p) f -> p kc f", p=128)
                    wu_e = dr["wu_t"].ap()[e_i].rearrange("(kc p) f -> p kc f", p=128)
                    wd_e = dr["wd_t"].ap()[e_i].rearrange("(mc p) d -> p mc d", p=128)
                    for gi, G in enumerate(c.GROUPS):
                        eg = e_i * len(c.GROUPS) + gi
                        if eg >= 2:
                            pv2 = divmod(eg - 2, len(c.GROUPS))
                            self.w(e, "T", T_P2_DONE[pv2])
                        self.inc(e.dma_start(
                            wd_b[:, eg % 2, 0:len(G), :],
                            wd_e[:, G[0]:G[0] + len(G), :],
                            max_dma_last_dim=8192), f"Wd{eg % 2}", 16)
                        npair = (len(G) + 1) // 2
                        for pi2 in range(npair):
                            mis = pair_of[(e_i, gi, pi2)]
                            mw = len(mis) * 128
                            f0 = (G[0] + mis[0]) * 128
                            gidx = PAIR_SEQ.index((e_i, gi, pi2))
                            if gidx >= 2:
                                self.w(e, "T", T_PAIR_DONE[PAIR_SEQ[gidx - 2]])
                            b = PAIR_B[(e_i, gi, pi2)]
                            self.inc(e.dma_start(
                                wgu_b[:, b, 0, :, 0:mw],
                                wg_e[:, :, f0:f0 + mw],
                                max_dma_last_dim=8192), f"Wg{b}", 16)
                            self.inc(e.dma_start(
                                wgu_b[:, b, 1, :, 0:mw],
                                wu_e[:, :, f0:f0 + mw],
                                max_dma_last_dim=8192), f"Wg{b}", 16)
            blk.gpsimd(gstream)


            def tffn(e):
                self.w(e, "V", v_psum_free)
                self.w(e, "A", a_psum_free)
                p1i = 0
                p2i = 0
                for e_i in range(c.E):
                    for gi, G in enumerate(c.GROUPS):
                        for mi in range(len(G)):
                            pi2, z = divmod(mi, 2)
                            b = PAIR_B[(e_i, gi, pi2)]
                            for tti, (off, sz) in enumerate(c.TT):
                                u = ("p1", e_i, gi, mi, tti)
                                if tti == 0 and z == 0:
                                    wb_, wv_ = W_PAIR[(e_i, gi, pi2)]
                                    self.w(e, f"Wg{wb_}", wv_)
                                ii = p1i % 2
                                if p1i >= 2:
                                    pu = p1list[p1i - 2]
                                    self.w(e, "A", A_U[pu])
                                    self.w(e, "V", V_U[pu])
                                for k in range(DC):
                                    e.matmul(PG[ii][:, 0:sz],
                                             wgu_b[:, b, 0, k,
                                                   z * 128:(z + 1) * 128],
                                             xn[:, k, 128 + off:128 + off + sz],
                                             start=(k == 0), stop=(k == DC - 1))
                                for k in range(DC):
                                    ins = e.matmul(
                                        PU[ii][:, 0:sz],
                                        wgu_b[:, b, 1, k, z * 128:(z + 1) * 128],
                                        xn[:, k, 128 + off:128 + off + sz],
                                        start=(k == 0), stop=(k == DC - 1))
                                self.inc(ins, "T")
                                assert self.cnt["T"] == T_U[u]
                                p1i += 1
                        eg = e_i * len(c.GROUPS) + gi
                        wdb_, wdv_ = W_WD[(e_i, gi)]
                        self.w(e, f"Wd{wdb_}", wdv_)
                        for tt2 in range(NT):
                            for dh in range(2):
                                u = ("p2", e_i, gi, tt2, dh)
                                jj = p2i % 2
                                if p2i >= 2:
                                    self.w(e, "V", V_U[p2list[p2i - 2]])
                                if tt2 == 0 and dh == 0:
                                    self.w(e, "V", V_P1_DONE[(e_i, gi)])
                                for mi in range(len(G)):
                                    ins = e.matmul(
                                        PYF[jj][:, 0:512],
                                        h_b[:, mi, tt2 * 128:(tt2 + 1) * 128],
                                        wd_b[:, eg % 2, mi,
                                             dh * 512:(dh + 1) * 512],
                                        start=(mi == 0),
                                        stop=(mi == len(G) - 1))
                                self.inc(ins, "T")
                                assert self.cnt["T"] == T_U[u]
                                p2i += 1
            blk.tensor(tffn)

            def affn(e):
                p1i = 0
                for e_i in range(c.E):
                    for gi, G in enumerate(c.GROUPS):
                        for mi in range(len(G)):
                            for tti, (off, sz) in enumerate(c.TT):
                                u = ("p1", e_i, gi, mi, tti)
                                ii = p1i % 2
                                self.w(e, "T", T_U[u])
                                if p1i >= 2:
                                    self.w(e, "V", V_U[p1list[p1i - 2]])
                                self.inc(e.activation(sg_b[:, ii, 0:sz],
                                                      PG[ii][:, 0:sz],
                                                      AF.Silu), "A")
                                assert self.cnt["A"] == A_U[u]
                                p1i += 1
            blk.scalar(affn)

            wsel_fence = self.v_wsel
            prev_evac = {}

            def vffn(e):
                self.w(e, "V", wsel_fence)
                self.w(e, "A", a_psum_free)
                self.w(e, "T", t_psum_free)
                p1i = 0
                p2i = 0
                for e_i in range(c.E):
                    for gi, G in enumerate(c.GROUPS):
                        for mi in range(len(G)):
                            for tti, (off, sz) in enumerate(c.TT):
                                u = ("p1", e_i, gi, mi, tti)
                                ii = p1i % 2
                                self.w(e, "A", A_U[u])
                                if mi == 0 and tti == 0:
                                    pv = prev_eg(e_i, gi)
                                    if pv is not None:
                                        self.w(e, "T", T_P2_DONE[pv])
                                self.inc(e.tensor_tensor(
                                    h_b[:, mi, off:off + sz],
                                    sg_b[:, ii, 0:sz], PU[ii][:, 0:sz],
                                    op=ALU.mult), "V")
                                assert self.cnt["V"] == V_U[u]
                                p1i += 1
                        for tt2 in range(NT):
                            for dh in range(2):
                                u = ("p2", e_i, gi, tt2, dh)
                                jj = p2i % 2
                                self.w(e, "T", T_U[u])
                                key = (tt2, dh)
                                if key in prev_evac:
                                    self.w(e, "V", prev_evac[key])
                                self.inc(e.scalar_tensor_tensor(
                                    x[:, tt2 + 1, dh * 512:(dh + 1) * 512],
                                    PYF[jj][:, 0:512],
                                    W_sel[:, tt2, e_i:e_i + 1],
                                    x[:, tt2 + 1, dh * 512:(dh + 1) * 512],
                                    op0=ALU.mult, op1=ALU.add), "V")
                                prev_evac[key] = self.cnt["V"]
                                assert self.cnt["V"] == V_U[u]
                                p2i += 1
            blk.vector(vffn)

    def _halo(self):
        c = self.c
        t = self.t
        blk = self.block
        x, selh_s = t["x"], t["selh"]
        dr = self.dram
        NT = c.NT

        with ExitStack() as pctx:
            G_sb = self.sb(pctx, "G_sb", [128, 16, c.D], F32)

            y0 = self.cnt["Y"]

            def ysync(e):
                self.inc(e.dma_start(dr["cc_h_in"][0:128, :], x[:, 1, :]),
                         "Y", 16)
                self.inc(e.dma_start(dr["cc_h_in"][128:256, :], x[:, NT, :]),
                         "Y", 16)
            blk.sync(ysync)

            def gcc(e):
                self.w(e, "Y", y0 + 32)
                self.inc(e.collective_compute(
                    "AllGather", ALU.bypass, replica_groups=[list(range(8))],
                    ins=[dr["cc_h_in"].ap().opt()],
                    outs=[dr["cc_h_out"].ap().opt()]), "C")
                self.w(e, "C", self.cnt["C"])
                self.inc(e.dma_start(G_sb[:], dr["cc_h_out"].ap().rearrange(
                    "(t p) d -> p t d", p=128)), "Wc", 16)
            blk.gpsimd(gcc)
            w_g = self.cnt["Wc"]

            def vsel(e):
                self.w(e, "Wc", w_g)
                for side, dst_ti, col0 in ((0, 0, 0), (1, c.NTK - 1, 16)):
                    ins = e.tensor_scalar(x[:, dst_ti, :], G_sb[:, 0, :],
                                          selh_s[:, col0:col0 + 1], None,
                                          op0=ALU.mult)
                    self.fence(e, ins)
                    for ti in range(1, 16):
                        ins = e.scalar_tensor_tensor(
                            x[:, dst_ti, :], G_sb[:, ti, :],
                            selh_s[:, col0 + ti:col0 + ti + 1],
                            x[:, dst_ti, :], op0=ALU.mult, op1=ALU.add)
                        self.fence(e, ins)
            blk.vector(vsel)

    def _attn_b(self):
        c = self.c
        t = self.t
        NT, DC, T, NTK, HB = c.NT, c.DC, c.T, c.NTK, c.HB
        blk = self.block
        x, xn = t["x"], t["xn"]
        cst_s, pcst_s, idb_s = t["cst"], t["pcst"], t["idb"]
        dr = self.dram

        with ExitStack() as pctx:
            sb, ps = self.sb, self.ps
            wb_b = sb(pctx, "wb_b", [128, DC, 1536], BF16)
            q_sb = sb(pctx, "q_sb_b", [128, 4, T], BF16)
            k_sb = sb(pctx, "k_sb_b", [128, 4, T + 256], BF16)
            v_tok = sb(pctx, "v_tok_b", [128, NTK, 8, 66], BF16)
            er = sb(pctx, "er_b", [128, 2, 2, 128], BF16)
            exp_sb = sb(pctx, "exp_sb_b", [128, 2, 384], BF16)
            rc_sb = sb(pctx, "rc_sb_b", [128, 2, 1], F32)
            PQ = [ps(pctx, "b_pq0", [128, 512], F32),
                  ps(pctx, "b_pq1", [128, 512], F32)]
            PS_ = [ps(pctx, "b_ps0", [128, 512], F32),
                   ps(pctx, "b_ps1", [128, 512], F32)]
            PY = [ps(pctx, "b_py0", [128, 512], F32),
                  ps(pctx, "b_py1", [128, 512], F32)]

            v_ln_done = self.cnt["V"]
            for grp in range(2):
                w0 = self.cnt["Wa"]

                def gw(e, grp=grp):
                    if grp > 0:
                        self.w(e, "T", self._b_t_end)
                        self.w(e, "V", self._b_v_end)
                    for part in range(3):
                        src = dr["wqkv"].ap().rearrange(
                            "(kc p) f -> p kc f", p=128)[
                            :, :, part * c.D + grp * 512:
                            part * c.D + grp * 512 + 512]
                        self.inc(e.dma_start(wb_b[:, :, part * 512:
                                                   part * 512 + 512], src),
                                 "Wa", 16)
                    self.inc(e.memset(v_tok[:, :, :, 64:66], 1.0), "G")
                blk.gpsimd(gw)
                W_G = w0 + 48
                G_ONES = self.cnt["G"]

                t0, v0 = self.cnt["T"], self.cnt["V"]
                T_PROJ = lambda j: t0 + j + 1
                V_PROJ = lambda j: v0 + j + 1

                def tproj(e, grp=grp):
                    self.w(e, "Wa", W_G)
                    self.w(e, "V", v_ln_done)
                    self.w(e, "G", G_ONES)
                    j = 0
                    for oc in range(4):
                        for (off, sz) in c.TT:
                            if j >= 2:
                                self.w(e, "V", V_PROJ(j - 2))
                            p = PQ[j % 2]
                            for k in range(DC):
                                ins = e.matmul(
                                    p[:, 0:sz],
                                    wb_b[:, k, oc * 128:(oc + 1) * 128],
                                    xn[:, k, 128 + off:128 + off + sz],
                                    start=(k == 0), stop=(k == DC - 1))
                            self.inc(ins, "T")
                            j += 1
                    for oc in range(4):
                        for (off, sz) in c.TTK:
                            if j >= 2:
                                self.w(e, "V", V_PROJ(j - 2))
                            p = PQ[j % 2]
                            for k in range(DC):
                                ins = e.matmul(
                                    p[:, 0:sz],
                                    wb_b[:, k, 512 + oc * 128:512 + (oc + 1) * 128],
                                    xn[:, k, off:off + sz],
                                    start=(k == 0), stop=(k == DC - 1))
                            self.inc(ins, "T")
                            j += 1
                    for ti in range(NTK):
                        if j >= 2:
                            self.w(e, "V", V_PROJ(j - 2))
                        p = PQ[j % 2]
                        for k in range(DC):
                            ins = e.matmul(p[:, 0:512],
                                           xn[:, k, ti * 128:(ti + 1) * 128],
                                           wb_b[:, k, 1024:1536],
                                           start=(k == 0), stop=(k == DC - 1))
                        self.inc(ins, "T")
                        j += 1
                blk.tensor(tproj)

                def vproj(e, grp=grp):
                    j = 0
                    for oc in range(4):
                        for (off, sz) in c.TT:
                            self.w(e, "T", T_PROJ(j))
                            p = PQ[j % 2]
                            self.inc(e.tensor_scalar(
                                q_sb[:, oc, off:off + sz], p[:, 0:sz],
                                cst_s[:, C_EIGHTH:C_EIGHTH + 1], None,
                                op0=ALU.mult), "V")
                            j += 1
                    for oc in range(4):
                        for (off, sz) in c.TTK:
                            self.w(e, "T", T_PROJ(j))
                            p = PQ[j % 2]
                            self.inc(e.tensor_copy(
                                k_sb[:, oc, off:off + sz], p[:, 0:sz]), "V")
                            j += 1
                    for ti in range(NTK):
                        self.w(e, "T", T_PROJ(j))
                        p = PQ[j % 2]
                        for hg in range(8):
                            ins = e.tensor_copy(v_tok[:, ti, hg, 0:64],
                                                p[:, hg * 64:(hg + 1) * 64])
                        self.inc(ins, "V")
                        j += 1
                blk.vector(vproj)

                NU = NT * 8
                bnd = lambda u: (u // 8) in (0, NT - 1)
                T_SC, T_AV = {}, {}
                A_EXP = {}
                V_MSK, V_RC, V_EV = {}, {}, {}
                cT, cA, cV = self.cnt["T"], self.cnt["A"], self.cnt["V"]
                for u in range(NU):
                    cT += 1
                    T_SC[u] = cT
                    cT += 1
                    T_AV[u] = cT
                    cA += 1
                    A_EXP[u] = cA
                for u in range(NU):
                    if bnd(u):
                        cV += 1
                        V_MSK[u] = cV
                    cV += 1
                    V_RC[u] = cV
                    if u + 1 < NU and not bnd(u + 1):
                        cV += 2
                        V_MSK[u + 1] = cV
                    else:
                        cV += 1
                    V_EV[u] = cV

                def tatt(e, grp=grp):
                    for u in range(NU):
                        qt, h = divmod(u, 8)
                        z = h % 2
                        i = u % 2
                        if u >= 2:
                            self.w(e, "A", A_EXP[u - 2])
                        for j, kc in enumerate((qt, qt + 1, qt + 2)):
                            ins = e.matmul(
                                PS_[i][:, j * 128:(j + 1) * 128],
                                k_sb[z * 64:z * 64 + 64, h // 2,
                                     kc * 128:(kc + 1) * 128],
                                q_sb[z * 64:z * 64 + 64, h // 2,
                                     qt * 128:(qt + 1) * 128],
                                start=True, stop=True)
                        self.inc(ins, "T")
                        self.w(e, "A", A_EXP[u])
                        self.w(e, "V", V_MSK[u])
                        if u >= 2:
                            self.w(e, "V", V_EV[u - 2])
                        for j, kc in enumerate((qt, qt + 1, qt + 2)):
                            ins = e.matmul(
                                PY[i][:, 0:66],
                                exp_sb[:, i, j * 128:(j + 1) * 128],
                                v_tok[:, kc, h, 0:66],
                                start=(j == 0), stop=(j == 2))
                        self.inc(ins, "T")
                blk.tensor(tatt)

                def aatt(e, grp=grp):
                    for u in range(NU):
                        qt, h = divmod(u, 8)
                        i = u % 2
                        self.w(e, "T", T_SC[u])
                        if u >= 2:
                            if bnd(u - 2):
                                self.w(e, "V", V_MSK[u - 2])
                            self.w(e, "T", T_AV[u - 2])
                        if qt == 0:
                            e.activation(er[:, i, 0, :], PS_[i][:, 0:128],
                                         AF.Exp)
                            ins = e.activation(exp_sb[:, i, 128:384],
                                               PS_[i][:, 128:384], AF.Exp)
                        elif qt == NT - 1:
                            e.activation(exp_sb[:, i, 0:256],
                                         PS_[i][:, 0:256], AF.Exp)
                            ins = e.activation(er[:, i, 1, :],
                                               PS_[i][:, 256:384], AF.Exp)
                        else:
                            ins = e.activation(exp_sb[:, i, 0:384],
                                               PS_[i][:, 0:384], AF.Exp)
                        self.inc(ins, "A")
                blk.scalar(aatt)

                def vatt(e, grp=grp):
                    for u in range(NU):
                        qt, h = divmod(u, 8)
                        i = u % 2
                        self.w(e, "A", A_EXP[u])
                        if u >= 2:
                            self.w(e, "T", T_AV[u - 2])
                        if qt == 0:
                            ins = e.tensor_scalar(
                                exp_sb[:, i, 0:128], er[:, i, 0, :],
                                pcst_s[:, 4:5], None, op0=ALU.mult)
                            self.inc(ins, "V")
                        elif qt == NT - 1:
                            ins = e.tensor_scalar(
                                exp_sb[:, i, 256:384], er[:, i, 1, :],
                                pcst_s[:, 5:6], None, op0=ALU.mult)
                            self.inc(ins, "V")
                        self.w(e, "T", T_AV[u])
                        self.inc(e.reciprocal(rc_sb[:, i, :],
                                              PY[i][:, 64:65]), "V")
                        self.w(e, "V", V_RC[u])
                        hh = grp * 8 + h
                        n_ev = 2 if (u + 1 < NU and not bnd(u + 1)) else 1
                        self.inc(e.scalar_tensor_tensor(
                            x[:, qt + 1, hh * 64:(hh + 1) * 64],
                            PY[i][:, 0:64], rc_sb[:, i, :],
                            x[:, qt + 1, hh * 64:(hh + 1) * 64],
                            op0=ALU.mult, op1=ALU.add), "V", n_ev)
                blk.vector(vatt)
                self._b_t_end = self.cnt["T"]
                self._b_v_end = self.cnt["V"]




def _prep_inputs(cfg, inputs):
    c = cfg
    T, D, S = c.T, c.D, c.S
    f32 = np.float32
    x = np.asarray(inputs["L0_attn_a_x"], f32)
    wq = np.ascontiguousarray(np.asarray(inputs["L0_attn_a_Wq"], f32))
    bq = np.asarray(inputs["L0_attn_a_bq"], f32)
    wk = np.asarray(inputs["L0_attn_a_Wk"], f32)
    bk = np.asarray(inputs["L0_attn_a_bk"], f32)
    wv = np.ascontiguousarray(np.asarray(inputs["L0_attn_a_Wv"], f32))
    bv = np.asarray(inputs["L0_attn_a_bv"], f32)
    wo = np.ascontiguousarray(np.asarray(inputs["L0_attn_a_Wo"], f32))
    bo = np.asarray(inputs["L0_attn_a_bo"], f32)
    wqkv = np.ascontiguousarray(np.asarray(inputs["L0_attn_b_w_qkv"], f32))
    gate_w = np.asarray(inputs["L0_ffn_gate_W"], f32)
    wg = np.asarray(inputs["L0_ffn_W_gate"], f32)
    wu = np.asarray(inputs["L0_ffn_W_up"], f32)
    wd = np.asarray(inputs["L0_ffn_W_down"], f32)

    wk_dup = np.ascontiguousarray(
        np.concatenate([wk, wk[:, 64:128], wk[:, 0:64]], axis=1))
    bk_dup = np.stack([bk, np.concatenate([bk[64:128], bk[0:64]])], axis=1)
    bk_dup = np.ascontiguousarray(bk_dup)
    bq_t = np.ascontiguousarray(bq.reshape(c.DC, 128).T)
    gate_wt = np.ascontiguousarray(gate_w.T)
    wg_t = np.ascontiguousarray(wg.transpose(0, 2, 1))
    wu_t = np.ascontiguousarray(wu.transpose(0, 2, 1))
    wd_t = np.ascontiguousarray(wd.transpose(0, 2, 1))

    u = np.arange(128)[:, None]
    tq = np.arange(128)[None, :]
    tri = np.concatenate([(u >= tq), (u < tq)], axis=1).astype(BFH)
    tri4 = np.concatenate([np.tile(u >= tq, (1, 4)),
                           np.tile(u < tq, (1, 4))], axis=1).astype(BFH)
    ident_b = np.eye(128, dtype=BFH)
    ident_f = np.eye(128, dtype=f32)
    onesf = np.ones((1, 128), f32)
    cstv = np.zeros((128, NCST), f32)
    cstv[:, C_EPS] = 1e-5
    cstv[:, C_NEGHALF] = -0.5
    cstv[:, C_NEG20] = -1.0 / 0.05
    cstv[:, C_EIGHTH] = 64.0 ** -0.5
    cstv[:, C_ONE] = 1.0
    cstv[:, C_NEG1E30] = -1e30
    cstv[:, C_NEGONE] = -1.0
    pp = np.arange(128)[:, None]
    esel = ((pp % 16) // 2 == np.arange(8)[None, :]).astype(f32)
    eselT = np.ascontiguousarray(esel.T)
    tsel = ((pp // 16) * 2 + (pp % 2) == np.arange(16)[None, :]).astype(f32)
    tselT = np.ascontiguousarray(tsel.T)

    shared = dict(wq=wq, wk_dup=wk_dup, wv=wv, wo=wo, wqkv=wqkv,
                  gate_wt=gate_wt, wg_t=wg_t, wu_t=wu_t, wd_t=wd_t,
                  bq_t=bq_t, bk_dupi=bk_dup, bo_row=bo[None, :].copy(),
                  bv_row=bv[None, :].copy(), tri=tri, tri4=tri4,
                  ident_b=ident_b, ones_r=np.ones((1, 128), f32),
                  ident_f=ident_f, onesf=onesf, cst=cstv, esel=esel,
                  eselT=eselT, tsel=tsel, tselT=tselT)

    in_maps = []
    npc = c.NCORE // c.B
    for core in range(c.NCORE):
        seq, ch = divmod(core, npc)
        t0 = ch * T
        xi = np.zeros((T + 256, D), f32)
        lo, hi = t0 - 128, t0 + T + 128
        slo, shi = max(0, lo), min(S, hi)
        xi[slo - lo: shi - lo] = x[seq, slo:shi]
        first = 1.0 if ch > 0 else 0.0
        last = 1.0 if ch < npc - 1 else 0.0
        pc = np.zeros((128, 6), f32)
        pc[:, 0] = first
        pc[:, 1] = 1.0 - first
        pc[:, 2] = last
        pc[:, 3] = 1.0 - last
        pc[:, 4] = first
        pc[:, 5] = last
        sh = np.zeros((128, 32), f32)
        if ch > 0:
            sh[:, 2 * (core - 1) + 1] = 1.0
        if ch < npc - 1:
            sh[:, 16 + 2 * (core + 1)] = 1.0
        m = dict(shared)
        m["x_in"] = xi
        m["pcst"] = pc
        m["sel_halo"] = sh
        in_maps.append(m)
    return in_maps


_NC_CACHE = {}


def _get_nc(cfg_key):
    if cfg_key not in _NC_CACHE:
        T, MF, n_sink = cfg_key
        _NC_CACHE[cfg_key] = MK(Cfg(T=T, MF=MF, n_sink=n_sink)).build()
    return _NC_CACHE[cfg_key]


def kernel(**inputs):
    cfg = Cfg()
    in_maps = _prep_inputs(cfg, inputs)
    nc = _get_nc((cfg.T, cfg.MF, cfg.n_sink))
    res = run_bass_kernel_spmd(nc, in_maps, core_ids=list(range(cfg.NCORE)))
    out = np.empty((cfg.B, cfg.S, cfg.D), np.float32)
    npc = cfg.NCORE // cfg.B
    for core in range(cfg.NCORE):
        seq, ch = divmod(core, npc)
        out[seq, ch * cfg.T:(ch + 1) * cfg.T] = res.results[core]["out"]
    return out

